# version 10
# baseline (speedup 1.0000x reference)
import numpy as np
import concourse.bass as bass
import concourse.mybir as mybir
from concourse.bass_utils import run_bass_kernel_spmd

F32 = mybir.dt.float32
BF16 = mybir.dt.bfloat16
I8 = mybir.dt.int8
I32 = mybir.dt.int32
AF = mybir.ActivationFunctionType
ALU = mybir.AluOpType

ENGS = ("sync", "scalar", "tensor", "vector", "gpsimd")


class Sem:
    def __init__(self, nc, name):
        self.h = nc.alloc_semaphore(name)
        self.n = 0

    def bump(self, k=1):
        self.n += k
        return self.n


class SAlloc:
    def __init__(self, nc, base, limit):
        self.nc, self.base, self.limit = nc, base, limit
        self.used = []

    def alloc(self, name, shape, dtype, align=64):
        per = int(np.prod(shape[1:])) * mybir.dt.size(dtype)
        per = (per + align - 1) // align * align
        off = self.base
        for (o, e, _) in sorted(self.used):
            if off + per <= o:
                break
            off = max(off, e)
        assert off + per <= self.limit, f"SBUF overflow: {name} {per}B at {off}"
        self.used.append((off, off + per, name))
        return self.nc.alloc_sbuf_tensor_at(name, list(shape), dtype, offset=off)

    def free(self, name):
        self.used = [u for u in self.used if u[2] != name]


class Plan:

    def __init__(self, nc):
        self.nc = nc
        self.q = {e: [] for e in ENGS}
        self.self_sems = {}

    def em(self, eng, fn):
        self.q[eng].append(fn)

    def wait(self, eng, sem, val=None):
        v = sem.n if val is None else val
        if v > 0:
            self.em(eng, lambda e, s=sem.h, vv=v: e.wait_ge(s, vv))

    def op(self, eng, fn, sem=None, k=1):
        ss = self.self_sems.get(eng)
        if ss is not None:
            assert sem is None or sem is ss
            self.wait(eng, ss)
            v = ss.bump(1)
            self.em(eng, lambda e, f=fn, s=ss.h: f(e).then_inc(s, 1))
            return v
        if sem is None:
            self.em(eng, fn)
            return None
        v = sem.bump(k)
        self.em(eng, lambda e, f=fn, s=sem.h, kk=k: f(e).then_inc(s, kk))
        return v

    def nopsig(self, eng, sem, k=1):
        ss = self.self_sems.get(eng)
        if ss is not None:
            assert sem is ss
            return ss.n
        v = sem.bump(k)
        self.em(eng, lambda e, s=sem.h, kk=k: e.nop().then_inc(s, kk))
        return v

    def dma(self, eng, dst, src, sem, **kw):
        v = sem.bump(16)
        self.em(eng, lambda e, d=dst, s=src, ss=sem.h, kw2=dict(kw): e.dma_start(d, s, **kw2).then_inc(ss, 16))
        return v


def build(cfg):
    nc = bass.Bass("TRN2", target_bir_lowering=False)
    B, S, DM, H, I = cfg.B, cfg.S, cfg.DM, cfg.H, cfg.I
    HL, TOKC, T8, DCH, NQC, CH, KT = cfg.HL, cfg.TOKC, cfg.T8, cfg.DCH, cfg.NQC, cfg.CH, cfg.KT
    SPB = S // 4
    NDIAG = cfg.NDIAG
    QCT = DM // 512
    NT = min(512, TOKC)
    NTT = TOKC // NT
    ICT = cfg.ICT
    TB, NTB, T128 = cfg.TB, cfg.NTB, cfg.T128
    taps = cfg.stop_after

    di = lambda n, sh, dt: nc.dram_tensor(n, sh, dt, kind="ExternalInput").ap()
    do = lambda n, sh, dt: nc.dram_tensor(n, sh, dt, kind="ExternalOutput").ap()
    dint = lambda n, sh, dt, out=False: (do(n, sh, dt) if out else nc.dram_tensor(n, sh, dt, kind="Internal").ap())

    xT_loc = di("xT_loc", [DM, TOKC], F32)
    w_qkv = di("w_qkv", [DM, 3 * DM], F32)
    w_out = di("w_out", [DM, DM], F32)
    q_gain = di("q_gain", [H], F32)
    gain_loc = di("gain_loc", [HL], F32)
    w_gate = di("w_gate", [DM, I], I8)
    w_up = di("w_up", [DM, I], I8)
    w_down = di("w_down", [I, DM], I8)
    s_gate = di("s_gate", [I], F32)
    s_up = di("s_up", [I], F32)
    s_down = di("s_down", [DM], F32)
    cosL = di("cosL", [TOKC, 64], F32)
    sinL = di("sinL", [TOKC, 64], F32)
    ident_in = di("ident", [128, 128], F32)
    offtab = di("offtab", [7 * HL], I32)

    BLK = 1536 * TOKC
    KOFF = 512 * TOKC
    VOFF = 1024 * TOKC
    agx_in = dint("agx_in", [HL * BLK], BF16, out=(taps == "A"))
    if taps != "A":
        agx_out = dint("agx_out", [4 * HL * BLK], BF16)
        ay_in = dint("ay_in", [4 * HL * 128, SPB], BF16, out=(taps == "B"))
    if taps in ("C", "C1", "C2", "C3"):
        ay_out = dint("ay_out", [16 * HL * 128, SPB], BF16)
        out_loc = do("out_loc", [TOKC, DM], F32)
    if taps == "C1":
        dbg_x1 = do("dbg_x1", [DM, TOKC], F32)
    if taps == "C2":
        dbg_xn2 = do("dbg_xn2", [DM, TOKC], F32)
    if taps == "C3":
        dbg_f = do("dbg_f", [I, TB], F32)

    PS0 = nc.alloc_psum_tensor("PS0", [128, 1024], F32)
    PS1 = nc.alloc_psum_tensor("PS1", [128, 1024], F32)
    PS2 = nc.alloc_psum_tensor("PS2", [128, 1024], F32)
    PT = nc.alloc_psum_tensor("PT", [128, 2048], BF16)

    sa = SAlloc(nc, (nc.sbuf_base + 63) // 64 * 64, nc.sbuf_top)
    ident_bf = sa.alloc("ident_bf", [128, 128], BF16)
    ident_f = sa.alloc("ident_f", [128, 128], F32)
    masks = sa.alloc("masks", [128, NDIAG * CH], BF16)
    cos_sb = sa.alloc("cos_sb", [128, T8 * 64], F32)
    sin_sb = sa.alloc("sin_sb", [128, T8 * 64], F32)
    ones_bf = sa.alloc("ones_bf", [128, 8], BF16)
    ones_fr = sa.alloc("ones_fr", [1, 128], F32)
    gain_sb = sa.alloc("gain_sb", [1, H], F32)
    gloc_sb = sa.alloc("gloc_sb", [1, HL], F32)
    negrow = sa.alloc("negrow", [1, HL], F32)
    gb = sa.alloc("gb", [128, H], F32)
    negabsg = sa.alloc("negabsg", [128, HL], F32)
    ind_sb = sa.alloc("ind_sb", [128, NQC * NQC], BF16)
    offtab_sb = sa.alloc("offtab_sb", [1, 7 * HL], I32)
    trash_s = sa.alloc("trash_s", [128, 1024], BF16)
    eps_un = sa.alloc("eps_un", [128, 1], F32)
    eps_ln = sa.alloc("eps_ln", [1, 1], F32)
    sg_sb = sa.alloc("sg_sb", [128, ICT], F32)
    su_sb = sa.alloc("su_sb", [128, ICT], F32)

    p = Plan(nc)
    gcc = Sem(nc, "gcc")
    gA = Sem(nc, "gA")
    tT = Sem(nc, "tT")
    vV = Sem(nc, "vV")
    sS = Sem(nc, "sS")
    g_const = Sem(nc, "g_const")
    sxd = Sem(nc, "sxd")
    gw = [Sem(nc, "gw0"), Sem(nc, "gw1")]
    gq = [Sem(nc, "gq0"), Sem(nc, "gq1")]
    gys = [Sem(nc, "gy%d" % i) for i in range(4)]
    h_const = Sem(nc, "h_const")
    hv = [Sem(nc, "hv0"), Sem(nc, "hv1")]
    ht = [Sem(nc, "ht0"), Sem(nc, "ht1")]
    hden = Sem(nc, "hden")
    hy = [Sem(nc, "hy0"), Sem(nc, "hy1")]
    hq = [Sem(nc, "hq0"), Sem(nc, "hq1")]
    hsd = Sem(nc, "hsd")
    hx = Sem(nc, "hx")
    ho = [Sem(nc, "ho0"), Sem(nc, "ho1")]
    all_g_dma = [g_const, sxd, gw[0], gw[1], gq[0], gq[1]] + gys
    p_selfsems_marker = True
    all_h_dma = [h_const, hv[0], hv[1], ht[0], ht[1], hden, hy[0], hy[1], hx, ho[0], ho[1], hq[0], hq[1], hsd]
    RPOOL = nc.alloc_register(mybir.EngineType.Pool, "roff")
    RSYNC = nc.alloc_register(mybir.EngineType.SP, "roffs")
    p.self_sems = {"vector": vV, "scalar": sS}
    gw_cons = {0: 0, 1: 0}

    p.dma("gpsimd", ident_bf[:, :], ident_in, g_const)
    p.dma("sync", ident_f[:, :], ident_in, h_const)
    p.dma("sync", cos_sb[:, :].rearrange("p (a j) -> p a j", j=64),
          cosL.rearrange("(a p) j -> p a j", p=128), h_const)
    p.dma("sync", sin_sb[:, :].rearrange("p (a j) -> p a j", j=64),
          sinL.rearrange("(a p) j -> p a j", p=128), h_const)
    p.dma("sync", gain_sb[0:1, :], q_gain[None, :], h_const)
    p.dma("sync", gloc_sb[0:1, :], gain_loc[None, :], h_const)
    h_off = p.dma("sync", offtab_sb[0:1, :], offtab[None, :], h_const)
    h_gain = h_const.n
    p.op("vector", lambda e: e.memset(ones_bf[:, :], 1.0), vV)
    p.op("vector", lambda e: e.memset(ones_fr[:, :], 1.0), vV)
    vz = p.op("vector", lambda e: e.memset(ind_sb[:, :], 0.0), vV)
    p.wait("vector", vV, vz)
    for qc in range(NQC):
        p.op("vector", lambda e, qc=qc: e.memset(ind_sb[:, qc * NQC + qc: qc * NQC + qc + 1], 1.0), vV)
    p.op("vector", lambda e: e.memset(masks[:, :], 1.0), vV)
    p.op("vector", lambda e: e.memset(eps_un[:, :], 1e-6), vV)
    p.op("vector", lambda e: e.memset(eps_ln[:, :], 1e-5), vV)
    v_consts = vV.n
    p.wait("gpsimd", vV, v_consts)
    for i in range(NDIAG):
        p.op("gpsimd", lambda e, i=i: e.affine_select(
            masks[:, i * CH:(i + 1) * CH], masks[:, i * CH:(i + 1) * CH],
            [[1, CH]], ALU.is_ge, 0.0, base=-i * 128, channel_multiplier=-1), gA)
    p.wait("scalar", h_const, h_gain)
    s_abs = p.op("scalar", lambda e: e.activation(negrow[0:1, :], gloc_sb[0:1, :], AF.Abs), sS)
    p.wait("vector", sS, s_abs)
    v_neg = p.op("vector", lambda e: e.tensor_scalar_mul(negrow[0:1, :], negrow[0:1, :], -1.0), vV)
    p.wait("tensor", h_const, h_gain)
    p.wait("tensor", vV, v_neg)
    p.op("tensor", lambda e: e.matmul(PS0[:, 0:H], ones_fr[0:1, :], gain_sb[0:1, :], start=True, stop=True), None)
    t_bc = p.op("tensor", lambda e: e.matmul(PS0[:, H:H + HL], ones_fr[0:1, :], negrow[0:1, :],
                                             start=True, stop=True), tT)
    p.wait("vector", tT, t_bc)
    p.op("vector", lambda e: e.tensor_copy(gb[:, :], PS0[:, 0:H]), vV)
    v_gainbc = p.op("vector", lambda e: e.tensor_copy(negabsg[:, :], PS0[:, H:H + HL]), vV)

    mrow = sa.alloc("mrow", [1, TOKC], F32)
    msqrow = sa.alloc("msqrow", [1, TOKC], F32)
    varrow = sa.alloc("varrow", [1, TOKC], F32)
    stdrow = sa.alloc("stdrow", [1, TOKC], F32)
    rrow = sa.alloc("rrow", [1, TOKC], F32)
    mrrow = sa.alloc("mrrow", [1, TOKC], F32)
    xsb = sa.alloc("xsb", [128, DCH * TOKC], BF16)
    xnc = sa.alloc("xnc", [128, TOKC], F32)
    xsq = sa.alloc("xsq", [128, 2 * TOKC], BF16)
    wqb = sa.alloc("wqb", [128, 2 * DCH * 512], BF16)
    qstage = sa.alloc("qstage", [128, H * T8 * 128], BF16)
    kstage = sa.alloc("kstage", [128, H * T8 * 128], BF16)
    vtmp = sa.alloc("vtmp", [128, 2 * 512], BF16)
    tstage = sa.alloc("tstage", [128, 2 * 512], BF16)
    rtmp = sa.alloc("rtmp", [128, 4 * T8 * 64], BF16)
    ssq4 = sa.alloc("ssq4", [128, 8], F32)
    std4 = sa.alloc("std4", [128, 8], F32)
    rq4 = sa.alloc("rq4", [128, 8], F32)

    xv = lambda c: xsb[:, c * TOKC:(c + 1) * TOKC]
    g_x = p.dma("gpsimd", xsb[:, :].rearrange("p (c t) -> p c t", t=TOKC),
                xT_loc.rearrange("(c p) t -> p c t", p=128), sxd)

    def ln_stats(src_of, sq_of, sqbuf, wait_vec=None):
        mmv = []
        sqv = []
        for c in range(DCH):
            if wait_vec is not None:
                p.wait("scalar", vV, wait_vec)
            if c >= 2:
                p.wait("scalar", tT, mmv[c - 2])
            sqv.append(p.op("scalar", lambda e, c=c: e.activation(
                sqbuf[:, (c % 2) * TOKC:(c % 2 + 1) * TOKC], src_of(c), AF.Square), sS))
            if wait_vec is not None:
                p.wait("tensor", vV, wait_vec)
            p.wait("tensor", sS, sqv[c])
            for tt in range(NTT):
                p.op("tensor", lambda e, c=c, tt=tt: e.matmul(
                    PS0[0:1, tt * NT:(tt + 1) * NT], ones_bf[:, 0:1],
                    src_of(c)[:, tt * NT:(tt + 1) * NT],
                    start=(c == 0), stop=(c == DCH - 1)), None)
                p.op("tensor", lambda e, c=c, tt=tt: e.matmul(
                    PS1[0:1, tt * NT:(tt + 1) * NT], ones_bf[:, 0:1],
                    sqbuf[:, (c % 2) * TOKC + tt * NT:(c % 2) * TOKC + (tt + 1) * NT],
                    start=(c == 0), stop=(c == DCH - 1)), tT if tt == NTT - 1 else None)
            mmv.append(tT.n)
        return tT.n

    def ln_finalize(t_wait, inv_n):
        p.wait("vector", tT, t_wait)
        p.op("vector", lambda e: e.tensor_scalar_mul(mrow[0:1, :], PS0[0:1, 0:TOKC], inv_n), None)
        p.op("vector", lambda e: e.tensor_scalar_mul(msqrow[0:1, :], PS1[0:1, 0:TOKC], inv_n), None)
        p.op("vector", lambda e: e.tensor_mul(varrow[0:1, :], mrow[0:1, :], mrow[0:1, :]), None)
        vsub = p.op("vector", lambda e: e.tensor_sub(varrow[0:1, :], msqrow[0:1, :], varrow[0:1, :]), vV)
        p.wait("scalar", vV, vsub)
        ss_ = p.op("scalar", lambda e: e.activation(stdrow[0:1, :], varrow[0:1, :], AF.Sqrt, bias=eps_ln[0:1, :]), sS)
        p.wait("vector", sS, ss_)
        p.op("vector", lambda e: e.reciprocal(rrow[0:1, :], stdrow[0:1, :]), None)
        vr = p.op("vector", lambda e: e.tensor_mul(mrrow[0:1, :], mrow[0:1, :], rrow[0:1, :]), vV)
        p.wait("tensor", vV, vr)
        for tt in range(NTT):
            p.op("tensor", lambda e, tt=tt: e.matmul(PS0[:, tt * NT:(tt + 1) * NT], ones_fr[0:1, :],
                                                     rrow[0:1, tt * NT:(tt + 1) * NT], start=True, stop=True), None)
            p.op("tensor", lambda e, tt=tt: e.matmul(PS1[:, tt * NT:(tt + 1) * NT], ones_fr[0:1, :],
                                                     mrrow[0:1, tt * NT:(tt + 1) * NT], start=True, stop=True),
                 tT if tt == NTT - 1 else None)
        return tT.n

    p.wait("scalar", sxd, g_x)
    p.wait("tensor", sxd, g_x)
    p.wait("tensor", vV, v_gainbc)
    t_st1 = ln_stats(xv, None, xsq)
    t_bc1 = ln_finalize(t_st1, 1.0 / DM)
    p.wait("vector", tT, t_bc1)
    for c in range(DCH):
        p.op("vector", lambda e, c=c: e.tensor_mul(xnc[:, :], xv(c), PS0[:, 0:TOKC]), None)
        p.op("vector", lambda e, c=c: e.tensor_sub(xv(c), xnc[:, :], PS1[:, 0:TOKC]), None)
    v_xn = p.nopsig("vector", vV)

    NCT = 3 * DM // 512
    wv = lambda par: wqb[:, :].rearrange("p (b c n) -> p b c n", b=2, n=512)[:, par]
    cosv = cos_sb[:, :].rearrange("p (a j) -> p a j", j=64)
    sinv = sin_sb[:, :].rearrange("p (a j) -> p a j", j=64)
    rt = lambda i: rtmp[:, :].rearrange("p (i a j) -> p i a j", i=4, j=64)[:, i]
    RG = [[0, 1, 2, 3], [4, 5, 6, 7]]
    g_ag0 = gcc.n
    evac_done = []
    psq_sig = []
    vdma = []
    vtile_done = []
    tcp = []
    tdma = []
    gidx = 0
    tgrp = 0
    ct_order = list(range(2 * QCT, 3 * QCT)) + [x for g in range(QCT) for x in (g, QCT + g)]
    for ci, ct in enumerate(ct_order):
        par_w = ci % 2
        p.wait("gpsimd", tT, gw_cons[par_w])
        g_w = p.dma("gpsimd", wv(par_w),
                    w_qkv[:, ct * 512:(ct + 1) * 512].rearrange("(c p) n -> p c n", p=128), gw[par_w])
        is_q = ct < QCT
        is_k = QCT <= ct < 2 * QCT
        h0 = 4 * (ct % QCT)
        for t8 in range(T8):
            gi = ci * T8 + t8
            par = gi % 2
            PSq = PS2[:, par * 512:(par + 1) * 512]
            p.wait("tensor", gw[par_w], g_w)
            p.wait("tensor", vV, v_xn)
            if gi >= 2:
                p.wait("tensor", vV, evac_done[gi - 2])
            for c in range(DCH):
                p.op("tensor", lambda e, c=c, t8=t8, ct=ct, par_w=par_w, PSq=PSq: e.matmul(
                    PSq, xsb[:, c * TOKC + t8 * 128: c * TOKC + (t8 + 1) * 128],
                    wv(par_w)[:, c, :], start=(c == 0), stop=(c == DCH - 1)),
                    tT if c == DCH - 1 else None)
            sig = tT.n
            psq_sig.append(sig)
            gw_cons[par_w] = sig
            if is_q or is_k:
                stg = qstage if is_q else kstage
                par2 = gi % 2
                p.wait("scalar", tT, sig)
                for j in range(4):
                    p.op("scalar", lambda e, j=j, PSq=PSq, par2=par2: e.activation(
                        trash_s[:, j * 128:(j + 1) * 128], PSq[:, j * 128:(j + 1) * 128], AF.Square,
                        accum_out=ssq4[:, par2 * 4 + j: par2 * 4 + j + 1]), None)
                ssig = p.op("scalar", lambda e, par2=par2: e.activation(
                    std4[:, par2 * 4:(par2 + 1) * 4], ssq4[:, par2 * 4:(par2 + 1) * 4],
                    AF.Sqrt, bias=eps_un[:, :]), sS)
                p.wait("vector", sS, ssig)
                p.op("vector", lambda e, par2=par2: e.reciprocal(
                    rq4[:, par2 * 4:(par2 + 1) * 4], std4[:, par2 * 4:(par2 + 1) * 4]), None)
                if is_q:
                    p.op("vector", lambda e, par2=par2, h0=h0: e.tensor_mul(
                        rq4[:, par2 * 4:(par2 + 1) * 4], rq4[:, par2 * 4:(par2 + 1) * 4],
                        gb[:, h0:h0 + 4]), None)
                for j in range(4):
                    p.op("vector", lambda e, j=j, h0=h0, t8=t8, par2=par2, PSq=PSq, stg=stg: e.tensor_scalar_mul(
                        stg[:, ((h0 + j) * T8 + t8) * 128:((h0 + j) * T8 + t8 + 1) * 128],
                        PSq[:, j * 128:(j + 1) * 128],
                        rq4[:, par2 * 4 + j: par2 * 4 + j + 1]), None)
                evac_done.append(p.nopsig("vector", vV))
            else:
                par2 = gi % 2
                p.wait("vector", tT, sig)
                if gidx >= 2:
                    p.wait("vector", hv[par2], vtile_done[gidx - 2])
                cp = p.op("vector", lambda e, par2=par2, PSq=PSq: e.tensor_copy(
                    vtmp[:, par2 * 512:(par2 + 1) * 512], PSq), vV)
                evac_done.append(cp)
                p.wait("sync", vV, cp)
                nsub = 4 // HL if HL < 4 else 1
                hps = 4 // nsub
                for s_i in range(nsub):
                    hq0 = h0 + s_i * hps
                    g_grp = hq0 // HL
                    off = VOFF + g_grp * TOKC * 128 + t8 * 128 * 128
                    vdma.append(p.dma("sync",
                                      bass.AP(agx_in.tensor, off,
                                              [[128, 128], [BLK, hps], [1, 128]]),
                                      vtmp[:, par2 * 512 + s_i * hps * 128: par2 * 512 + (s_i + 1) * hps * 128]
                                      .rearrange("p (h d) -> p h d", d=128),
                                      hv[par2]))
                vtile_done.append(hv[par2].n)
                gidx += 1
        if is_k:
            g = ct - QCT
            for stg in (qstage, kstage):
                sv = stg[:, :].rearrange("p (h a d) -> p h a d", h=H, d=128)
                for h in range(h0, h0 + 4):
                    if h % 2 == 0:
                        lo = sv[:, h, :, 0:64]
                        hi = sv[:, h, :, 64:128]
                        p.op("vector", lambda e, lo=lo: e.tensor_mul(rt(0), lo, cosv), None)
                        p.op("vector", lambda e, hi=hi: e.tensor_mul(rt(1), hi, sinv), None)
                        p.op("vector", lambda e, lo=lo: e.tensor_mul(rt(2), lo, sinv), None)
                        p.op("vector", lambda e, hi=hi: e.tensor_mul(rt(3), hi, cosv), None)
                        p.op("vector", lambda e, lo=lo: e.tensor_sub(lo, rt(0), rt(1)), None)
                        p.op("vector", lambda e, hi=hi: e.tensor_add(hi, rt(2), rt(3)), None)
            v_hset = p.nopsig("vector", vV)
            p.wait("tensor", vV, v_hset)
            p.wait("tensor", g_const)
            for stg, koff in ((qstage, 0), (kstage, KOFF)):
                for h in range(h0, h0 + 4):
                    ghead = h // HL
                    hh_l = h % HL
                    for grp in range(max(1, T8 // 4)):
                        n4 = min(4, T8 - grp * 4)
                        par_t = tgrp % 2
                        if tgrp >= 2:
                            p.wait("tensor", vV, tcp[tgrp - 2])
                        for j in range(n4):
                            t8 = grp * 4 + j
                            p.op("tensor", lambda e, h=h, t8=t8, par_t=par_t, j=j, stg=stg: e.transpose(
                                PT[:, par_t * 1024 + j * 128: par_t * 1024 + (j + 1) * 128],
                                stg[:, (h * T8 + t8) * 128:(h * T8 + t8 + 1) * 128],
                                ident_bf[:, :]), tT if j == n4 - 1 else None)
                        ts = tT.n
                        p.wait("vector", tT, ts)
                        if tgrp >= 2:
                            p.wait("vector", ht[tgrp % 2], tdma[tgrp - 2])
                        c2 = p.op("vector", lambda e, par_t=par_t, n4=n4: e.tensor_copy(
                            tstage[:, par_t * 512:par_t * 512 + n4 * 128],
                            PT[:, par_t * 1024:par_t * 1024 + n4 * 128]), vV)
                        tcp.append(c2)
                        p.wait("sync", vV, c2)
                        tdma.append(p.dma("sync",
                                          bass.AP(agx_in.tensor,
                                                  hh_l * BLK + koff + ghead * 128 * TOKC + grp * 512,
                                                  [[TOKC, 128], [1, n4 * 128]]),
                                          tstage[:, par_t * 512:par_t * 512 + n4 * 128], ht[par_t]))
                        tgrp += 1
    sa.free("cos_sb")
    sa.free("sin_sb")
    if taps != "A":
        p.wait("gpsimd", ht[0])
        p.wait("gpsimd", ht[1])
        p.wait("gpsimd", hv[0])
        p.wait("gpsimd", hv[1])
        for hh in range(HL):
            p.op("gpsimd", lambda e, hh=hh: e.collective_compute(
                "AllGather", ALU.bypass, replica_groups=RG,
                ins=[agx_in[hh * BLK:(hh + 1) * BLK]],
                outs=[agx_out[hh * 4 * BLK:(hh + 1) * 4 * BLK]]), gcc)
    g_agqkv = gcc.n
    for n in ("xsb", "xnc", "xsq", "wqb", "qstage", "kstage", "vtmp", "tstage", "rtmp", "ssq4", "std4", "rq4"):
        sa.free(n)

    if taps == "A":
        finish(p, nc, all_g_dma, all_h_dma, gcc)
        return nc

    qsb = sa.alloc("qsb", [128, 2 * S], BF16)
    ksb = sa.alloc("ksb", [128, 2 * S], BF16)
    vsb = sa.alloc("vsb", [128, 2 * S], BF16)
    y_un = sa.alloc("y_un", [128, S], BF16)
    pTs = sa.alloc("pTs", [128, 4 * CH], BF16)
    den_sb = sa.alloc("den_sb", [NQC, CH], F32)
    den_rows = sa.alloc("den_rows", [1, NQC * CH], F32)
    ysc = sa.alloc("ysc", [128, 2 * CH], BF16)

    def regdma(dst_ap, src_handle, tabidx, pattern, sem, eng="gpsimd"):
        v = sem.bump(16)
        reg = RPOOL if eng == "gpsimd" else RSYNC

        def f(e, dst=dst_ap, th=src_handle, ti=tabidx, pat=pattern, s=sem.h, reg=reg):
            e.reg_load(reg, offtab_sb[0:1, ti:ti + 1])
            e.dma_start(dst, bass.AP(th, reg, pat)).then_inc(s, 16)
        p.em(eng, f)
        return v

    agx_h = agx_out.tensor
    b_ld = []
    b_ldv = []
    b_freed = []
    ydma = []
    yqi = 0
    g_ag0 = g_agqkv - HL
    p.wait("gpsimd", h_const, h_off)
    for hh in range(HL):
        par = hh % 2
        p.wait("gpsimd", gcc, g_ag0 + hh + 1)
        if hh >= 2:
            p.wait("gpsimd", tT, b_freed[hh - 2])
        regdma(qsb[:, par * S:(par + 1) * S].rearrange("p (a t) -> p a t", t=SPB),
               agx_h, 4 * HL + hh, [[SPB, 128], [BLK, 4], [1, SPB]], gq[par])
        regdma(ksb[:, par * S:(par + 1) * S].rearrange("p (a t) -> p a t", t=SPB),
               agx_h, 5 * HL + hh, [[SPB, 128], [BLK, 4], [1, SPB]], gq[par])
        p.wait("sync", gcc, g_ag0 + hh + 1)
        if hh == 0:
            p.wait("sync", h_const, h_off)
        if hh >= 2:
            p.wait("sync", tT, b_freed[hh - 2])
        for rp in range(4):
            regdma(vsb[:, par * S + rp * SPB: par * S + (rp + 1) * SPB].rearrange("p (b d) -> p b d", d=128),
                   agx_h, hh * 4 + rp, [[128, 128], [128 * 128, SPB // 128], [1, 128]], hq[par], eng="sync")
        b_ld.append(gq[par].n)
        b_ldv.append(hq[par].n)
        p.wait("tensor", vV)
        exp_sig = []
        cons_sig = []
        av_ev = []
        den_started = False
        pairs = []
        for qc in range(NQC):
            nkt = (qc + 1) * CH // KT
            for jp in range(nkt // 2):
                pairs.append((qc, jp, nkt))
        NP = len(pairs)

        def emit_scores(j):
            qc, jp, nkt = pairs[j]
            ppar = j % 2
            PSsc = PS0 if ppar == 0 else PS1
            p.wait("tensor", gq[par], b_ld[hh])
            p.wait("tensor", hq[par], b_ldv[hh])
            if j >= 2:
                p.wait("tensor", sS, exp_sig[j - 2])
            for u in range(2):
                kt = jp * 2 + u
                p.op("tensor", lambda e, kt=kt, u=u, PSsc=PSsc, qc=qc, par=par: e.matmul(
                    PSsc[:, u * CH:(u + 1) * CH],
                    ksb[:, par * S + kt * KT: par * S + (kt + 1) * KT],
                    qsb[:, par * S + qc * CH: par * S + (qc + 1) * CH],
                    start=True, stop=True, skip_group_check=True), tT if u == 1 else None)
            return tT.n

        def emit_exp_mask(j, scs):
            qc, jp, nkt = pairs[j]
            ppar = j % 2
            PSsc = PS0 if ppar == 0 else PS1
            p.wait("scalar", tT, scs)
            if j >= 2:
                p.wait("scalar", tT, cons_sig[j - 2])
            es = p.op("scalar", lambda e, ppar=ppar, PSsc=PSsc, hh=hh: e.activation(
                pTs[:, ppar * 2 * CH:(ppar + 1) * 2 * CH], PSsc[:, 0:2 * CH], AF.Exp,
                bias=negabsg[:, hh:hh + 1]), sS)
            exp_sig.append(es)
            base_kt = jp * 2
            need_mask = [u for u in range(2) if (base_kt + u) * KT >= qc * CH]
            vm = None
            if need_mask:
                p.wait("vector", sS, es)
                for u in need_mask:
                    kt = base_kt + u
                    i = (kt * KT - qc * CH) // 128
                    vm = p.op("vector", lambda e, u=u, ppar=ppar, i=i: e.tensor_mul(
                        pTs[:, (ppar * 2 + u) * CH:(ppar * 2 + u + 1) * CH],
                        pTs[:, (ppar * 2 + u) * CH:(ppar * 2 + u + 1) * CH],
                        masks[:, i * CH:(i + 1) * CH]), vV)
            return es, vm

        def emit_av(j, es, vm):
            nonlocal den_started
            qc, jp, nkt = pairs[j]
            ppar = j % 2
            if vm is not None:
                p.wait("tensor", vV, vm)
            else:
                p.wait("tensor", sS, es)
            if jp == 0 and qc > 0:
                p.wait("tensor", vV, av_ev[qc - 1])
            for u in range(2):
                kt = jp * 2 + u
                p.op("tensor", lambda e, kt=kt, u=u, ppar=ppar, nkt=nkt, par=par: e.matmul(
                    PS2[:, 0:CH], vsb[:, par * S + kt * KT: par * S + (kt + 1) * KT],
                    pTs[:, (ppar * 2 + u) * CH:(ppar * 2 + u + 1) * CH],
                    start=(kt == 0), stop=(kt == nkt - 1), skip_group_check=True), None)
                p.op("tensor", lambda e, u=u, ppar=ppar, qc=qc, ds=den_started, last=(
                    qc == NQC - 1 and jp == nkt // 2 - 1 and u == 1): e.matmul(
                    PS2[0:NQC, 512:512 + CH], ind_sb[:, qc * NQC:(qc + 1) * NQC],
                    pTs[:, (ppar * 2 + u) * CH:(ppar * 2 + u + 1) * CH],
                    start=(not ds), stop=last, skip_group_check=True), tT if u == 1 else None)
                den_started = True
            cons_sig.append(tT.n)
            if jp == nkt // 2 - 1:
                p.wait("vector", tT, cons_sig[-1])
                av_ev.append(p.op("vector", lambda e, qc=qc: e.tensor_copy(
                    y_un[:, qc * CH:(qc + 1) * CH], PS2[:, 0:CH]), vV))

        scs = {0: emit_scores(0)}
        for j in range(NP):
            es, vm = emit_exp_mask(j, scs[j])
            if j + 1 < NP:
                scs[j + 1] = emit_scores(j + 1)
            emit_av(j, es, vm)
        b_freed.append(cons_sig[-1])
        vr = p.op("vector", lambda e: e.reciprocal(den_sb[:, :], PS2[0:NQC, 512:512 + CH]), vV)
        p.wait("sync", vV, vr)
        dsd = p.dma("sync", den_rows[0:1, :], den_sb[:, :], hden)
        yv = None
        for qc in range(NQC):
            p.wait("tensor", hden, dsd)
            p.wait("tensor", sS, exp_sig[-1])
            if yv is not None:
                p.wait("tensor", vV, yv)
            tb_ = p.op("tensor", lambda e, qc=qc: e.matmul(
                PS1[:, 0:CH], ones_fr[0:1, :], den_rows[0:1, qc * CH:(qc + 1) * CH],
                start=True, stop=True, skip_group_check=True), tT)
            p.wait("vector", tT, tb_)
            if yqi >= 2:
                p.wait("vector", hy[yqi % 2], ydma[yqi - 2])
            yv = p.op("vector", lambda e, qc=qc, yqi=yqi: e.tensor_mul(
                ysc[:, (yqi % 2) * CH:(yqi % 2 + 1) * CH],
                y_un[:, qc * CH:(qc + 1) * CH], PS1[:, 0:CH]), vV)
            p.wait("sync", vV, yv)
            tb2 = qc * CH // SPB
            cof = (qc * CH) % SPB
            ydma.append(p.dma("sync",
                              ay_in[(hh * 4 + tb2) * 128:(hh * 4 + tb2 + 1) * 128, cof:cof + CH],
                              ysc[:, (yqi % 2) * CH:(yqi % 2 + 1) * CH], hy[yqi % 2]))
            yqi += 1
        if taps != "B":
            p.wait("gpsimd", hy[0])
            p.wait("gpsimd", hy[1])
            p.op("gpsimd", lambda e, hh=hh: e.collective_compute(
                "AllGather", ALU.bypass, replica_groups=RG,
                ins=[ay_in[hh * 512:(hh + 1) * 512, :]],
                outs=[ay_out[hh * 2048:(hh + 1) * 2048, :]]), gcc)
    if taps == "B":
        finish(p, nc, all_g_dma, all_h_dma, gcc)
        return nc
    g_agy = gcc.n

    for n in ("qsb", "ksb", "vsb", "y_un", "pTs", "den_sb", "den_rows", "ysc"):
        sa.free(n)
    sa.free("masks")
    yT = sa.alloc("yT", [128, DCH * TOKC], BF16)
    x1 = sa.alloc("x1", [128, DCH * TOKC], F32)
    wob = sa.alloc("wob", [128, DCH * DM], BF16)
    ay_h = ay_out.tensor
    p.wait("sync", hy[0])
    p.wait("sync", hy[1])
    p.wait("sync", hden)
    h_x1 = p.dma("sync", x1[:, :].rearrange("p (c t) -> p c t", t=TOKC),
                 xT_loc.rearrange("(c p) t -> p c t", p=128), hx)
    p.dma("sync", sg_sb[:, :], s_gate.rearrange("(a p) -> p a", p=128), hx,
          allow_slow_non_contiguous=True)
    h_su = p.dma("sync", su_sb[:, :], s_up.rearrange("(a p) -> p a", p=128), hx,
                 allow_slow_non_contiguous=True)
    h_x1 = h_su

    gwo = p.dma("gpsimd", wob[:, :].rearrange("p (c n) -> p c n", n=DM),
                w_out.rearrange("(c p) n -> p c n", p=128), gw[0])
    wo_ev = []
    wi = 0
    for hh in range(HL):
        p.wait("gpsimd", gcc, g_agy - HL + hh + 1)
        regdma(yT[:, hh * 4 * TOKC:(hh + 1) * 4 * TOKC].rearrange("p (s t) -> p s t", t=TOKC),
               ay_h, 6 * HL + hh, [[SPB, 128], [512 * SPB, 4], [1, SPB]], gys[hh])
        p.wait("tensor", gys[hh], gys[hh].n)
        if hh == 0:
            p.wait("tensor", gw[0], gwo)
        for oct_ in range(DCH):
            for tt in range(NTT):
                ppar = wi % 2
                PSo = PS0[:, ppar * 512: ppar * 512 + NT]
                if wi >= 2:
                    p.wait("tensor", vV, wo_ev[wi - 2])
                for src_ in range(4):
                    c = src_ * HL + hh
                    pos = hh * 4 + src_
                    p.op("tensor", lambda e, c=c, pos=pos, oct_=oct_, tt=tt, PSo=PSo, src_=src_: e.matmul(
                        PSo, wob[:, c * DM + oct_ * 128: c * DM + (oct_ + 1) * 128],
                        yT[:, pos * TOKC + tt * NT: pos * TOKC + (tt + 1) * NT],
                        start=(src_ == 0), stop=(src_ == 3)), tT if src_ == 3 else None)
                tsig = tT.n
                p.wait("vector", tT, tsig)
                p.wait("vector", hx, h_x1)
                wo_ev.append(p.op("vector", lambda e, oct_=oct_, tt=tt, PSo=PSo: e.tensor_add(
                    x1[:, oct_ * TOKC + tt * NT: oct_ * TOKC + (tt + 1) * NT], PSo,
                    x1[:, oct_ * TOKC + tt * NT: oct_ * TOKC + (tt + 1) * NT]), vV))
                wi += 1
    v_x1 = p.nopsig("vector", vV)
    if taps == "C1":
        p.wait("sync", vV, v_x1)
        p.dma("sync", dbg_x1.rearrange("(c p) t -> p c t", p=128),
              x1[:, :].rearrange("p (c t) -> p c t", t=TOKC), hx)
        finish(p, nc, all_g_dma, all_h_dma, gcc)
        return nc
    sa.free("yT")
    sa.free("wob")

    xn2 = sa.alloc("xn2", [128, DCH * TOKC], BF16)
    xsq2 = sa.alloc("xsq2", [128, 2 * TOKC], BF16)
    xnc2 = sa.alloc("xnc2", [128, TOKC], F32)
    p.wait("vector", vV, v_x1)
    for c in range(DCH):
        p.op("vector", lambda e, c=c: e.tensor_copy(xn2[:, c * TOKC:(c + 1) * TOKC],
                                                    x1[:, c * TOKC:(c + 1) * TOKC]), None)
    v_xb = p.nopsig("vector", vV)
    xv2 = lambda c: xn2[:, c * TOKC:(c + 1) * TOKC]
    t_st2 = ln_stats(xv2, None, xsq2, wait_vec=v_xb)
    t_bc2 = ln_finalize(t_st2, 1.0 / DM)
    for n in ("mrow", "msqrow", "varrow", "stdrow", "rrow", "mrrow"):
        sa.free(n)
    p.wait("vector", tT, t_bc2)
    for c in range(DCH):
        p.op("vector", lambda e, c=c: e.tensor_mul(xnc2[:, :], xv2(c), PS0[:, 0:TOKC]), None)
        p.op("vector", lambda e, c=c: e.tensor_sub(xv2(c), xnc2[:, :], PS1[:, 0:TOKC]), None)
    v_xn2 = p.nopsig("vector", vV)
    if taps == "C2":
        p.wait("gpsimd", vV, v_xn2)
        p.dma("gpsimd", dbg_xn2.rearrange("(c p) t -> p c t", p=128),
              xn2[:, :].rearrange("p (c t) -> p c t", t=TOKC), sxd)
        finish(p, nc, all_g_dma, all_h_dma, gcc)
        return nc

    p.wait("tensor", h_const)
    sa.free("xsq2")
    sa.free("xnc2")
    x1T = sa.alloc("x1T", [128, T8 * DM], BF16)
    tpi = 0
    tp_cp = []
    for t8 in range(T8):
        for c in range(DCH):
            par = tpi % 2
            if tpi >= 2:
                p.wait("tensor", vV, tp_cp[tpi - 2])
            tsg = p.op("tensor", lambda e, t8=t8, c=c, par=par: e.transpose(
                PS2[:, par * 512: par * 512 + 128],
                x1[:, c * TOKC + t8 * 128: c * TOKC + (t8 + 1) * 128], ident_f[:, :]), tT)
            p.wait("vector", tT, tsg)
            tp_cp.append(p.op("vector", lambda e, t8=t8, c=c, par=par: e.tensor_copy(
                x1T[:, t8 * DM + c * 128: t8 * DM + (c + 1) * 128],
                PS2[:, par * 512: par * 512 + 128]), vV))
            tpi += 1
    v_x1T = p.nopsig("vector", vV)
    gw_cons[0] = max(gw_cons[0], tT.n)
    gw_cons[1] = max(gw_cons[1], tT.n)
    sa.free("x1")

    fuse = sa.alloc("fuse", [128, ICT * TB], BF16)
    wgb = sa.alloc("wgb", [128, 2 * DCH * 256], BF16)
    wub = sa.alloc("wub", [128, 2 * DCH * 256], BF16)
    wdb = sa.alloc("wdb", [128, 2 * 4 * (DM // 2)], BF16)
    outb = sa.alloc("outb", [128, 2 * 512], F32)
    sdb = sa.alloc("sdb", [128, DM], F32)
    sd_row = sa.alloc("sd_row", [1, DM], F32)
    h_sd = p.dma("sync", sd_row[0:1, :], s_down[None, :], hsd)
    p.wait("tensor", hsd, h_sd)
    p.wait("tensor", vV, v_x1T)
    for tt in range(DM // 512):
        if tt > 0:
            p.wait("tensor", vV)
        tsd = p.op("tensor", lambda e, tt=tt: e.matmul(PS2[:, 0:512], ones_fr[0:1, :],
                                                       sd_row[0:1, tt * 512:(tt + 1) * 512],
                                                       start=True, stop=True), tT)
        p.wait("vector", tT, tsd)
        p.op("vector", lambda e, tt=tt: e.tensor_copy(sdb[:, tt * 512:(tt + 1) * 512], PS2[:, 0:512]), vV)
    v_sdb = p.nopsig("vector", vV)
    sa.free("sd_row")

    NICG = ICT // 2
    wgv = lambda par: wgb[:, :].rearrange("p (b c n) -> p b c n", b=2, n=256)[:, par]
    wuv = lambda par: wub[:, :].rearrange("p (b c n) -> p b c n", b=2, n=256)[:, par]
    wdv = lambda par: wdb[:, :].rearrange("p (b a n) -> p b a n", b=2, a=4)[:, par]
    odma = []
    oi = 0
    for tb in range(NTB):
        gu_ev = []
        gu_sact = []
        gu_mm = []
        for icg in range(NICG):
            par = icg % 2
            p.wait("gpsimd", tT, gw_cons[par])
            p.dma("gpsimd", wgv(par), w_gate[:, icg * 256:(icg + 1) * 256]
                  .rearrange("(c p) n -> p c n", p=128), gw[par])
            gw2 = p.dma("gpsimd", wuv(par), w_up[:, icg * 256:(icg + 1) * 256]
                        .rearrange("(c p) n -> p c n", p=128), gw[par])
            for j in range(2):
                ic = icg * 2 + j
                pgu = ic % 2
                PSgu = PS2 if pgu == 0 else PS0
                PSg = PSgu[:, 0:TB]
                PSu = PSgu[:, 512:512 + TB]
                p.wait("tensor", gw[par], gw2)
                p.wait("tensor", vV, v_xn2)
                p.wait("tensor", vV, v_sdb)
                if ic >= 2:
                    p.wait("tensor", vV, gu_ev[ic - 2])
                    p.wait("tensor", sS, gu_sact[ic - 2])
                for c in range(DCH):
                    p.op("tensor", lambda e, c=c, j=j, par=par, tb=tb, PSg=PSg: e.matmul(
                        PSg, wgv(par)[:, c, j * 128:(j + 1) * 128],
                        xn2[:, c * TOKC + tb * TB: c * TOKC + (tb + 1) * TB],
                        start=(c == 0), stop=(c == DCH - 1)), None)
                for c in range(DCH):
                    p.op("tensor", lambda e, c=c, j=j, par=par, tb=tb, PSu=PSu: e.matmul(
                        PSu, wuv(par)[:, c, j * 128:(j + 1) * 128],
                        xn2[:, c * TOKC + tb * TB: c * TOKC + (tb + 1) * TB],
                        start=(c == 0), stop=(c == DCH - 1)), tT if c == DCH - 1 else None)
                tsg = tT.n
                gu_mm.append(tT.n)
                p.wait("scalar", tT, tsg)
                if ic >= 2:
                    p.wait("scalar", vV, gu_ev[ic - 2])
                p.op("scalar", lambda e, ic=ic, PSg=PSg, pgu=pgu: e.activation(
                    trash_s[:, pgu * TB:(pgu + 1) * TB], PSg, AF.Silu, scale=sg_sb[:, ic:ic + 1]), None)
                gu_sact.append(p.nopsig("scalar", sS))
                p.wait("vector", sS, gu_sact[-1])
                gu_ev.append(p.op("vector", lambda e, ic=ic, PSu=PSu, pgu=pgu: e.scalar_tensor_tensor(
                    fuse[:, ic * TB:(ic + 1) * TB], PSu, su_sb[:, ic:ic + 1], trash_s[:, pgu * TB:(pgu + 1) * TB],
                    ALU.mult, ALU.mult), vV))
                gw_cons[par] = tT.n
        v_fuse = p.nopsig("vector", vV)
        if taps == "C3" and tb == 0:
            p.wait("gpsimd", vV, v_fuse)
            p.dma("gpsimd", dbg_f.rearrange("(c p) t -> p c t", p=128),
                  fuse[:, :].rearrange("p (c t) -> p c t", t=TB), sxd)
            finish(p, nc, all_g_dma, all_h_dma, gcc)
            return nc
        for t1p in range(max(1, T128 // 2)):
            nt1 = min(2, T128)
            NOG2 = max(1, DM // 1024)
            for oph in range(2):
                dn_mm = []
                p.wait("tensor", vV)
                for icg4 in range(ICT // 4):
                    par = icg4 % 2
                    p.wait("gpsimd", tT, gw_cons[par])
                    gwd = p.dma("gpsimd", wdv(par),
                                w_down[icg4 * 512:(icg4 + 1) * 512, oph * (DM // 2):(oph + 1) * (DM // 2)]
                                .rearrange("(a p) n -> p a n", p=128),
                                gw[par])
                    p.wait("tensor", gw[par], gwd)
                    p.wait("tensor", vV, v_fuse)
                    for j4 in range(4):
                        ic = icg4 * 4 + j4
                        for tt1 in range(nt1):
                            t1 = t1p * 2 + tt1
                            for og2 in range(NOG2):
                                PSd = (PS0 if tt1 == 0 else PS1)[:, og2 * 512:(og2 + 1) * 512]
                                p.op("tensor", lambda e, ic=ic, j4=j4, og2=og2, par=par, t1=t1, tb=tb, PSd=PSd: e.matmul(
                                    PSd, fuse[:, ic * TB + t1 * 128: ic * TB + (t1 + 1) * 128],
                                    wdv(par)[:, j4, og2 * 512:(og2 + 1) * 512],
                                    start=(ic == 0), stop=(ic == ICT - 1), skip_group_check=True),
                                    tT if (j4 == 3 and tt1 == nt1 - 1 and og2 == NOG2 - 1) else None)
                    dn_mm.append(tT.n)
                    gw_cons[par] = tT.n
                for tt1 in range(nt1):
                    t1 = t1p * 2 + tt1
                    trow = tb * TB + t1 * 128
                    for og2 in range(NOG2):
                        og = oph * NOG2 + og2
                        PSd = (PS0 if tt1 == 0 else PS1)[:, og2 * 512:(og2 + 1) * 512]
                        p.wait("vector", tT, dn_mm[-1])
                        obp = oi % 2
                        if oi >= 2:
                            p.wait("vector", ho[oi % 2], odma[oi - 2])
                        p.op("vector", lambda e, og=og, PSd=PSd, obp=obp: e.tensor_mul(
                            outb[:, obp * 512:(obp + 1) * 512], PSd, sdb[:, og * 512:(og + 1) * 512]), None)
                        vo = p.op("vector", lambda e, og=og, trow=trow, obp=obp: e.tensor_add(
                            outb[:, obp * 512:(obp + 1) * 512], outb[:, obp * 512:(obp + 1) * 512],
                            x1T[:, (trow // 128) * DM + og * 512: (trow // 128) * DM + (og + 1) * 512]), vV)
                        p.wait("sync", vV, vo)
                        odma.append(p.dma("sync", out_loc[trow:trow + 128, og * 512:(og + 1) * 512],
                                          outb[:, obp * 512:(obp + 1) * 512], ho[oi % 2]))
                        oi += 1
    finish(p, nc, all_g_dma, all_h_dma, gcc)
    return nc


def finish(p, nc, all_g_dma, all_h_dma, gcc):
    for s_ in all_h_dma:
        p.wait("sync", s_)
    for s_ in all_g_dma:
        p.wait("gpsimd", s_)
    p.wait("gpsimd", gcc)
    with nc.Block() as block:
        @block.sync
        def _(e):
            for f in p.q["sync"]:
                f(e)

        @block.scalar
        def _(e):
            for f in p.q["scalar"]:
                f(e)

        @block.tensor
        def _(e):
            for f in p.q["tensor"]:
                f(e)

        @block.vector
        def _(e):
            for f in p.q["vector"]:
                f(e)

        @block.gpsimd
        def _(e):
            for f in p.q["gpsimd"]:
                f(e)


class _Cfg:
    def __init__(self):
        self.B = 2; self.S = 4096; self.H = 16; self.dh = 128
        self.DM = 2048; self.I = 8192; self.CH = 512; self.KT = 128; self.NC = 8
        self.HL = 4; self.TOKC = 1024; self.T8 = 8; self.DCH = 16
        self.NQC = 8; self.ICT = 64; self.NDIAG = 4; self.QCT = 4
        self.stop_after = "C"
        self.TB = 512; self.NTB = 2; self.T128 = 4


def _host_inputs(cfg, x, w_qkv, w_out, q_gain, w_gate, w_up, w_down, s_gate, s_up, s_down):
    H, HL, SPB = cfg.H, cfg.HL, cfg.S // 4
    x = np.ascontiguousarray(np.asarray(x, dtype=np.float32))
    xT = np.ascontiguousarray(np.transpose(x, (0, 2, 1)))
    inv = np.power(10000.0, -np.arange(64, dtype=np.float64) / 64.0)
    th = np.arange(cfg.S, dtype=np.float64)[:, None] * inv[None, :]
    cosT = np.cos(th).astype(np.float32)
    sinT = np.sin(th).astype(np.float32)
    ident = np.eye(128, dtype=np.float32)
    q_gain = np.asarray(q_gain, dtype=np.float32)
    in_maps = []
    for c in range(cfg.NC):
        b, r = c // 4, c % 4
        s0 = r * SPB
        BLK = 1536 * SPB
        offs = []
        for hh in range(HL):
            for rp in range(4):
                offs.append(hh * 4 * BLK + rp * BLK + 1024 * SPB + r * SPB * 128)
        for hh in range(HL):
            offs.append(hh * 4 * BLK + r * 128 * SPB)
        for hh in range(HL):
            offs.append(hh * 4 * BLK + 512 * SPB + r * 128 * SPB)
        for hh in range(HL):
            offs.append(hh * 2048 * SPB + r * 128 * SPB)
        in_maps.append({
            "xT_loc": np.ascontiguousarray(xT[b, :, s0:s0 + SPB]),
            "w_qkv": np.ascontiguousarray(np.asarray(w_qkv, dtype=np.float32)),
            "w_out": np.ascontiguousarray(np.asarray(w_out, dtype=np.float32)),
            "q_gain": q_gain,
            "gain_loc": np.ascontiguousarray(q_gain[r * HL:(r + 1) * HL]),
            "w_gate": np.ascontiguousarray(np.asarray(w_gate, dtype=np.int8)),
            "w_up": np.ascontiguousarray(np.asarray(w_up, dtype=np.int8)),
            "w_down": np.ascontiguousarray(np.asarray(w_down, dtype=np.int8)),
            "s_gate": np.ascontiguousarray(np.asarray(s_gate, dtype=np.float32)),
            "s_up": np.ascontiguousarray(np.asarray(s_up, dtype=np.float32)),
            "s_down": np.ascontiguousarray(np.asarray(s_down, dtype=np.float32)),
            "cosL": np.ascontiguousarray(cosT[s0:s0 + SPB]),
            "sinL": np.ascontiguousarray(sinT[s0:s0 + SPB]),
            "ident": ident,
            "offtab": np.asarray(offs, dtype=np.int32),
        })
    return in_maps


def kernel(L0_attn_x, L0_attn_w_qkv, L0_attn_w_out, L0_attn_q_gain,
           L0_ffn_w_gate, L0_ffn_w_up, L0_ffn_w_down,
           L0_ffn_s_gate, L0_ffn_s_up, L0_ffn_s_down):
    cfg = _Cfg()
    in_maps = _host_inputs(cfg, L0_attn_x, L0_attn_w_qkv, L0_attn_w_out, L0_attn_q_gain,
                           L0_ffn_w_gate, L0_ffn_w_up, L0_ffn_w_down,
                           L0_ffn_s_gate, L0_ffn_s_up, L0_ffn_s_down)
    nc = build(cfg)
    res = run_bass_kernel_spmd(nc, in_maps, core_ids=list(range(8)))
    SPB = cfg.S // 4
    out = np.empty((cfg.B, cfg.S, cfg.DM), dtype=np.float32)
    for c in range(cfg.NC):
        b, r = c // 4, c % 4
        out[b, r * SPB:(r + 1) * SPB, :] = np.asarray(res.results[c]["out_loc"])
    return out


# version 14
# speedup vs baseline: 1.0620x; 1.0620x over previous
import numpy as np
import concourse.bass as bass
import concourse.mybir as mybir
from concourse.bass_utils import run_bass_kernel_spmd

F32 = mybir.dt.float32
BF16 = mybir.dt.bfloat16
I8 = mybir.dt.int8
I32 = mybir.dt.int32
AF = mybir.ActivationFunctionType
ALU = mybir.AluOpType

ENGS = ("sync", "scalar", "tensor", "vector", "gpsimd")


class Sem:
    def __init__(self, nc, name):
        self.h = nc.alloc_semaphore(name)
        self.n = 0

    def bump(self, k=1):
        self.n += k
        return self.n


class SAlloc:
    def __init__(self, nc, base, limit):
        self.nc, self.base, self.limit = nc, base, limit
        self.used = []

    def alloc(self, name, shape, dtype, align=64):
        per = int(np.prod(shape[1:])) * mybir.dt.size(dtype)
        per = (per + align - 1) // align * align
        off = self.base
        for (o, e, _) in sorted(self.used):
            if off + per <= o:
                break
            off = max(off, e)
        assert off + per <= self.limit, f"SBUF overflow: {name} {per}B at {off}"
        self.used.append((off, off + per, name))
        return self.nc.alloc_sbuf_tensor_at(name, list(shape), dtype, offset=off)

    def free(self, name):
        self.used = [u for u in self.used if u[2] != name]


class Plan:

    def __init__(self, nc):
        self.nc = nc
        self.q = {e: [] for e in ENGS}
        self.self_sems = {}

    def em(self, eng, fn):
        self.q[eng].append(fn)

    def wait(self, eng, sem, val=None):
        v = sem.n if val is None else val
        if v > 0:
            self.em(eng, lambda e, s=sem.h, vv=v: e.wait_ge(s, vv))

    def op(self, eng, fn, sem=None, k=1):
        ss = self.self_sems.get(eng)
        if ss is not None:
            assert sem is None or sem is ss
            self.wait(eng, ss)
            v = ss.bump(1)
            self.em(eng, lambda e, f=fn, s=ss.h: f(e).then_inc(s, 1))
            return v
        if sem is None:
            self.em(eng, fn)
            return None
        v = sem.bump(k)
        self.em(eng, lambda e, f=fn, s=sem.h, kk=k: f(e).then_inc(s, kk))
        return v

    def nopsig(self, eng, sem, k=1):
        ss = self.self_sems.get(eng)
        if ss is not None:
            assert sem is ss
            return ss.n
        v = sem.bump(k)
        self.em(eng, lambda e, s=sem.h, kk=k: e.nop().then_inc(s, kk))
        return v

    def dma(self, eng, dst, src, sem, **kw):
        v = sem.bump(16)
        self.em(eng, lambda e, d=dst, s=src, ss=sem.h, kw2=dict(kw): e.dma_start(d, s, **kw2).then_inc(ss, 16))
        return v


def build(cfg):
    nc = bass.Bass("TRN2", target_bir_lowering=False)
    B, S, DM, H, I = cfg.B, cfg.S, cfg.DM, cfg.H, cfg.I
    HL, TOKC, T8, DCH, NQC, CH, KT = cfg.HL, cfg.TOKC, cfg.T8, cfg.DCH, cfg.NQC, cfg.CH, cfg.KT
    SPB = S // 4
    NDIAG = cfg.NDIAG
    QCT = DM // 512
    NT = min(512, TOKC)
    NTT = TOKC // NT
    ICT = cfg.ICT
    TB, NTB, T128 = cfg.TB, cfg.NTB, cfg.T128
    taps = cfg.stop_after

    di = lambda n, sh, dt: nc.dram_tensor(n, sh, dt, kind="ExternalInput").ap()
    do = lambda n, sh, dt: nc.dram_tensor(n, sh, dt, kind="ExternalOutput").ap()
    dint = lambda n, sh, dt, out=False: (do(n, sh, dt) if out else nc.dram_tensor(n, sh, dt, kind="Internal").ap())

    xT_loc = di("xT_loc", [DM, TOKC], F32)
    w_qkv = di("w_qkv", [DM, 3 * DM], F32)
    w_out = di("w_out", [DM, DM], F32)
    q_gain = di("q_gain", [H], F32)
    gain_loc = di("gain_loc", [HL], F32)
    w_gate = di("w_gate", [DM, I], I8)
    w_up = di("w_up", [DM, I], I8)
    w_down = di("w_down", [I, DM], I8)
    s_gate = di("s_gate", [I], F32)
    s_up = di("s_up", [I], F32)
    s_down = di("s_down", [DM], F32)
    cosL = di("cosL", [TOKC, 64], F32)
    sinL = di("sinL", [TOKC, 64], F32)
    ident_in = di("ident", [128, 128], F32)
    offtab = di("offtab", [7 * HL], I32)

    BLK = 1536 * TOKC
    KOFF = 512 * TOKC
    VOFF = 1024 * TOKC
    agx_in = dint("agx_in", [HL * BLK], BF16, out=(taps == "A"))
    if taps != "A":
        agx_out = dint("agx_out", [4 * HL * BLK], BF16)
        ay_in = dint("ay_in", [4 * HL * 128, SPB], BF16, out=(taps == "B"))
    if taps in ("C", "C1", "C2", "C3"):
        ay_out = dint("ay_out", [16 * HL * 128, SPB], BF16)
        out_loc = do("out_loc", [TOKC, DM], F32)
    if taps == "C1":
        dbg_x1 = do("dbg_x1", [DM, TOKC], F32)
    if taps == "C2":
        dbg_xn2 = do("dbg_xn2", [DM, TOKC], F32)
    if taps == "C3":
        dbg_f = do("dbg_f", [I, TB], F32)

    PS0 = nc.alloc_psum_tensor("PS0", [128, 1024], F32)
    PS1 = nc.alloc_psum_tensor("PS1", [128, 1024], F32)
    PS2 = nc.alloc_psum_tensor("PS2", [128, 1024], F32)
    _pt_guard = nc.psum_tensor("PT", [128, 2048], BF16)
    PT = _pt_guard.__enter__()
    PS3 = None

    sa = SAlloc(nc, (nc.sbuf_base + 63) // 64 * 64, nc.sbuf_top)
    ident_bf = sa.alloc("ident_bf", [128, 128], BF16)
    ident_f = sa.alloc("ident_f", [128, 128], F32)
    masks = sa.alloc("masks", [128, NDIAG * CH], BF16)
    cos_sb = sa.alloc("cos_sb", [128, T8 * 64], F32)
    sin_sb = sa.alloc("sin_sb", [128, T8 * 64], F32)
    ones_bf = sa.alloc("ones_bf", [128, 8], BF16)
    ones_fr = sa.alloc("ones_fr", [1, 128], F32)
    gain_sb = sa.alloc("gain_sb", [1, H], F32)
    gloc_sb = sa.alloc("gloc_sb", [1, HL], F32)
    negrow = sa.alloc("negrow", [1, HL], F32)
    gb = sa.alloc("gb", [128, H], F32)
    negabsg = sa.alloc("negabsg", [128, HL], F32)
    ind_sb = sa.alloc("ind_sb", [128, NQC * NQC], BF16)
    offtab_sb = sa.alloc("offtab_sb", [1, 7 * HL], I32)
    trash_s = sa.alloc("trash_s", [128, 1024], BF16)
    eps_un = sa.alloc("eps_un", [128, 1], F32)
    eps_ln = sa.alloc("eps_ln", [1, 1], F32)
    sg_sb = sa.alloc("sg_sb", [128, ICT], F32)
    su_sb = sa.alloc("su_sb", [128, ICT], F32)

    p = Plan(nc)
    gcc = Sem(nc, "gcc")
    gA = Sem(nc, "gA")
    tT = Sem(nc, "tT")
    vV = Sem(nc, "vV")
    sS = Sem(nc, "sS")
    g_const = Sem(nc, "g_const")
    sxd = Sem(nc, "sxd")
    gw = [Sem(nc, "gw0"), Sem(nc, "gw1")]
    gq = [Sem(nc, "gq0"), Sem(nc, "gq1")]
    gys = [Sem(nc, "gy%d" % i) for i in range(4)]
    h_const = Sem(nc, "h_const")
    hv = [Sem(nc, "hv0"), Sem(nc, "hv1")]
    ht = [Sem(nc, "ht0"), Sem(nc, "ht1")]
    hden = Sem(nc, "hden")
    hy = [Sem(nc, "hy0"), Sem(nc, "hy1")]
    hq = [Sem(nc, "hq0"), Sem(nc, "hq1")]
    hsd = Sem(nc, "hsd")
    hts = [Sem(nc, "hts0"), Sem(nc, "hts1")]
    hx = Sem(nc, "hx")
    ho = [Sem(nc, "ho0"), Sem(nc, "ho1")]
    all_g_dma = [g_const, sxd, gw[0], gw[1], gq[0], gq[1]] + gys
    p_selfsems_marker = True
    all_h_dma = [h_const, hv[0], hv[1], ht[0], ht[1], hden, hy[0], hy[1], hx, ho[0], ho[1], hq[0], hq[1], hsd]
    all_hts = hts
    RPOOL = nc.alloc_register(mybir.EngineType.Pool, "roff")
    RSYNC = nc.alloc_register(mybir.EngineType.SP, "roffs")
    p.self_sems = {"vector": vV, "scalar": sS}
    gw_cons = {0: 0, 1: 0}

    p.dma("gpsimd", ident_bf[:, :], ident_in, g_const)
    p.dma("sync", ident_f[:, :], ident_in, h_const)
    p.dma("sync", cos_sb[:, :].rearrange("p (a j) -> p a j", j=64),
          cosL.rearrange("(a p) j -> p a j", p=128), h_const)
    p.dma("sync", sin_sb[:, :].rearrange("p (a j) -> p a j", j=64),
          sinL.rearrange("(a p) j -> p a j", p=128), h_const)
    p.dma("sync", gain_sb[0:1, :], q_gain[None, :], h_const)
    p.dma("sync", gloc_sb[0:1, :], gain_loc[None, :], h_const)
    h_off = p.dma("sync", offtab_sb[0:1, :], offtab[None, :], h_const)
    h_gain = h_const.n
    p.op("vector", lambda e: e.memset(ones_bf[:, :], 1.0), vV)
    p.op("vector", lambda e: e.memset(ones_fr[:, :], 1.0), vV)
    vz = p.op("vector", lambda e: e.memset(ind_sb[:, :], 0.0), vV)
    p.wait("vector", vV, vz)
    for qc in range(NQC):
        p.op("vector", lambda e, qc=qc: e.memset(ind_sb[:, qc * NQC + qc: qc * NQC + qc + 1], 1.0), vV)
    p.op("vector", lambda e: e.memset(masks[:, :], 1.0), vV)
    p.op("vector", lambda e: e.memset(eps_un[:, :], 1e-6), vV)
    p.op("vector", lambda e: e.memset(eps_ln[:, :], 1e-5), vV)
    v_consts = vV.n
    p.wait("gpsimd", vV, v_consts)
    for i in range(NDIAG):
        p.op("gpsimd", lambda e, i=i: e.affine_select(
            masks[:, i * CH:(i + 1) * CH], masks[:, i * CH:(i + 1) * CH],
            [[1, CH]], ALU.is_ge, 0.0, base=-i * 128, channel_multiplier=-1), gA)
    p.wait("scalar", h_const, h_gain)
    s_abs = p.op("scalar", lambda e: e.activation(negrow[0:1, :], gloc_sb[0:1, :], AF.Abs), sS)
    p.wait("vector", sS, s_abs)
    v_neg = p.op("vector", lambda e: e.tensor_scalar_mul(negrow[0:1, :], negrow[0:1, :], -1.0), vV)
    p.wait("tensor", h_const, h_gain)
    p.wait("tensor", vV, v_neg)
    p.op("tensor", lambda e: e.matmul(PS0[:, 0:H], ones_fr[0:1, :], gain_sb[0:1, :], start=True, stop=True), None)
    t_bc = p.op("tensor", lambda e: e.matmul(PS0[:, H:H + HL], ones_fr[0:1, :], negrow[0:1, :],
                                             start=True, stop=True), tT)
    p.wait("vector", tT, t_bc)
    p.op("vector", lambda e: e.tensor_copy(gb[:, :], PS0[:, 0:H]), vV)
    v_gainbc = p.op("vector", lambda e: e.tensor_copy(negabsg[:, :], PS0[:, H:H + HL]), vV)

    mrow = sa.alloc("mrow", [1, TOKC], F32)
    msqrow = sa.alloc("msqrow", [1, TOKC], F32)
    varrow = sa.alloc("varrow", [1, TOKC], F32)
    stdrow = sa.alloc("stdrow", [1, TOKC], F32)
    rrow = sa.alloc("rrow", [1, TOKC], F32)
    mrrow = sa.alloc("mrrow", [1, TOKC], F32)
    xsb = sa.alloc("xsb", [128, DCH * TOKC], BF16)
    xnc = sa.alloc("xnc", [128, TOKC], F32)
    xsq = sa.alloc("xsq", [128, 2 * TOKC], BF16)
    wqb = sa.alloc("wqb", [128, 2 * DCH * 512], BF16)
    qstage = sa.alloc("qstage", [128, H * T8 * 128], BF16)
    kstage = sa.alloc("kstage", [128, H * T8 * 128], BF16)
    vtmp = sa.alloc("vtmp", [128, 2 * 512], BF16)
    tstage = sa.alloc("tstage", [128, 2 * 512], BF16)
    rtmp = sa.alloc("rtmp", [128, 4 * T8 * 64], BF16)
    ssq4 = sa.alloc("ssq4", [128, 8], F32)
    std4 = sa.alloc("std4", [128, 8], F32)
    rq4 = sa.alloc("rq4", [128, 8], F32)

    xv = lambda c: xsb[:, c * TOKC:(c + 1) * TOKC]
    g_x = p.dma("gpsimd", xsb[:, :].rearrange("p (c t) -> p c t", t=TOKC),
                xT_loc.rearrange("(c p) t -> p c t", p=128), sxd)

    def ln_stats(src_of, sq_of, sqbuf, wait_vec=None):
        mmv = []
        sqv = []
        for c in range(DCH):
            if wait_vec is not None:
                p.wait("scalar", vV, wait_vec)
            if c >= 2:
                p.wait("scalar", tT, mmv[c - 2])
            sqv.append(p.op("scalar", lambda e, c=c: e.activation(
                sqbuf[:, (c % 2) * TOKC:(c % 2 + 1) * TOKC], src_of(c), AF.Square), sS))
            if wait_vec is not None:
                p.wait("tensor", vV, wait_vec)
            p.wait("tensor", sS, sqv[c])
            for tt in range(NTT):
                p.op("tensor", lambda e, c=c, tt=tt: e.matmul(
                    PS0[0:1, tt * NT:(tt + 1) * NT], ones_bf[:, 0:1],
                    src_of(c)[:, tt * NT:(tt + 1) * NT],
                    start=(c == 0), stop=(c == DCH - 1)), None)
                p.op("tensor", lambda e, c=c, tt=tt: e.matmul(
                    PS1[0:1, tt * NT:(tt + 1) * NT], ones_bf[:, 0:1],
                    sqbuf[:, (c % 2) * TOKC + tt * NT:(c % 2) * TOKC + (tt + 1) * NT],
                    start=(c == 0), stop=(c == DCH - 1)), tT if tt == NTT - 1 else None)
            mmv.append(tT.n)
        return tT.n

    def ln_finalize(t_wait, inv_n):
        p.wait("vector", tT, t_wait)
        p.op("vector", lambda e: e.tensor_scalar_mul(mrow[0:1, :], PS0[0:1, 0:TOKC], inv_n), None)
        p.op("vector", lambda e: e.tensor_scalar_mul(msqrow[0:1, :], PS1[0:1, 0:TOKC], inv_n), None)
        p.op("vector", lambda e: e.tensor_mul(varrow[0:1, :], mrow[0:1, :], mrow[0:1, :]), None)
        vsub = p.op("vector", lambda e: e.tensor_sub(varrow[0:1, :], msqrow[0:1, :], varrow[0:1, :]), vV)
        p.wait("scalar", vV, vsub)
        ss_ = p.op("scalar", lambda e: e.activation(stdrow[0:1, :], varrow[0:1, :], AF.Sqrt, bias=eps_ln[0:1, :]), sS)
        p.wait("vector", sS, ss_)
        p.op("vector", lambda e: e.reciprocal(rrow[0:1, :], stdrow[0:1, :]), None)
        vr = p.op("vector", lambda e: e.tensor_mul(mrrow[0:1, :], mrow[0:1, :], rrow[0:1, :]), vV)
        p.wait("tensor", vV, vr)
        for tt in range(NTT):
            p.op("tensor", lambda e, tt=tt: e.matmul(PS0[:, tt * NT:(tt + 1) * NT], ones_fr[0:1, :],
                                                     rrow[0:1, tt * NT:(tt + 1) * NT], start=True, stop=True), None)
            p.op("tensor", lambda e, tt=tt: e.matmul(PS1[:, tt * NT:(tt + 1) * NT], ones_fr[0:1, :],
                                                     mrrow[0:1, tt * NT:(tt + 1) * NT], start=True, stop=True),
                 tT if tt == NTT - 1 else None)
        return tT.n

    p.wait("scalar", sxd, g_x)
    p.wait("tensor", sxd, g_x)
    p.wait("tensor", vV, v_gainbc)
    t_st1 = ln_stats(xv, None, xsq)
    t_bc1 = ln_finalize(t_st1, 1.0 / DM)
    p.wait("vector", tT, t_bc1)
    for c in range(DCH):
        p.op("vector", lambda e, c=c: e.tensor_mul(xnc[:, :], xv(c), PS0[:, 0:TOKC]), None)
        p.op("vector", lambda e, c=c: e.tensor_sub(xv(c), xnc[:, :], PS1[:, 0:TOKC]), None)
    v_xn = p.nopsig("vector", vV)

    NCT = 3 * DM // 512
    wv = lambda par: wqb[:, :].rearrange("p (b c n) -> p b c n", b=2, n=512)[:, par]
    cosv = cos_sb[:, :].rearrange("p (a j) -> p a j", j=64)
    sinv = sin_sb[:, :].rearrange("p (a j) -> p a j", j=64)
    rt = lambda i: rtmp[:, :].rearrange("p (i a j) -> p i a j", i=4, j=64)[:, i]
    RG = [[0, 1, 2, 3], [4, 5, 6, 7]]
    g_ag0 = gcc.n
    evac_done = []
    psq_sig = []
    vdma = []
    vtile_done = []
    tcp = []
    tdma = []
    gidx = 0
    tgrp = 0
    ct_order = list(range(2 * QCT, 3 * QCT)) + [x for g in range(QCT) for x in (g, QCT + g)]
    for ci, ct in enumerate(ct_order):
        par_w = ci % 2
        p.wait("gpsimd", tT, gw_cons[par_w])
        g_w = p.dma("gpsimd", wv(par_w),
                    w_qkv[:, ct * 512:(ct + 1) * 512].rearrange("(c p) n -> p c n", p=128), gw[par_w])
        is_q = ct < QCT
        is_k = QCT <= ct < 2 * QCT
        h0 = 4 * (ct % QCT)
        for t8 in range(T8):
            gi = ci * T8 + t8
            par = gi % 2
            PSq = PS2[:, par * 512:(par + 1) * 512]
            p.wait("tensor", gw[par_w], g_w)
            p.wait("tensor", vV, v_xn)
            if gi >= 2:
                p.wait("tensor", vV, evac_done[gi - 2])
            for c in range(DCH):
                p.op("tensor", lambda e, c=c, t8=t8, ct=ct, par_w=par_w, PSq=PSq: e.matmul(
                    PSq, xsb[:, c * TOKC + t8 * 128: c * TOKC + (t8 + 1) * 128],
                    wv(par_w)[:, c, :], start=(c == 0), stop=(c == DCH - 1)),
                    tT if c == DCH - 1 else None)
            sig = tT.n
            psq_sig.append(sig)
            gw_cons[par_w] = sig
            if is_q or is_k:
                stg = qstage if is_q else kstage
                par2 = gi % 2
                p.wait("scalar", tT, sig)
                for j in range(4):
                    p.op("scalar", lambda e, j=j, PSq=PSq, par2=par2: e.activation(
                        trash_s[:, j * 128:(j + 1) * 128], PSq[:, j * 128:(j + 1) * 128], AF.Square,
                        accum_out=ssq4[:, par2 * 4 + j: par2 * 4 + j + 1]), None)
                ssig = p.op("scalar", lambda e, par2=par2: e.activation(
                    std4[:, par2 * 4:(par2 + 1) * 4], ssq4[:, par2 * 4:(par2 + 1) * 4],
                    AF.Sqrt, bias=eps_un[:, :]), sS)
                p.wait("vector", sS, ssig)
                p.op("vector", lambda e, par2=par2: e.reciprocal(
                    rq4[:, par2 * 4:(par2 + 1) * 4], std4[:, par2 * 4:(par2 + 1) * 4]), None)
                if is_q:
                    p.op("vector", lambda e, par2=par2, h0=h0: e.tensor_mul(
                        rq4[:, par2 * 4:(par2 + 1) * 4], rq4[:, par2 * 4:(par2 + 1) * 4],
                        gb[:, h0:h0 + 4]), None)
                for j in range(4):
                    p.op("vector", lambda e, j=j, h0=h0, t8=t8, par2=par2, PSq=PSq, stg=stg: e.tensor_scalar_mul(
                        stg[:, ((h0 + j) * T8 + t8) * 128:((h0 + j) * T8 + t8 + 1) * 128],
                        PSq[:, j * 128:(j + 1) * 128],
                        rq4[:, par2 * 4 + j: par2 * 4 + j + 1]), None)
                evac_done.append(p.nopsig("vector", vV))
            else:
                par2 = gi % 2
                p.wait("vector", tT, sig)
                if gidx >= 2:
                    p.wait("vector", hv[par2], vtile_done[gidx - 2])
                cp = p.op("vector", lambda e, par2=par2, PSq=PSq: e.tensor_copy(
                    vtmp[:, par2 * 512:(par2 + 1) * 512], PSq), vV)
                evac_done.append(cp)
                p.wait("sync", vV, cp)
                nsub = 4 // HL if HL < 4 else 1
                hps = 4 // nsub
                for s_i in range(nsub):
                    hq0 = h0 + s_i * hps
                    g_grp = hq0 // HL
                    off = VOFF + g_grp * TOKC * 128 + t8 * 128 * 128
                    vdma.append(p.dma("sync",
                                      bass.AP(agx_in.tensor, off,
                                              [[128, 128], [BLK, hps], [1, 128]]),
                                      vtmp[:, par2 * 512 + s_i * hps * 128: par2 * 512 + (s_i + 1) * hps * 128]
                                      .rearrange("p (h d) -> p h d", d=128),
                                      hv[par2]))
                vtile_done.append(hv[par2].n)
                gidx += 1
        if is_k:
            g = ct - QCT
            for stg in (qstage, kstage):
                sv = stg[:, :].rearrange("p (h a d) -> p h a d", h=H, d=128)
                for h in range(h0, h0 + 4):
                    if h % 2 == 0:
                        lo = sv[:, h, :, 0:64]
                        hi = sv[:, h, :, 64:128]
                        p.op("vector", lambda e, lo=lo: e.tensor_mul(rt(0), lo, cosv), None)
                        p.op("vector", lambda e, hi=hi: e.tensor_mul(rt(1), hi, sinv), None)
                        p.op("vector", lambda e, lo=lo: e.tensor_mul(rt(2), lo, sinv), None)
                        p.op("vector", lambda e, hi=hi: e.tensor_mul(rt(3), hi, cosv), None)
                        p.op("vector", lambda e, lo=lo: e.tensor_sub(lo, rt(0), rt(1)), None)
                        p.op("vector", lambda e, hi=hi: e.tensor_add(hi, rt(2), rt(3)), None)
            v_hset = p.nopsig("vector", vV)
            p.wait("tensor", vV, v_hset)
            p.wait("tensor", g_const)
            for stg, koff in ((qstage, 0), (kstage, KOFF)):
                for h in range(h0, h0 + 4):
                    ghead = h // HL
                    hh_l = h % HL
                    for grp in range(max(1, T8 // 4)):
                        n4 = min(4, T8 - grp * 4)
                        par_t = tgrp % 2
                        if tgrp >= 2:
                            p.wait("tensor", vV, tcp[tgrp - 2])
                        for j in range(n4):
                            t8 = grp * 4 + j
                            p.op("tensor", lambda e, h=h, t8=t8, par_t=par_t, j=j, stg=stg: e.transpose(
                                PT[:, par_t * 1024 + j * 128: par_t * 1024 + (j + 1) * 128],
                                stg[:, (h * T8 + t8) * 128:(h * T8 + t8 + 1) * 128],
                                ident_bf[:, :]), tT if j == n4 - 1 else None)
                        ts = tT.n
                        p.wait("vector", tT, ts)
                        if tgrp >= 2:
                            p.wait("vector", ht[tgrp % 2], tdma[tgrp - 2])
                        c2 = p.op("vector", lambda e, par_t=par_t, n4=n4: e.tensor_copy(
                            tstage[:, par_t * 512:par_t * 512 + n4 * 128],
                            PT[:, par_t * 1024:par_t * 1024 + n4 * 128]), vV)
                        tcp.append(c2)
                        p.wait("sync", vV, c2)
                        tdma.append(p.dma("sync",
                                          bass.AP(agx_in.tensor,
                                                  hh_l * BLK + koff + ghead * 128 * TOKC + grp * 512,
                                                  [[TOKC, 128], [1, n4 * 128]]),
                                          tstage[:, par_t * 512:par_t * 512 + n4 * 128], ht[par_t]))
                        tgrp += 1
    sa.free("cos_sb")
    sa.free("sin_sb")
    if taps != "A":
        p.wait("gpsimd", ht[0])
        p.wait("gpsimd", ht[1])
        p.wait("gpsimd", hv[0])
        p.wait("gpsimd", hv[1])
        for hh in range(HL):
            p.op("gpsimd", lambda e, hh=hh: e.collective_compute(
                "AllGather", ALU.bypass, replica_groups=RG,
                ins=[agx_in[hh * BLK:(hh + 1) * BLK]],
                outs=[agx_out[hh * 4 * BLK:(hh + 1) * 4 * BLK]]), gcc)
    g_agqkv = gcc.n
    for n in ("xsb", "xnc", "xsq", "wqb", "qstage", "kstage", "vtmp", "tstage", "rtmp", "ssq4", "std4", "rq4"):
        sa.free(n)

    _pt_guard.__exit__(None, None, None)
    PS3 = nc.alloc_psum_tensor("PS3", [128, 1024], F32)
    if taps == "A":
        finish(p, nc, all_g_dma, all_h_dma, gcc, all_hts)
        return nc

    qsb = sa.alloc("qsb", [128, 2 * S], BF16)
    ksb = sa.alloc("ksb", [128, 2 * S], BF16)
    vsb = sa.alloc("vsb", [128, 2 * S], BF16)
    y_un = sa.alloc("y_un", [128, S], BF16)
    pTs = sa.alloc("pTs", [128, 6 * CH], BF16)
    den_sb = sa.alloc("den_sb", [NQC, CH], F32)
    den_rows = sa.alloc("den_rows", [1, NQC * CH], F32)
    ysc = sa.alloc("ysc", [128, 2 * CH], BF16)

    def regdma(dst_ap, src_handle, tabidx, pattern, sem, eng="gpsimd"):
        v = sem.bump(16)
        reg = RPOOL if eng == "gpsimd" else RSYNC

        def f(e, dst=dst_ap, th=src_handle, ti=tabidx, pat=pattern, s=sem.h, reg=reg):
            e.reg_load(reg, offtab_sb[0:1, ti:ti + 1])
            e.dma_start(dst, bass.AP(th, reg, pat)).then_inc(s, 16)
        p.em(eng, f)
        return v

    agx_h = agx_out.tensor
    b_ld = []
    b_ldv = []
    b_freed = []
    ydma = []
    yqi = 0
    g_ag0 = g_agqkv - HL
    p.wait("gpsimd", h_const, h_off)
    for hh in range(HL):
        par = hh % 2
        p.wait("gpsimd", gcc, g_ag0 + hh + 1)
        if hh >= 2:
            p.wait("gpsimd", tT, b_freed[hh - 2])
        regdma(qsb[:, par * S:(par + 1) * S].rearrange("p (a t) -> p a t", t=SPB),
               agx_h, 4 * HL + hh, [[SPB, 128], [BLK, 4], [1, SPB]], gq[par])
        regdma(ksb[:, par * S:(par + 1) * S].rearrange("p (a t) -> p a t", t=SPB),
               agx_h, 5 * HL + hh, [[SPB, 128], [BLK, 4], [1, SPB]], gq[par])
        p.wait("sync", gcc, g_ag0 + hh + 1)
        if hh == 0:
            p.wait("sync", h_const, h_off)
        if hh >= 2:
            p.wait("sync", tT, b_freed[hh - 2])
        for rp in range(4):
            regdma(vsb[:, par * S + rp * SPB: par * S + (rp + 1) * SPB].rearrange("p (b d) -> p b d", d=128),
                   agx_h, hh * 4 + rp, [[128, 128], [128 * 128, SPB // 128], [1, 128]], hq[par], eng="sync")
        b_ld.append(gq[par].n)
        b_ldv.append(hq[par].n)
        p.wait("tensor", vV)
        exp_sig = []
        cons_sig = []
        av_ev = []
        den_started = False
        pairs = []
        for qc in range(NQC):
            nkt = (qc + 1) * CH // KT
            for jp in range(nkt // 2):
                pairs.append((qc, jp, nkt))
        NP = len(pairs)

        def emit_scores(j):
            qc, jp, nkt = pairs[j]
            ppar = j % 3
            PSsc = (PS0, PS1, PS3)[ppar]
            p.wait("tensor", gq[par], b_ld[hh])
            p.wait("tensor", hq[par], b_ldv[hh])
            if j >= 3:
                p.wait("tensor", sS, exp_sig[j - 3])
            for u in range(2):
                kt = jp * 2 + u
                p.op("tensor", lambda e, kt=kt, u=u, PSsc=PSsc, qc=qc, par=par: e.matmul(
                    PSsc[:, u * CH:(u + 1) * CH],
                    ksb[:, par * S + kt * KT: par * S + (kt + 1) * KT],
                    qsb[:, par * S + qc * CH: par * S + (qc + 1) * CH],
                    start=True, stop=True, skip_group_check=True), tT if u == 1 else None)
            return tT.n

        def emit_exp_mask(j, scs):
            qc, jp, nkt = pairs[j]
            ppar = j % 3
            PSsc = (PS0, PS1, PS3)[ppar]
            p.wait("scalar", tT, scs)
            if j >= 3:
                p.wait("scalar", tT, cons_sig[j - 3])
            es = p.op("scalar", lambda e, ppar=ppar, PSsc=PSsc, hh=hh: e.activation(
                pTs[:, ppar * 2 * CH:(ppar + 1) * 2 * CH], PSsc[:, 0:2 * CH], AF.Exp,
                bias=negabsg[:, hh:hh + 1]), sS)
            exp_sig.append(es)
            base_kt = jp * 2
            need_mask = [u for u in range(2) if (base_kt + u) * KT >= qc * CH]
            vm = None
            if need_mask:
                p.wait("vector", sS, es)
                for u in need_mask:
                    kt = base_kt + u
                    i = (kt * KT - qc * CH) // 128
                    vm = p.op("vector", lambda e, u=u, ppar=ppar, i=i: e.tensor_mul(
                        pTs[:, (ppar * 2 + u) * CH:(ppar * 2 + u + 1) * CH],
                        pTs[:, (ppar * 2 + u) * CH:(ppar * 2 + u + 1) * CH],
                        masks[:, i * CH:(i + 1) * CH]), vV)
            return es, vm

        def emit_av(j, es, vm):
            nonlocal den_started
            qc, jp, nkt = pairs[j]
            ppar = j % 3
            if vm is not None:
                p.wait("tensor", vV, vm)
            else:
                p.wait("tensor", sS, es)
            if jp == 0 and qc > 0:
                p.wait("tensor", vV, av_ev[qc - 1])
            for u in range(2):
                kt = jp * 2 + u
                p.op("tensor", lambda e, kt=kt, u=u, ppar=ppar, nkt=nkt, par=par: e.matmul(
                    PS2[:, 0:CH], vsb[:, par * S + kt * KT: par * S + (kt + 1) * KT],
                    pTs[:, (ppar * 2 + u) * CH:(ppar * 2 + u + 1) * CH],
                    start=(kt == 0), stop=(kt == nkt - 1), skip_group_check=True), None)
                p.op("tensor", lambda e, u=u, ppar=ppar, qc=qc, ds=den_started, last=(
                    qc == NQC - 1 and jp == nkt // 2 - 1 and u == 1): e.matmul(
                    PS2[0:NQC, 512:512 + CH], ind_sb[:, qc * NQC:(qc + 1) * NQC],
                    pTs[:, (ppar * 2 + u) * CH:(ppar * 2 + u + 1) * CH],
                    start=(not ds), stop=last, skip_group_check=True), tT if u == 1 else None)
                den_started = True
            cons_sig.append(tT.n)
            if jp == nkt // 2 - 1:
                p.wait("vector", tT, cons_sig[-1])
                av_ev.append(p.op("vector", lambda e, qc=qc: e.tensor_copy(
                    y_un[:, qc * CH:(qc + 1) * CH], PS2[:, 0:CH]), vV))

        scs = {0: emit_scores(0)}
        for j in range(NP):
            es, vm = emit_exp_mask(j, scs[j])
            if j + 1 < NP:
                scs[j + 1] = emit_scores(j + 1)
            emit_av(j, es, vm)
        b_freed.append(cons_sig[-1])
        vr = p.op("vector", lambda e: e.reciprocal(den_sb[:, :], PS2[0:NQC, 512:512 + CH]), vV)
        p.wait("sync", vV, vr)
        dsd = p.dma("sync", den_rows[0:1, :], den_sb[:, :], hden)
        yv = None
        for qc in range(NQC):
            p.wait("tensor", hden, dsd)
            p.wait("tensor", sS, exp_sig[-1])
            if yv is not None:
                p.wait("tensor", vV, yv)
            tb_ = p.op("tensor", lambda e, qc=qc: e.matmul(
                PS1[:, 0:CH], ones_fr[0:1, :], den_rows[0:1, qc * CH:(qc + 1) * CH],
                start=True, stop=True, skip_group_check=True), tT)
            p.wait("vector", tT, tb_)
            if yqi >= 2:
                p.wait("vector", hy[yqi % 2], ydma[yqi - 2])
            yv = p.op("vector", lambda e, qc=qc, yqi=yqi: e.tensor_mul(
                ysc[:, (yqi % 2) * CH:(yqi % 2 + 1) * CH],
                y_un[:, qc * CH:(qc + 1) * CH], PS1[:, 0:CH]), vV)
            p.wait("sync", vV, yv)
            tb2 = qc * CH // SPB
            cof = (qc * CH) % SPB
            ydma.append(p.dma("sync",
                              ay_in[(hh * 4 + tb2) * 128:(hh * 4 + tb2 + 1) * 128, cof:cof + CH],
                              ysc[:, (yqi % 2) * CH:(yqi % 2 + 1) * CH], hy[yqi % 2]))
            yqi += 1
        if taps != "B":
            p.wait("gpsimd", hy[0])
            p.wait("gpsimd", hy[1])
            p.op("gpsimd", lambda e, hh=hh: e.collective_compute(
                "AllGather", ALU.bypass, replica_groups=RG,
                ins=[ay_in[hh * 512:(hh + 1) * 512, :]],
                outs=[ay_out[hh * 2048:(hh + 1) * 2048, :]]), gcc)
    if taps == "B":
        finish(p, nc, all_g_dma, all_h_dma, gcc, all_hts)
        return nc
    g_agy = gcc.n

    for n in ("qsb", "ksb", "vsb", "y_un", "pTs", "den_sb", "den_rows", "ysc"):
        sa.free(n)
    sa.free("masks")
    yT = sa.alloc("yT", [128, DCH * TOKC], BF16)
    x1 = sa.alloc("x1", [128, DCH * TOKC], F32)
    wob = sa.alloc("wob", [128, DCH * DM], BF16)
    ay_h = ay_out.tensor
    p.wait("sync", hy[0])
    p.wait("sync", hy[1])
    p.wait("sync", hden)
    h_x1 = p.dma("sync", x1[:, :].rearrange("p (c t) -> p c t", t=TOKC),
                 xT_loc.rearrange("(c p) t -> p c t", p=128), hx)
    p.dma("sync", sg_sb[:, :], s_gate.rearrange("(a p) -> p a", p=128), hx,
          allow_slow_non_contiguous=True)
    h_su = p.dma("sync", su_sb[:, :], s_up.rearrange("(a p) -> p a", p=128), hx,
                 allow_slow_non_contiguous=True)
    h_x1 = h_su

    gwo = p.dma("gpsimd", wob[:, :].rearrange("p (c n) -> p c n", n=DM),
                w_out.rearrange("(c p) n -> p c n", p=128), gw[0])
    wo_ev = []
    wi = 0
    for hh in range(HL):
        p.wait("gpsimd", gcc, g_agy - HL + hh + 1)
        regdma(yT[:, hh * 4 * TOKC:(hh + 1) * 4 * TOKC].rearrange("p (s t) -> p s t", t=TOKC),
               ay_h, 6 * HL + hh, [[SPB, 128], [512 * SPB, 4], [1, SPB]], gys[hh])
        p.wait("tensor", gys[hh], gys[hh].n)
        if hh == 0:
            p.wait("tensor", gw[0], gwo)
        for oct_ in range(DCH):
            for tt in range(NTT):
                ppar = wi % 2
                PSo = PS0[:, ppar * 512: ppar * 512 + NT]
                if wi >= 2:
                    p.wait("tensor", vV, wo_ev[wi - 2])
                for src_ in range(4):
                    c = src_ * HL + hh
                    pos = hh * 4 + src_
                    p.op("tensor", lambda e, c=c, pos=pos, oct_=oct_, tt=tt, PSo=PSo, src_=src_: e.matmul(
                        PSo, wob[:, c * DM + oct_ * 128: c * DM + (oct_ + 1) * 128],
                        yT[:, pos * TOKC + tt * NT: pos * TOKC + (tt + 1) * NT],
                        start=(src_ == 0), stop=(src_ == 3)), tT if src_ == 3 else None)
                tsig = tT.n
                p.wait("vector", tT, tsig)
                p.wait("vector", hx, h_x1)
                wo_ev.append(p.op("vector", lambda e, oct_=oct_, tt=tt, PSo=PSo: e.tensor_add(
                    x1[:, oct_ * TOKC + tt * NT: oct_ * TOKC + (tt + 1) * NT], PSo,
                    x1[:, oct_ * TOKC + tt * NT: oct_ * TOKC + (tt + 1) * NT]), vV))
                wi += 1
    v_x1 = p.nopsig("vector", vV)
    if taps == "C1":
        p.wait("sync", vV, v_x1)
        p.dma("sync", dbg_x1.rearrange("(c p) t -> p c t", p=128),
              x1[:, :].rearrange("p (c t) -> p c t", t=TOKC), hx)
        finish(p, nc, all_g_dma, all_h_dma, gcc, all_hts)
        return nc
    sa.free("yT")
    sa.free("wob")

    xn2 = sa.alloc("xn2", [128, DCH * TOKC], BF16)
    xsq2 = sa.alloc("xsq2", [128, 2 * TOKC], BF16)
    xnc2 = sa.alloc("xnc2", [128, TOKC], F32)
    p.wait("vector", vV, v_x1)
    for c in range(DCH):
        p.op("vector", lambda e, c=c: e.tensor_copy(xn2[:, c * TOKC:(c + 1) * TOKC],
                                                    x1[:, c * TOKC:(c + 1) * TOKC]), None)
    v_xb = p.nopsig("vector", vV)
    xv2 = lambda c: xn2[:, c * TOKC:(c + 1) * TOKC]
    t_st2 = ln_stats(xv2, None, xsq2, wait_vec=v_xb)
    t_bc2 = ln_finalize(t_st2, 1.0 / DM)
    for n in ("mrow", "msqrow", "varrow", "stdrow", "rrow", "mrrow"):
        sa.free(n)
    p.wait("vector", tT, t_bc2)
    for c in range(DCH):
        p.op("vector", lambda e, c=c: e.tensor_mul(xnc2[:, :], xv2(c), PS0[:, 0:TOKC]), None)
        p.op("vector", lambda e, c=c: e.tensor_sub(xv2(c), xnc2[:, :], PS1[:, 0:TOKC]), None)
    v_xn2 = p.nopsig("vector", vV)
    if taps == "C2":
        p.wait("gpsimd", vV, v_xn2)
        p.dma("gpsimd", dbg_xn2.rearrange("(c p) t -> p c t", p=128),
              xn2[:, :].rearrange("p (c t) -> p c t", t=TOKC), sxd)
        finish(p, nc, all_g_dma, all_h_dma, gcc, all_hts)
        return nc

    p.wait("tensor", h_const)
    sa.free("xsq2")
    sa.free("xnc2")
    x1T = sa.alloc("x1T", [128, T8 * DM], BF16)
    tpi = 0
    tp_cp = []
    for t8 in range(T8):
        for c in range(DCH):
            par = tpi % 2
            if tpi >= 2:
                p.wait("tensor", vV, tp_cp[tpi - 2])
            tsg = p.op("tensor", lambda e, t8=t8, c=c, par=par: e.transpose(
                PS2[:, par * 512: par * 512 + 128],
                x1[:, c * TOKC + t8 * 128: c * TOKC + (t8 + 1) * 128], ident_f[:, :]), tT)
            p.wait("vector", tT, tsg)
            tp_cp.append(p.op("vector", lambda e, t8=t8, c=c, par=par: e.tensor_copy(
                x1T[:, t8 * DM + c * 128: t8 * DM + (c + 1) * 128],
                PS2[:, par * 512: par * 512 + 128]), vV))
            tpi += 1
    v_x1T = p.nopsig("vector", vV)
    gw_cons[0] = max(gw_cons[0], tT.n)
    gw_cons[1] = max(gw_cons[1], tT.n)
    sa.free("x1")

    fuse = sa.alloc("fuse", [128, ICT * TB], BF16)
    wgb = sa.alloc("wgb", [128, 2 * DCH * 256], BF16)
    wub = sa.alloc("wub", [128, 2 * DCH * 256], BF16)
    wdb = sa.alloc("wdb", [128, 2 * 4 * (DM // 2)], BF16)
    outb = sa.alloc("outb", [128, 2 * 512], F32)
    sdb = sa.alloc("sdb", [128, DM], F32)
    sd_row = sa.alloc("sd_row", [1, DM], F32)
    h_sd = p.dma("sync", sd_row[0:1, :], s_down[None, :], hsd)
    p.wait("tensor", hsd, h_sd)
    p.wait("tensor", vV, v_x1T)
    for tt in range(DM // 512):
        if tt > 0:
            p.wait("tensor", vV)
        tsd = p.op("tensor", lambda e, tt=tt: e.matmul(PS2[:, 0:512], ones_fr[0:1, :],
                                                       sd_row[0:1, tt * 512:(tt + 1) * 512],
                                                       start=True, stop=True), tT)
        p.wait("vector", tT, tsd)
        p.op("vector", lambda e, tt=tt: e.tensor_copy(sdb[:, tt * 512:(tt + 1) * 512], PS2[:, 0:512]), vV)
    v_sdb = p.nopsig("vector", vV)
    sa.free("sd_row")

    NICG = ICT // 2
    wgv = lambda par: wgb[:, :].rearrange("p (b c n) -> p b c n", b=2, n=256)[:, par]
    wuv = lambda par: wub[:, :].rearrange("p (b c n) -> p b c n", b=2, n=256)[:, par]
    wdv = lambda par: wdb[:, :].rearrange("p (b a n) -> p b a n", b=2, a=4)[:, par]
    odma = []
    oi = 0
    for tb in range(NTB):
        p.wait("tensor", vV)
        gu_ev = []
        gu_sact = []
        gu_mm = []
        for icg in range(NICG):
            par = icg % 2
            p.wait("gpsimd", tT, gw_cons[par])
            p.dma("gpsimd", wgv(par), w_gate[:, icg * 256:(icg + 1) * 256]
                  .rearrange("(c p) n -> p c n", p=128), gw[par])
            gw2 = p.dma("gpsimd", wuv(par), w_up[:, icg * 256:(icg + 1) * 256]
                        .rearrange("(c p) n -> p c n", p=128), gw[par])
            for j in range(2):
                ic = icg * 2 + j
                pgu = ic % 2
                PSgu = PS2 if pgu == 0 else PS0
                PSg = PSgu[:, 0:TB]
                PSu = PSgu[:, 512:512 + TB]
                p.wait("tensor", gw[par], gw2)
                p.wait("tensor", vV, v_xn2)
                p.wait("tensor", vV, v_sdb)
                if ic >= 2:
                    p.wait("tensor", vV, gu_ev[ic - 2])
                    p.wait("tensor", sS, gu_sact[ic - 2])
                for c in range(DCH):
                    p.op("tensor", lambda e, c=c, j=j, par=par, tb=tb, PSg=PSg: e.matmul(
                        PSg, wgv(par)[:, c, j * 128:(j + 1) * 128],
                        xn2[:, c * TOKC + tb * TB: c * TOKC + (tb + 1) * TB],
                        start=(c == 0), stop=(c == DCH - 1)), None)
                for c in range(DCH):
                    p.op("tensor", lambda e, c=c, j=j, par=par, tb=tb, PSu=PSu: e.matmul(
                        PSu, wuv(par)[:, c, j * 128:(j + 1) * 128],
                        xn2[:, c * TOKC + tb * TB: c * TOKC + (tb + 1) * TB],
                        start=(c == 0), stop=(c == DCH - 1)), tT if c == DCH - 1 else None)
                tsg = tT.n
                gu_mm.append(tT.n)
                p.wait("scalar", tT, tsg)
                if ic >= 2:
                    p.wait("scalar", vV, gu_ev[ic - 2])
                p.op("scalar", lambda e, ic=ic, PSg=PSg, pgu=pgu: e.activation(
                    trash_s[:, pgu * TB:(pgu + 1) * TB], PSg, AF.Silu, scale=sg_sb[:, ic:ic + 1]), None)
                gu_sact.append(p.nopsig("scalar", sS))
                p.wait("vector", sS, gu_sact[-1])
                gu_ev.append(p.op("vector", lambda e, ic=ic, PSu=PSu, pgu=pgu: e.scalar_tensor_tensor(
                    fuse[:, ic * TB:(ic + 1) * TB], PSu, su_sb[:, ic:ic + 1], trash_s[:, pgu * TB:(pgu + 1) * TB],
                    ALU.mult, ALU.mult), vV))
                gw_cons[par] = tT.n
        v_fuse = p.nopsig("vector", vV)
        if taps == "C3" and tb == 0:
            p.wait("gpsimd", vV, v_fuse)
            p.dma("gpsimd", dbg_f.rearrange("(c p) t -> p c t", p=128),
                  fuse[:, :].rearrange("p (c t) -> p c t", t=TB), sxd)
            finish(p, nc, all_g_dma, all_h_dma, gcc)
            return nc
        PSDN = (PS0, PS1, PS2, PS3)
        for oph in range(2):
            NOG2 = max(1, DM // 1024)
            dn_mm = []
            p.wait("tensor", vV)
            for icg4 in range(ICT // 4):
                par = icg4 % 2
                p.wait("gpsimd", tT, gw_cons[par])
                gwd = p.dma("gpsimd", wdv(par),
                            w_down[icg4 * 512:(icg4 + 1) * 512, oph * (DM // 2):(oph + 1) * (DM // 2)]
                            .rearrange("(a p) n -> p a n", p=128),
                            gw[par])
                p.wait("tensor", gw[par], gwd)
                p.wait("tensor", vV, v_fuse)
                for j4 in range(4):
                    ic = icg4 * 4 + j4
                    for t1 in range(T128):
                        for og2 in range(NOG2):
                            PSd = PSDN[t1 % 4][:, og2 * 512:(og2 + 1) * 512]
                            p.op("tensor", lambda e, ic=ic, j4=j4, og2=og2, par=par, t1=t1, tb=tb, PSd=PSd: e.matmul(
                                PSd, fuse[:, ic * TB + t1 * 128: ic * TB + (t1 + 1) * 128],
                                wdv(par)[:, j4, og2 * 512:(og2 + 1) * 512],
                                start=(ic == 0), stop=(ic == ICT - 1), skip_group_check=True),
                                tT if (j4 == 3 and t1 == T128 - 1 and og2 == NOG2 - 1) else None)
                dn_mm.append(tT.n)
                gw_cons[par] = tT.n
            for t1 in range(T128):
                trow = tb * TB + t1 * 128
                for og2 in range(NOG2):
                    og = oph * NOG2 + og2
                    PSd = PSDN[t1 % 4][:, og2 * 512:(og2 + 1) * 512]
                    p.wait("vector", tT, dn_mm[-1])
                    obp = oi % 2
                    if oi >= 2:
                        p.wait("vector", ho[oi % 2], odma[oi - 2])
                    p.op("vector", lambda e, og=og, PSd=PSd, obp=obp: e.tensor_mul(
                        outb[:, obp * 512:(obp + 1) * 512], PSd, sdb[:, og * 512:(og + 1) * 512]), None)
                    vo = p.op("vector", lambda e, og=og, trow=trow, obp=obp: e.tensor_add(
                        outb[:, obp * 512:(obp + 1) * 512], outb[:, obp * 512:(obp + 1) * 512],
                        x1T[:, (trow // 128) * DM + og * 512: (trow // 128) * DM + (og + 1) * 512]), vV)
                    p.wait("sync", vV, vo)
                    odma.append(p.dma("sync", out_loc[trow:trow + 128, og * 512:(og + 1) * 512],
                                      outb[:, obp * 512:(obp + 1) * 512], ho[oi % 2]))
                    oi += 1
    finish(p, nc, all_g_dma, all_h_dma, gcc, all_hts)
    return nc


def finish(p, nc, all_g_dma, all_h_dma, gcc, all_hts=()):
    for s_ in all_h_dma:
        p.wait("sync", s_)
    for s_ in all_hts:
        p.wait("scalar", s_)
    for s_ in all_g_dma:
        p.wait("gpsimd", s_)
    p.wait("gpsimd", gcc)
    with nc.Block() as block:
        @block.sync
        def _(e):
            for f in p.q["sync"]:
                f(e)

        @block.scalar
        def _(e):
            for f in p.q["scalar"]:
                f(e)

        @block.tensor
        def _(e):
            for f in p.q["tensor"]:
                f(e)

        @block.vector
        def _(e):
            for f in p.q["vector"]:
                f(e)

        @block.gpsimd
        def _(e):
            for f in p.q["gpsimd"]:
                f(e)


class _Cfg:
    def __init__(self):
        self.B = 2; self.S = 4096; self.H = 16; self.dh = 128
        self.DM = 2048; self.I = 8192; self.CH = 512; self.KT = 128; self.NC = 8
        self.HL = 4; self.TOKC = 1024; self.T8 = 8; self.DCH = 16
        self.NQC = 8; self.ICT = 64; self.NDIAG = 4; self.QCT = 4
        self.stop_after = "C"
        self.TB = 512; self.NTB = 2; self.T128 = 4


def _host_inputs(cfg, x, w_qkv, w_out, q_gain, w_gate, w_up, w_down, s_gate, s_up, s_down):
    H, HL, SPB = cfg.H, cfg.HL, cfg.S // 4
    x = np.ascontiguousarray(np.asarray(x, dtype=np.float32))
    xT = np.ascontiguousarray(np.transpose(x, (0, 2, 1)))
    inv = np.power(10000.0, -np.arange(64, dtype=np.float64) / 64.0)
    th = np.arange(cfg.S, dtype=np.float64)[:, None] * inv[None, :]
    cosT = np.cos(th).astype(np.float32)
    sinT = np.sin(th).astype(np.float32)
    ident = np.eye(128, dtype=np.float32)
    q_gain = np.asarray(q_gain, dtype=np.float32)
    in_maps = []
    for c in range(cfg.NC):
        b, r = c // 4, c % 4
        s0 = r * SPB
        BLK = 1536 * SPB
        offs = []
        for hh in range(HL):
            for rp in range(4):
                offs.append(hh * 4 * BLK + rp * BLK + 1024 * SPB + r * SPB * 128)
        for hh in range(HL):
            offs.append(hh * 4 * BLK + r * 128 * SPB)
        for hh in range(HL):
            offs.append(hh * 4 * BLK + 512 * SPB + r * 128 * SPB)
        for hh in range(HL):
            offs.append(hh * 2048 * SPB + r * 128 * SPB)
        in_maps.append({
            "xT_loc": np.ascontiguousarray(xT[b, :, s0:s0 + SPB]),
            "w_qkv": np.ascontiguousarray(np.asarray(w_qkv, dtype=np.float32)),
            "w_out": np.ascontiguousarray(np.asarray(w_out, dtype=np.float32)),
            "q_gain": q_gain,
            "gain_loc": np.ascontiguousarray(q_gain[r * HL:(r + 1) * HL]),
            "w_gate": np.ascontiguousarray(np.asarray(w_gate, dtype=np.int8)),
            "w_up": np.ascontiguousarray(np.asarray(w_up, dtype=np.int8)),
            "w_down": np.ascontiguousarray(np.asarray(w_down, dtype=np.int8)),
            "s_gate": np.ascontiguousarray(np.asarray(s_gate, dtype=np.float32)),
            "s_up": np.ascontiguousarray(np.asarray(s_up, dtype=np.float32)),
            "s_down": np.ascontiguousarray(np.asarray(s_down, dtype=np.float32)),
            "cosL": np.ascontiguousarray(cosT[s0:s0 + SPB]),
            "sinL": np.ascontiguousarray(sinT[s0:s0 + SPB]),
            "ident": ident,
            "offtab": np.asarray(offs, dtype=np.int32),
        })
    return in_maps


def kernel(L0_attn_x, L0_attn_w_qkv, L0_attn_w_out, L0_attn_q_gain,
           L0_ffn_w_gate, L0_ffn_w_up, L0_ffn_w_down,
           L0_ffn_s_gate, L0_ffn_s_up, L0_ffn_s_down):
    cfg = _Cfg()
    in_maps = _host_inputs(cfg, L0_attn_x, L0_attn_w_qkv, L0_attn_w_out, L0_attn_q_gain,
                           L0_ffn_w_gate, L0_ffn_w_up, L0_ffn_w_down,
                           L0_ffn_s_gate, L0_ffn_s_up, L0_ffn_s_down)
    nc = build(cfg)
    res = run_bass_kernel_spmd(nc, in_maps, core_ids=list(range(8)))
    SPB = cfg.S // 4
    out = np.empty((cfg.B, cfg.S, cfg.DM), dtype=np.float32)
    for c in range(cfg.NC):
        b, r = c // 4, c % 4
        out[b, r * SPB:(r + 1) * SPB, :] = np.asarray(res.results[c]["out_loc"])
    return out


# version 15
# speedup vs baseline: 1.0621x; 1.0001x over previous
import numpy as np
import concourse.bass as bass
import concourse.mybir as mybir
from concourse.bass_utils import run_bass_kernel_spmd

F32 = mybir.dt.float32
BF16 = mybir.dt.bfloat16
I8 = mybir.dt.int8
I32 = mybir.dt.int32
AF = mybir.ActivationFunctionType
ALU = mybir.AluOpType

ENGS = ("sync", "scalar", "tensor", "vector", "gpsimd")


class Sem:
    def __init__(self, nc, name):
        self.h = nc.alloc_semaphore(name)
        self.n = 0

    def bump(self, k=1):
        self.n += k
        return self.n


class SAlloc:
    def __init__(self, nc, base, limit):
        self.nc, self.base, self.limit = nc, base, limit
        self.used = []

    def alloc(self, name, shape, dtype, align=64):
        per = int(np.prod(shape[1:])) * mybir.dt.size(dtype)
        per = (per + align - 1) // align * align
        off = self.base
        for (o, e, _) in sorted(self.used):
            if off + per <= o:
                break
            off = max(off, e)
        assert off + per <= self.limit, f"SBUF overflow: {name} {per}B at {off}"
        self.used.append((off, off + per, name))
        return self.nc.alloc_sbuf_tensor_at(name, list(shape), dtype, offset=off)

    def free(self, name):
        self.used = [u for u in self.used if u[2] != name]


class Plan:

    def __init__(self, nc):
        self.nc = nc
        self.q = {e: [] for e in ENGS}
        self.self_sems = {}

    def em(self, eng, fn):
        self.q[eng].append(fn)

    def wait(self, eng, sem, val=None):
        v = sem.n if val is None else val
        if v > 0:
            self.em(eng, lambda e, s=sem.h, vv=v: e.wait_ge(s, vv))

    def op(self, eng, fn, sem=None, k=1):
        ss = self.self_sems.get(eng)
        if ss is not None:
            assert sem is None or sem is ss
            self.wait(eng, ss)
            v = ss.bump(1)
            self.em(eng, lambda e, f=fn, s=ss.h: f(e).then_inc(s, 1))
            return v
        if sem is None:
            self.em(eng, fn)
            return None
        v = sem.bump(k)
        self.em(eng, lambda e, f=fn, s=sem.h, kk=k: f(e).then_inc(s, kk))
        return v

    def nopsig(self, eng, sem, k=1):
        ss = self.self_sems.get(eng)
        if ss is not None:
            assert sem is ss
            return ss.n
        v = sem.bump(k)
        self.em(eng, lambda e, s=sem.h, kk=k: e.nop().then_inc(s, kk))
        return v

    def dma(self, eng, dst, src, sem, **kw):
        v = sem.bump(16)
        self.em(eng, lambda e, d=dst, s=src, ss=sem.h, kw2=dict(kw): e.dma_start(d, s, **kw2).then_inc(ss, 16))
        return v


def build(cfg):
    nc = bass.Bass("TRN2", target_bir_lowering=False)
    B, S, DM, H, I = cfg.B, cfg.S, cfg.DM, cfg.H, cfg.I
    HL, TOKC, T8, DCH, NQC, CH, KT = cfg.HL, cfg.TOKC, cfg.T8, cfg.DCH, cfg.NQC, cfg.CH, cfg.KT
    SPB = S // 4
    NDIAG = cfg.NDIAG
    QCT = DM // 512
    NT = min(512, TOKC)
    NTT = TOKC // NT
    ICT = cfg.ICT
    TB, NTB, T128 = cfg.TB, cfg.NTB, cfg.T128
    taps = cfg.stop_after

    di = lambda n, sh, dt: nc.dram_tensor(n, sh, dt, kind="ExternalInput").ap()
    do = lambda n, sh, dt: nc.dram_tensor(n, sh, dt, kind="ExternalOutput").ap()
    dint = lambda n, sh, dt, out=False: (do(n, sh, dt) if out else nc.dram_tensor(n, sh, dt, kind="Internal").ap())

    xT_loc = di("xT_loc", [DM, TOKC], F32)
    w_qkv = di("w_qkv", [DM, 3 * DM], F32)
    w_out = di("w_out", [DM, DM], F32)
    q_gain = di("q_gain", [H], F32)
    gain_loc = di("gain_loc", [HL], F32)
    w_gate = di("w_gate", [DM, I], I8)
    w_up = di("w_up", [DM, I], I8)
    w_down = di("w_down", [I, DM], I8)
    s_gate = di("s_gate", [I], F32)
    s_up = di("s_up", [I], F32)
    s_down = di("s_down", [DM], F32)
    cosL = di("cosL", [TOKC, 64], F32)
    sinL = di("sinL", [TOKC, 64], F32)
    ident_in = di("ident", [128, 128], F32)
    offtab = di("offtab", [7 * HL], I32)

    BLK = 1536 * TOKC
    KOFF = 512 * TOKC
    VOFF = 1024 * TOKC
    agx_in = dint("agx_in", [HL * BLK], BF16, out=(taps == "A"))
    if taps != "A":
        agx_out = dint("agx_out", [4 * HL * BLK], BF16)
        ay_in = dint("ay_in", [4 * HL * 128, SPB], BF16, out=(taps == "B"))
    if taps in ("C", "C1", "C2", "C3"):
        ay_out = dint("ay_out", [16 * HL * 128, SPB], BF16)
        out_loc = do("out_loc", [TOKC, DM], F32)
    if taps == "C1":
        dbg_x1 = do("dbg_x1", [DM, TOKC], F32)
    if taps == "C2":
        dbg_xn2 = do("dbg_xn2", [DM, TOKC], F32)
    if taps == "C3":
        dbg_f = do("dbg_f", [I, TB], F32)

    PS0 = nc.alloc_psum_tensor("PS0", [128, 1024], F32)
    PS1 = nc.alloc_psum_tensor("PS1", [128, 1024], F32)
    PS2 = nc.alloc_psum_tensor("PS2", [128, 1024], F32)
    _pt_guard = nc.psum_tensor("PT", [128, 2048], BF16)
    PT = _pt_guard.__enter__()
    PS3 = None

    sa = SAlloc(nc, (nc.sbuf_base + 63) // 64 * 64, nc.sbuf_top)
    ident_bf = sa.alloc("ident_bf", [128, 128], BF16)
    ident_f = sa.alloc("ident_f", [128, 128], F32)
    masks = sa.alloc("masks", [128, NDIAG * CH], BF16)
    cos_sb = sa.alloc("cos_sb", [128, T8 * 64], F32)
    sin_sb = sa.alloc("sin_sb", [128, T8 * 64], F32)
    ones_bf = sa.alloc("ones_bf", [128, 8], BF16)
    ones_fr = sa.alloc("ones_fr", [1, 128], F32)
    gain_sb = sa.alloc("gain_sb", [1, H], F32)
    gloc_sb = sa.alloc("gloc_sb", [1, HL], F32)
    negrow = sa.alloc("negrow", [1, HL], F32)
    gb = sa.alloc("gb", [128, H], F32)
    negabsg = sa.alloc("negabsg", [128, HL], F32)
    ind_sb = sa.alloc("ind_sb", [128, NQC * NQC], BF16)
    offtab_sb = sa.alloc("offtab_sb", [1, 7 * HL], I32)
    trash_s = sa.alloc("trash_s", [128, 1024], BF16)
    eps_un = sa.alloc("eps_un", [128, 1], F32)
    eps_ln = sa.alloc("eps_ln", [1, 1], F32)
    sg_sb = sa.alloc("sg_sb", [128, ICT], F32)
    su_sb = sa.alloc("su_sb", [128, ICT], F32)

    p = Plan(nc)
    gcc = Sem(nc, "gcc")
    gA = Sem(nc, "gA")
    tT = Sem(nc, "tT")
    vV = Sem(nc, "vV")
    sS = Sem(nc, "sS")
    g_const = Sem(nc, "g_const")
    sxd = Sem(nc, "sxd")
    gw = [Sem(nc, "gw0"), Sem(nc, "gw1")]
    gq = [Sem(nc, "gq0"), Sem(nc, "gq1")]
    gys = [Sem(nc, "gy%d" % i) for i in range(4)]
    h_const = Sem(nc, "h_const")
    hv = [Sem(nc, "hv0"), Sem(nc, "hv1")]
    ht = [Sem(nc, "ht0"), Sem(nc, "ht1")]
    hden = Sem(nc, "hden")
    hy = [Sem(nc, "hy0"), Sem(nc, "hy1")]
    hq = [Sem(nc, "hq0"), Sem(nc, "hq1")]
    hsd = Sem(nc, "hsd")
    hts = [Sem(nc, "hts0"), Sem(nc, "hts1")]
    hx = Sem(nc, "hx")
    ho = [Sem(nc, "ho0"), Sem(nc, "ho1")]
    all_g_dma = [g_const, sxd, gw[0], gw[1], gq[0], gq[1]] + gys
    p_selfsems_marker = True
    all_h_dma = [h_const, hv[0], hv[1], ht[0], ht[1], hden, hy[0], hy[1], hx, ho[0], ho[1], hq[0], hq[1], hsd]
    all_hts = hts
    RPOOL = nc.alloc_register(mybir.EngineType.Pool, "roff")
    RSYNC = nc.alloc_register(mybir.EngineType.SP, "roffs")
    p.self_sems = {"vector": vV, "scalar": sS}
    gw_cons = {0: 0, 1: 0}

    p.dma("gpsimd", ident_bf[:, :], ident_in, g_const)
    p.dma("sync", ident_f[:, :], ident_in, h_const)
    p.dma("sync", cos_sb[:, :].rearrange("p (a j) -> p a j", j=64),
          cosL.rearrange("(a p) j -> p a j", p=128), h_const)
    p.dma("sync", sin_sb[:, :].rearrange("p (a j) -> p a j", j=64),
          sinL.rearrange("(a p) j -> p a j", p=128), h_const)
    p.dma("sync", gain_sb[0:1, :], q_gain[None, :], h_const)
    p.dma("sync", gloc_sb[0:1, :], gain_loc[None, :], h_const)
    h_off = p.dma("sync", offtab_sb[0:1, :], offtab[None, :], h_const)
    h_gain = h_const.n
    p.op("vector", lambda e: e.memset(ones_bf[:, :], 1.0), vV)
    p.op("vector", lambda e: e.memset(ones_fr[:, :], 1.0), vV)
    vz = p.op("vector", lambda e: e.memset(ind_sb[:, :], 0.0), vV)
    p.wait("vector", vV, vz)
    for qc in range(NQC):
        p.op("vector", lambda e, qc=qc: e.memset(ind_sb[:, qc * NQC + qc: qc * NQC + qc + 1], 1.0), vV)
    p.op("vector", lambda e: e.memset(masks[:, :], 1.0), vV)
    p.op("vector", lambda e: e.memset(eps_un[:, :], 1e-6), vV)
    p.op("vector", lambda e: e.memset(eps_ln[:, :], 1e-5), vV)
    v_consts = vV.n
    p.wait("gpsimd", vV, v_consts)
    for i in range(NDIAG):
        p.op("gpsimd", lambda e, i=i: e.affine_select(
            masks[:, i * CH:(i + 1) * CH], masks[:, i * CH:(i + 1) * CH],
            [[1, CH]], ALU.is_ge, 0.0, base=-i * 128, channel_multiplier=-1), gA)
    p.wait("scalar", h_const, h_gain)
    s_abs = p.op("scalar", lambda e: e.activation(negrow[0:1, :], gloc_sb[0:1, :], AF.Abs), sS)
    p.wait("vector", sS, s_abs)
    v_neg = p.op("vector", lambda e: e.tensor_scalar_mul(negrow[0:1, :], negrow[0:1, :], -1.0), vV)
    p.wait("tensor", h_const, h_gain)
    p.wait("tensor", vV, v_neg)
    p.op("tensor", lambda e: e.matmul(PS0[:, 0:H], ones_fr[0:1, :], gain_sb[0:1, :], start=True, stop=True), None)
    t_bc = p.op("tensor", lambda e: e.matmul(PS0[:, H:H + HL], ones_fr[0:1, :], negrow[0:1, :],
                                             start=True, stop=True), tT)
    p.wait("vector", tT, t_bc)
    p.op("vector", lambda e: e.tensor_copy(gb[:, :], PS0[:, 0:H]), vV)
    v_gainbc = p.op("vector", lambda e: e.tensor_copy(negabsg[:, :], PS0[:, H:H + HL]), vV)

    mrow = sa.alloc("mrow", [1, TOKC], F32)
    msqrow = sa.alloc("msqrow", [1, TOKC], F32)
    varrow = sa.alloc("varrow", [1, TOKC], F32)
    stdrow = sa.alloc("stdrow", [1, TOKC], F32)
    rrow = sa.alloc("rrow", [1, TOKC], F32)
    mrrow = sa.alloc("mrrow", [1, TOKC], F32)
    xsb = sa.alloc("xsb", [128, DCH * TOKC], BF16)
    xnc = sa.alloc("xnc", [128, TOKC], F32)
    xsq = sa.alloc("xsq", [128, 2 * TOKC], BF16)
    wqb = sa.alloc("wqb", [128, 2 * DCH * 512], BF16)
    qstage = sa.alloc("qstage", [128, H * T8 * 128], BF16)
    kstage = sa.alloc("kstage", [128, H * T8 * 128], BF16)
    vtmp = sa.alloc("vtmp", [128, 2 * 512], BF16)
    tstage = sa.alloc("tstage", [128, 2 * 512], BF16)
    rtmp = sa.alloc("rtmp", [128, 4 * T8 * 64], BF16)
    ssq4 = sa.alloc("ssq4", [128, 8], F32)
    std4 = sa.alloc("std4", [128, 8], F32)
    rq4 = sa.alloc("rq4", [128, 8], F32)

    xv = lambda c: xsb[:, c * TOKC:(c + 1) * TOKC]
    g_x = p.dma("gpsimd", xsb[:, :].rearrange("p (c t) -> p c t", t=TOKC),
                xT_loc.rearrange("(c p) t -> p c t", p=128), sxd)

    def ln_stats(src_of, sq_of, sqbuf, wait_vec=None):
        mmv = []
        sqv = []
        for c in range(DCH):
            if wait_vec is not None:
                p.wait("scalar", vV, wait_vec)
            if c >= 2:
                p.wait("scalar", tT, mmv[c - 2])
            sqv.append(p.op("scalar", lambda e, c=c: e.activation(
                sqbuf[:, (c % 2) * TOKC:(c % 2 + 1) * TOKC], src_of(c), AF.Square), sS))
            if wait_vec is not None:
                p.wait("tensor", vV, wait_vec)
            p.wait("tensor", sS, sqv[c])
            for tt in range(NTT):
                p.op("tensor", lambda e, c=c, tt=tt: e.matmul(
                    PS0[0:1, tt * NT:(tt + 1) * NT], ones_bf[:, 0:1],
                    src_of(c)[:, tt * NT:(tt + 1) * NT],
                    start=(c == 0), stop=(c == DCH - 1)), None)
                p.op("tensor", lambda e, c=c, tt=tt: e.matmul(
                    PS1[0:1, tt * NT:(tt + 1) * NT], ones_bf[:, 0:1],
                    sqbuf[:, (c % 2) * TOKC + tt * NT:(c % 2) * TOKC + (tt + 1) * NT],
                    start=(c == 0), stop=(c == DCH - 1)), tT if tt == NTT - 1 else None)
            mmv.append(tT.n)
        return tT.n

    def ln_finalize(t_wait, inv_n):
        p.wait("vector", tT, t_wait)
        p.op("vector", lambda e: e.tensor_scalar_mul(mrow[0:1, :], PS0[0:1, 0:TOKC], inv_n), None)
        p.op("vector", lambda e: e.tensor_scalar_mul(msqrow[0:1, :], PS1[0:1, 0:TOKC], inv_n), None)
        p.op("vector", lambda e: e.tensor_mul(varrow[0:1, :], mrow[0:1, :], mrow[0:1, :]), None)
        vsub = p.op("vector", lambda e: e.tensor_sub(varrow[0:1, :], msqrow[0:1, :], varrow[0:1, :]), vV)
        p.wait("scalar", vV, vsub)
        ss_ = p.op("scalar", lambda e: e.activation(stdrow[0:1, :], varrow[0:1, :], AF.Sqrt, bias=eps_ln[0:1, :]), sS)
        p.wait("vector", sS, ss_)
        p.op("vector", lambda e: e.reciprocal(rrow[0:1, :], stdrow[0:1, :]), None)
        vr = p.op("vector", lambda e: e.tensor_mul(mrrow[0:1, :], mrow[0:1, :], rrow[0:1, :]), vV)
        p.wait("tensor", vV, vr)
        for tt in range(NTT):
            p.op("tensor", lambda e, tt=tt: e.matmul(PS0[:, tt * NT:(tt + 1) * NT], ones_fr[0:1, :],
                                                     rrow[0:1, tt * NT:(tt + 1) * NT], start=True, stop=True), None)
            p.op("tensor", lambda e, tt=tt: e.matmul(PS1[:, tt * NT:(tt + 1) * NT], ones_fr[0:1, :],
                                                     mrrow[0:1, tt * NT:(tt + 1) * NT], start=True, stop=True),
                 tT if tt == NTT - 1 else None)
        return tT.n

    p.wait("scalar", sxd, g_x)
    p.wait("tensor", sxd, g_x)
    p.wait("tensor", vV, v_gainbc)
    t_st1 = ln_stats(xv, None, xsq)
    t_bc1 = ln_finalize(t_st1, 1.0 / DM)
    p.wait("vector", tT, t_bc1)
    for c in range(DCH):
        p.op("vector", lambda e, c=c: e.tensor_mul(xnc[:, :], xv(c), PS0[:, 0:TOKC]), None)
        p.op("vector", lambda e, c=c: e.tensor_sub(xv(c), xnc[:, :], PS1[:, 0:TOKC]), None)
    v_xn = p.nopsig("vector", vV)

    NCT = 3 * DM // 512
    wv = lambda par: wqb[:, :].rearrange("p (b c n) -> p b c n", b=2, n=512)[:, par]
    cosv = cos_sb[:, :].rearrange("p (a j) -> p a j", j=64)
    sinv = sin_sb[:, :].rearrange("p (a j) -> p a j", j=64)
    rt = lambda i: rtmp[:, :].rearrange("p (i a j) -> p i a j", i=4, j=64)[:, i]
    RG = [[0, 1, 2, 3], [4, 5, 6, 7]]
    g_ag0 = gcc.n
    evac_done = []
    psq_sig = []
    vdma = []
    vtile_done = []
    tcp = []
    tdma = []
    gidx = 0
    tgrp = 0
    ct_order = list(range(2 * QCT, 3 * QCT)) + [x for g in range(QCT) for x in (g, QCT + g)]
    for ci, ct in enumerate(ct_order):
        par_w = ci % 2
        p.wait("gpsimd", tT, gw_cons[par_w])
        g_w = p.dma("gpsimd", wv(par_w),
                    w_qkv[:, ct * 512:(ct + 1) * 512].rearrange("(c p) n -> p c n", p=128), gw[par_w])
        is_q = ct < QCT
        is_k = QCT <= ct < 2 * QCT
        h0 = 4 * (ct % QCT)
        for t8 in range(T8):
            gi = ci * T8 + t8
            par = gi % 2
            PSq = PS2[:, par * 512:(par + 1) * 512]
            p.wait("tensor", gw[par_w], g_w)
            p.wait("tensor", vV, v_xn)
            if gi >= 2:
                p.wait("tensor", vV, evac_done[gi - 2])
            for c in range(DCH):
                p.op("tensor", lambda e, c=c, t8=t8, ct=ct, par_w=par_w, PSq=PSq: e.matmul(
                    PSq, xsb[:, c * TOKC + t8 * 128: c * TOKC + (t8 + 1) * 128],
                    wv(par_w)[:, c, :], start=(c == 0), stop=(c == DCH - 1)),
                    tT if c == DCH - 1 else None)
            sig = tT.n
            psq_sig.append(sig)
            gw_cons[par_w] = sig
            if is_q or is_k:
                stg = qstage if is_q else kstage
                par2 = gi % 2
                p.wait("scalar", tT, sig)
                for j in range(4):
                    p.op("scalar", lambda e, j=j, PSq=PSq, par2=par2: e.activation(
                        trash_s[:, j * 128:(j + 1) * 128], PSq[:, j * 128:(j + 1) * 128], AF.Square,
                        accum_out=ssq4[:, par2 * 4 + j: par2 * 4 + j + 1]), None)
                ssig = p.op("scalar", lambda e, par2=par2: e.activation(
                    std4[:, par2 * 4:(par2 + 1) * 4], ssq4[:, par2 * 4:(par2 + 1) * 4],
                    AF.Sqrt, bias=eps_un[:, :]), sS)
                p.wait("vector", sS, ssig)
                p.op("vector", lambda e, par2=par2: e.reciprocal(
                    rq4[:, par2 * 4:(par2 + 1) * 4], std4[:, par2 * 4:(par2 + 1) * 4]), None)
                if is_q:
                    p.op("vector", lambda e, par2=par2, h0=h0: e.tensor_mul(
                        rq4[:, par2 * 4:(par2 + 1) * 4], rq4[:, par2 * 4:(par2 + 1) * 4],
                        gb[:, h0:h0 + 4]), None)
                for j in range(4):
                    p.op("vector", lambda e, j=j, h0=h0, t8=t8, par2=par2, PSq=PSq, stg=stg: e.tensor_scalar_mul(
                        stg[:, ((h0 + j) * T8 + t8) * 128:((h0 + j) * T8 + t8 + 1) * 128],
                        PSq[:, j * 128:(j + 1) * 128],
                        rq4[:, par2 * 4 + j: par2 * 4 + j + 1]), None)
                evac_done.append(p.nopsig("vector", vV))
            else:
                par2 = gi % 2
                p.wait("vector", tT, sig)
                if gidx >= 2:
                    p.wait("vector", hv[par2], vtile_done[gidx - 2])
                cp = p.op("vector", lambda e, par2=par2, PSq=PSq: e.tensor_copy(
                    vtmp[:, par2 * 512:(par2 + 1) * 512], PSq), vV)
                evac_done.append(cp)
                p.wait("sync", vV, cp)
                nsub = 4 // HL if HL < 4 else 1
                hps = 4 // nsub
                for s_i in range(nsub):
                    hq0 = h0 + s_i * hps
                    g_grp = hq0 // HL
                    off = VOFF + g_grp * TOKC * 128 + t8 * 128 * 128
                    vdma.append(p.dma("sync",
                                      bass.AP(agx_in.tensor, off,
                                              [[128, 128], [BLK, hps], [1, 128]]),
                                      vtmp[:, par2 * 512 + s_i * hps * 128: par2 * 512 + (s_i + 1) * hps * 128]
                                      .rearrange("p (h d) -> p h d", d=128),
                                      hv[par2]))
                vtile_done.append(hv[par2].n)
                gidx += 1
        if is_k:
            g = ct - QCT
            for stg in (qstage, kstage):
                sv = stg[:, :].rearrange("p (h a d) -> p h a d", h=H, d=128)
                for h in range(h0, h0 + 4):
                    if h % 2 == 0:
                        lo = sv[:, h, :, 0:64]
                        hi = sv[:, h, :, 64:128]
                        p.op("vector", lambda e, lo=lo: e.tensor_mul(rt(0), lo, cosv), None)
                        p.op("vector", lambda e, hi=hi: e.tensor_mul(rt(1), hi, sinv), None)
                        p.op("vector", lambda e, lo=lo: e.tensor_mul(rt(2), lo, sinv), None)
                        p.op("vector", lambda e, hi=hi: e.tensor_mul(rt(3), hi, cosv), None)
                        p.op("vector", lambda e, lo=lo: e.tensor_sub(lo, rt(0), rt(1)), None)
                        p.op("vector", lambda e, hi=hi: e.tensor_add(hi, rt(2), rt(3)), None)
            v_hset = p.nopsig("vector", vV)
            p.wait("tensor", vV, v_hset)
            p.wait("tensor", g_const)
            for stg, koff in ((qstage, 0), (kstage, KOFF)):
                for h in range(h0, h0 + 4):
                    ghead = h // HL
                    hh_l = h % HL
                    for grp in range(max(1, T8 // 4)):
                        n4 = min(4, T8 - grp * 4)
                        par_t = tgrp % 2
                        if tgrp >= 2:
                            p.wait("tensor", vV, tcp[tgrp - 2])
                        for j in range(n4):
                            t8 = grp * 4 + j
                            p.op("tensor", lambda e, h=h, t8=t8, par_t=par_t, j=j, stg=stg: e.transpose(
                                PT[:, par_t * 1024 + j * 128: par_t * 1024 + (j + 1) * 128],
                                stg[:, (h * T8 + t8) * 128:(h * T8 + t8 + 1) * 128],
                                ident_bf[:, :]), tT if j == n4 - 1 else None)
                        ts = tT.n
                        p.wait("vector", tT, ts)
                        if tgrp >= 2:
                            p.wait("vector", ht[tgrp % 2], tdma[tgrp - 2])
                        c2 = p.op("vector", lambda e, par_t=par_t, n4=n4: e.tensor_copy(
                            tstage[:, par_t * 512:par_t * 512 + n4 * 128],
                            PT[:, par_t * 1024:par_t * 1024 + n4 * 128]), vV)
                        tcp.append(c2)
                        p.wait("sync", vV, c2)
                        tdma.append(p.dma("sync",
                                          bass.AP(agx_in.tensor,
                                                  hh_l * BLK + koff + ghead * 128 * TOKC + grp * 512,
                                                  [[TOKC, 128], [1, n4 * 128]]),
                                          tstage[:, par_t * 512:par_t * 512 + n4 * 128], ht[par_t]))
                        tgrp += 1
    sa.free("cos_sb")
    sa.free("sin_sb")
    if taps != "A":
        p.wait("gpsimd", ht[0])
        p.wait("gpsimd", ht[1])
        p.wait("gpsimd", hv[0])
        p.wait("gpsimd", hv[1])
        for hh in range(HL):
            p.op("gpsimd", lambda e, hh=hh: e.collective_compute(
                "AllGather", ALU.bypass, replica_groups=RG,
                ins=[agx_in[hh * BLK:(hh + 1) * BLK]],
                outs=[agx_out[hh * 4 * BLK:(hh + 1) * 4 * BLK]]), gcc)
    g_agqkv = gcc.n
    for n in ("xsb", "xnc", "xsq", "wqb", "qstage", "kstage", "vtmp", "tstage", "rtmp", "ssq4", "std4", "rq4"):
        sa.free(n)

    _pt_guard.__exit__(None, None, None)
    PS3 = nc.alloc_psum_tensor("PS3", [128, 1024], F32)
    if taps == "A":
        finish(p, nc, all_g_dma, all_h_dma, gcc, all_hts)
        return nc

    qsb = sa.alloc("qsb", [128, 2 * S], BF16)
    ksb = sa.alloc("ksb", [128, 2 * S], BF16)
    vsb = sa.alloc("vsb", [128, 2 * S], BF16)
    y_un = sa.alloc("y_un", [128, S], BF16)
    pTs = sa.alloc("pTs", [128, 6 * CH], BF16)
    den_sb = sa.alloc("den_sb", [NQC, CH], F32)
    den_rows = sa.alloc("den_rows", [1, NQC * CH], F32)
    ysc = sa.alloc("ysc", [128, 2 * CH], BF16)

    def regdma(dst_ap, src_handle, tabidx, pattern, sem, eng="gpsimd"):
        v = sem.bump(16)
        reg = RPOOL if eng == "gpsimd" else RSYNC

        def f(e, dst=dst_ap, th=src_handle, ti=tabidx, pat=pattern, s=sem.h, reg=reg):
            e.reg_load(reg, offtab_sb[0:1, ti:ti + 1])
            e.dma_start(dst, bass.AP(th, reg, pat)).then_inc(s, 16)
        p.em(eng, f)
        return v

    agx_h = agx_out.tensor
    b_ld = []
    b_ldv = []
    b_freed = []
    prev_exp_last = 0
    prev_cons_last = 0
    prev_yv_last = 0
    ydma = []
    yqi = 0
    g_ag0 = g_agqkv - HL
    p.wait("gpsimd", h_const, h_off)
    for hh in range(HL):
        par = hh % 2
        p.wait("gpsimd", gcc, g_ag0 + hh + 1)
        if hh >= 2:
            p.wait("gpsimd", tT, b_freed[hh - 2])
        regdma(qsb[:, par * S:(par + 1) * S].rearrange("p (a t) -> p a t", t=SPB),
               agx_h, 4 * HL + hh, [[SPB, 128], [BLK, 4], [1, SPB]], gq[par])
        regdma(ksb[:, par * S:(par + 1) * S].rearrange("p (a t) -> p a t", t=SPB),
               agx_h, 5 * HL + hh, [[SPB, 128], [BLK, 4], [1, SPB]], gq[par])
        p.wait("sync", gcc, g_ag0 + hh + 1)
        if hh == 0:
            p.wait("sync", h_const, h_off)
        if hh >= 2:
            p.wait("sync", tT, b_freed[hh - 2])
        for rp in range(4):
            regdma(vsb[:, par * S + rp * SPB: par * S + (rp + 1) * SPB].rearrange("p (b d) -> p b d", d=128),
                   agx_h, hh * 4 + rp, [[128, 128], [128 * 128, SPB // 128], [1, 128]], hq[par], eng="sync")
        b_ld.append(gq[par].n)
        b_ldv.append(hq[par].n)
        exp_sig = []
        cons_sig = []
        av_ev = []
        den_started = False
        pairs = []
        for qc in range(NQC):
            nkt = (qc + 1) * CH // KT
            for jp in range(nkt // 2):
                pairs.append((qc, jp, nkt))
        NP = len(pairs)

        def emit_scores(j):
            qc, jp, nkt = pairs[j]
            ppar = j % 3
            PSsc = (PS0, PS1, PS3)[ppar]
            p.wait("tensor", gq[par], b_ld[hh])
            p.wait("tensor", hq[par], b_ldv[hh])
            if j >= 3:
                p.wait("tensor", sS, exp_sig[j - 3])
            elif prev_exp_last:
                p.wait("tensor", sS, prev_exp_last)
            for u in range(2):
                kt = jp * 2 + u
                p.op("tensor", lambda e, kt=kt, u=u, PSsc=PSsc, qc=qc, par=par: e.matmul(
                    PSsc[:, u * CH:(u + 1) * CH],
                    ksb[:, par * S + kt * KT: par * S + (kt + 1) * KT],
                    qsb[:, par * S + qc * CH: par * S + (qc + 1) * CH],
                    start=True, stop=True, skip_group_check=True), tT if u == 1 else None)
            return tT.n

        def emit_exp_mask(j, scs):
            qc, jp, nkt = pairs[j]
            ppar = j % 3
            PSsc = (PS0, PS1, PS3)[ppar]
            p.wait("scalar", tT, scs)
            if j >= 3:
                p.wait("scalar", tT, cons_sig[j - 3])
            elif prev_cons_last:
                p.wait("scalar", tT, prev_cons_last)
            es = p.op("scalar", lambda e, ppar=ppar, PSsc=PSsc, hh=hh: e.activation(
                pTs[:, ppar * 2 * CH:(ppar + 1) * 2 * CH], PSsc[:, 0:2 * CH], AF.Exp,
                bias=negabsg[:, hh:hh + 1]), sS)
            exp_sig.append(es)
            base_kt = jp * 2
            need_mask = [u for u in range(2) if (base_kt + u) * KT >= qc * CH]
            vm = None
            if need_mask:
                p.wait("vector", sS, es)
                for u in need_mask:
                    kt = base_kt + u
                    i = (kt * KT - qc * CH) // 128
                    vm = p.op("vector", lambda e, u=u, ppar=ppar, i=i: e.tensor_mul(
                        pTs[:, (ppar * 2 + u) * CH:(ppar * 2 + u + 1) * CH],
                        pTs[:, (ppar * 2 + u) * CH:(ppar * 2 + u + 1) * CH],
                        masks[:, i * CH:(i + 1) * CH]), vV)
            return es, vm

        def emit_av(j, es, vm):
            nonlocal den_started
            qc, jp, nkt = pairs[j]
            ppar = j % 3
            if vm is not None:
                p.wait("tensor", vV, vm)
            else:
                p.wait("tensor", sS, es)
            if jp == 0 and qc > 0:
                p.wait("tensor", vV, av_ev[qc - 1])
            if jp == 0 and qc == 0 and prev_yv_last:
                p.wait("tensor", vV, prev_yv_last)
            for u in range(2):
                kt = jp * 2 + u
                p.op("tensor", lambda e, kt=kt, u=u, ppar=ppar, nkt=nkt, par=par: e.matmul(
                    PS2[:, 0:CH], vsb[:, par * S + kt * KT: par * S + (kt + 1) * KT],
                    pTs[:, (ppar * 2 + u) * CH:(ppar * 2 + u + 1) * CH],
                    start=(kt == 0), stop=(kt == nkt - 1), skip_group_check=True), None)
                p.op("tensor", lambda e, u=u, ppar=ppar, qc=qc, ds=den_started, last=(
                    qc == NQC - 1 and jp == nkt // 2 - 1 and u == 1): e.matmul(
                    PS2[0:NQC, 512:512 + CH], ind_sb[:, qc * NQC:(qc + 1) * NQC],
                    pTs[:, (ppar * 2 + u) * CH:(ppar * 2 + u + 1) * CH],
                    start=(not ds), stop=last, skip_group_check=True), tT if u == 1 else None)
                den_started = True
            cons_sig.append(tT.n)
            if jp == nkt // 2 - 1:
                p.wait("vector", tT, cons_sig[-1])
                av_ev.append(p.op("vector", lambda e, qc=qc: e.tensor_copy(
                    y_un[:, qc * CH:(qc + 1) * CH], PS2[:, 0:CH]), vV))

        scs = {0: emit_scores(0)}
        for j in range(NP):
            es, vm = emit_exp_mask(j, scs[j])
            if j + 1 < NP:
                scs[j + 1] = emit_scores(j + 1)
            emit_av(j, es, vm)
        b_freed.append(cons_sig[-1])
        vr = p.op("vector", lambda e: e.reciprocal(den_sb[:, :], PS2[0:NQC, 512:512 + CH]), vV)
        p.wait("sync", vV, vr)
        dsd = p.dma("sync", den_rows[0:1, :], den_sb[:, :], hden)
        yv = None
        for qc in range(NQC):
            p.wait("tensor", hden, dsd)
            p.wait("tensor", vV, av_ev[-1])
            if yv is not None:
                p.wait("tensor", vV, yv)
            tb_ = p.op("tensor", lambda e, qc=qc: e.matmul(
                PS2[:, 0:CH], ones_fr[0:1, :], den_rows[0:1, qc * CH:(qc + 1) * CH],
                start=True, stop=True, skip_group_check=True), tT)
            p.wait("vector", tT, tb_)
            if yqi >= 2:
                p.wait("vector", hy[yqi % 2], ydma[yqi - 2])
            yv = p.op("vector", lambda e, qc=qc, yqi=yqi: e.tensor_mul(
                ysc[:, (yqi % 2) * CH:(yqi % 2 + 1) * CH],
                y_un[:, qc * CH:(qc + 1) * CH], PS2[:, 0:CH]), vV)
            p.wait("sync", vV, yv)
            tb2 = qc * CH // SPB
            cof = (qc * CH) % SPB
            ydma.append(p.dma("sync",
                              ay_in[(hh * 4 + tb2) * 128:(hh * 4 + tb2 + 1) * 128, cof:cof + CH],
                              ysc[:, (yqi % 2) * CH:(yqi % 2 + 1) * CH], hy[yqi % 2]))
            yqi += 1
        prev_exp_last = exp_sig[-1]
        prev_cons_last = cons_sig[-1]
        prev_yv_last = yv
        if taps != "B":
            p.wait("gpsimd", hy[0])
            p.wait("gpsimd", hy[1])
            p.op("gpsimd", lambda e, hh=hh: e.collective_compute(
                "AllGather", ALU.bypass, replica_groups=RG,
                ins=[ay_in[hh * 512:(hh + 1) * 512, :]],
                outs=[ay_out[hh * 2048:(hh + 1) * 2048, :]]), gcc)
    if taps == "B":
        finish(p, nc, all_g_dma, all_h_dma, gcc, all_hts)
        return nc
    g_agy = gcc.n

    for n in ("qsb", "ksb", "vsb", "y_un", "pTs", "den_sb", "den_rows", "ysc"):
        sa.free(n)
    sa.free("masks")
    yT = sa.alloc("yT", [128, DCH * TOKC], BF16)
    x1 = sa.alloc("x1", [128, DCH * TOKC], F32)
    wob = sa.alloc("wob", [128, DCH * DM], BF16)
    ay_h = ay_out.tensor
    p.wait("sync", hy[0])
    p.wait("sync", hy[1])
    p.wait("sync", hden)
    h_x1 = p.dma("sync", x1[:, :].rearrange("p (c t) -> p c t", t=TOKC),
                 xT_loc.rearrange("(c p) t -> p c t", p=128), hx)
    p.dma("sync", sg_sb[:, :], s_gate.rearrange("(a p) -> p a", p=128), hx,
          allow_slow_non_contiguous=True)
    h_su = p.dma("sync", su_sb[:, :], s_up.rearrange("(a p) -> p a", p=128), hx,
                 allow_slow_non_contiguous=True)
    h_x1 = h_su

    gwo = p.dma("gpsimd", wob[:, :].rearrange("p (c n) -> p c n", n=DM),
                w_out.rearrange("(c p) n -> p c n", p=128), gw[0])
    wo_ev = []
    wi = 0
    for hh in range(HL):
        p.wait("gpsimd", gcc, g_agy - HL + hh + 1)
        regdma(yT[:, hh * 4 * TOKC:(hh + 1) * 4 * TOKC].rearrange("p (s t) -> p s t", t=TOKC),
               ay_h, 6 * HL + hh, [[SPB, 128], [512 * SPB, 4], [1, SPB]], gys[hh])
        p.wait("tensor", gys[hh], gys[hh].n)
        if hh == 0:
            p.wait("tensor", gw[0], gwo)
        for oct_ in range(DCH):
            for tt in range(NTT):
                ppar = wi % 2
                PSo = PS0[:, ppar * 512: ppar * 512 + NT]
                if wi >= 2:
                    p.wait("tensor", vV, wo_ev[wi - 2])
                for src_ in range(4):
                    c = src_ * HL + hh
                    pos = hh * 4 + src_
                    p.op("tensor", lambda e, c=c, pos=pos, oct_=oct_, tt=tt, PSo=PSo, src_=src_: e.matmul(
                        PSo, wob[:, c * DM + oct_ * 128: c * DM + (oct_ + 1) * 128],
                        yT[:, pos * TOKC + tt * NT: pos * TOKC + (tt + 1) * NT],
                        start=(src_ == 0), stop=(src_ == 3)), tT if src_ == 3 else None)
                tsig = tT.n
                p.wait("vector", tT, tsig)
                p.wait("vector", hx, h_x1)
                wo_ev.append(p.op("vector", lambda e, oct_=oct_, tt=tt, PSo=PSo: e.tensor_add(
                    x1[:, oct_ * TOKC + tt * NT: oct_ * TOKC + (tt + 1) * NT], PSo,
                    x1[:, oct_ * TOKC + tt * NT: oct_ * TOKC + (tt + 1) * NT]), vV))
                wi += 1
    v_x1 = p.nopsig("vector", vV)
    if taps == "C1":
        p.wait("sync", vV, v_x1)
        p.dma("sync", dbg_x1.rearrange("(c p) t -> p c t", p=128),
              x1[:, :].rearrange("p (c t) -> p c t", t=TOKC), hx)
        finish(p, nc, all_g_dma, all_h_dma, gcc, all_hts)
        return nc
    sa.free("yT")
    sa.free("wob")

    xn2 = sa.alloc("xn2", [128, DCH * TOKC], BF16)
    xsq2 = sa.alloc("xsq2", [128, 2 * TOKC], BF16)
    xnc2 = sa.alloc("xnc2", [128, TOKC], F32)
    p.wait("vector", vV, v_x1)
    for c in range(DCH):
        p.op("vector", lambda e, c=c: e.tensor_copy(xn2[:, c * TOKC:(c + 1) * TOKC],
                                                    x1[:, c * TOKC:(c + 1) * TOKC]), None)
    v_xb = p.nopsig("vector", vV)
    xv2 = lambda c: xn2[:, c * TOKC:(c + 1) * TOKC]
    t_st2 = ln_stats(xv2, None, xsq2, wait_vec=v_xb)
    t_bc2 = ln_finalize(t_st2, 1.0 / DM)
    for n in ("mrow", "msqrow", "varrow", "stdrow", "rrow", "mrrow"):
        sa.free(n)
    p.wait("vector", tT, t_bc2)
    for c in range(DCH):
        p.op("vector", lambda e, c=c: e.tensor_mul(xnc2[:, :], xv2(c), PS0[:, 0:TOKC]), None)
        p.op("vector", lambda e, c=c: e.tensor_sub(xv2(c), xnc2[:, :], PS1[:, 0:TOKC]), None)
    v_xn2 = p.nopsig("vector", vV)
    if taps == "C2":
        p.wait("gpsimd", vV, v_xn2)
        p.dma("gpsimd", dbg_xn2.rearrange("(c p) t -> p c t", p=128),
              xn2[:, :].rearrange("p (c t) -> p c t", t=TOKC), sxd)
        finish(p, nc, all_g_dma, all_h_dma, gcc, all_hts)
        return nc

    p.wait("tensor", h_const)
    sa.free("xsq2")
    sa.free("xnc2")
    x1T = sa.alloc("x1T", [128, T8 * DM], BF16)
    tpi = 0
    tp_cp = []
    for t8 in range(T8):
        for c in range(DCH):
            par = tpi % 2
            if tpi >= 2:
                p.wait("tensor", vV, tp_cp[tpi - 2])
            tsg = p.op("tensor", lambda e, t8=t8, c=c, par=par: e.transpose(
                PS2[:, par * 512: par * 512 + 128],
                x1[:, c * TOKC + t8 * 128: c * TOKC + (t8 + 1) * 128], ident_f[:, :]), tT)
            p.wait("vector", tT, tsg)
            tp_cp.append(p.op("vector", lambda e, t8=t8, c=c, par=par: e.tensor_copy(
                x1T[:, t8 * DM + c * 128: t8 * DM + (c + 1) * 128],
                PS2[:, par * 512: par * 512 + 128]), vV))
            tpi += 1
    v_x1T = p.nopsig("vector", vV)
    gw_cons[0] = max(gw_cons[0], tT.n)
    gw_cons[1] = max(gw_cons[1], tT.n)
    sa.free("x1")

    fuse = sa.alloc("fuse", [128, ICT * TB], BF16)
    wgb = sa.alloc("wgb", [128, 2 * DCH * 256], BF16)
    wub = sa.alloc("wub", [128, 2 * DCH * 256], BF16)
    wdb = sa.alloc("wdb", [128, 2 * 4 * (DM // 2)], BF16)
    outb = sa.alloc("outb", [128, 2 * 512], F32)
    sdb = sa.alloc("sdb", [128, DM], F32)
    sd_row = sa.alloc("sd_row", [1, DM], F32)
    h_sd = p.dma("sync", sd_row[0:1, :], s_down[None, :], hsd)
    p.wait("tensor", hsd, h_sd)
    p.wait("tensor", vV, v_x1T)
    for tt in range(DM // 512):
        if tt > 0:
            p.wait("tensor", vV)
        tsd = p.op("tensor", lambda e, tt=tt: e.matmul(PS2[:, 0:512], ones_fr[0:1, :],
                                                       sd_row[0:1, tt * 512:(tt + 1) * 512],
                                                       start=True, stop=True), tT)
        p.wait("vector", tT, tsd)
        p.op("vector", lambda e, tt=tt: e.tensor_copy(sdb[:, tt * 512:(tt + 1) * 512], PS2[:, 0:512]), vV)
    v_sdb = p.nopsig("vector", vV)
    sa.free("sd_row")

    NICG = ICT // 2
    wgv = lambda par: wgb[:, :].rearrange("p (b c n) -> p b c n", b=2, n=256)[:, par]
    wuv = lambda par: wub[:, :].rearrange("p (b c n) -> p b c n", b=2, n=256)[:, par]
    wdv = lambda par: wdb[:, :].rearrange("p (b a n) -> p b a n", b=2, a=4)[:, par]
    odma = []
    oi = 0
    for tb in range(NTB):
        p.wait("tensor", vV)
        gu_ev = []
        gu_sact = []
        gu_mm = []
        for icg in range(NICG):
            par = icg % 2
            p.wait("gpsimd", tT, gw_cons[par])
            p.dma("gpsimd", wgv(par), w_gate[:, icg * 256:(icg + 1) * 256]
                  .rearrange("(c p) n -> p c n", p=128), gw[par])
            gw2 = p.dma("gpsimd", wuv(par), w_up[:, icg * 256:(icg + 1) * 256]
                        .rearrange("(c p) n -> p c n", p=128), gw[par])
            for j in range(2):
                ic = icg * 2 + j
                pgu = ic % 2
                PSgu = PS2 if pgu == 0 else PS0
                PSg = PSgu[:, 0:TB]
                PSu = PSgu[:, 512:512 + TB]
                p.wait("tensor", gw[par], gw2)
                p.wait("tensor", vV, v_xn2)
                p.wait("tensor", vV, v_sdb)
                if ic >= 2:
                    p.wait("tensor", vV, gu_ev[ic - 2])
                    p.wait("tensor", sS, gu_sact[ic - 2])
                for c in range(DCH):
                    p.op("tensor", lambda e, c=c, j=j, par=par, tb=tb, PSg=PSg: e.matmul(
                        PSg, wgv(par)[:, c, j * 128:(j + 1) * 128],
                        xn2[:, c * TOKC + tb * TB: c * TOKC + (tb + 1) * TB],
                        start=(c == 0), stop=(c == DCH - 1)), None)
                for c in range(DCH):
                    p.op("tensor", lambda e, c=c, j=j, par=par, tb=tb, PSu=PSu: e.matmul(
                        PSu, wuv(par)[:, c, j * 128:(j + 1) * 128],
                        xn2[:, c * TOKC + tb * TB: c * TOKC + (tb + 1) * TB],
                        start=(c == 0), stop=(c == DCH - 1)), tT if c == DCH - 1 else None)
                tsg = tT.n
                gu_mm.append(tT.n)
                p.wait("scalar", tT, tsg)
                if ic >= 2:
                    p.wait("scalar", vV, gu_ev[ic - 2])
                p.op("scalar", lambda e, ic=ic, PSg=PSg, pgu=pgu: e.activation(
                    trash_s[:, pgu * TB:(pgu + 1) * TB], PSg, AF.Silu, scale=sg_sb[:, ic:ic + 1]), None)
                gu_sact.append(p.nopsig("scalar", sS))
                p.wait("vector", sS, gu_sact[-1])
                gu_ev.append(p.op("vector", lambda e, ic=ic, PSu=PSu, pgu=pgu: e.scalar_tensor_tensor(
                    fuse[:, ic * TB:(ic + 1) * TB], PSu, su_sb[:, ic:ic + 1], trash_s[:, pgu * TB:(pgu + 1) * TB],
                    ALU.mult, ALU.mult), vV))
                gw_cons[par] = tT.n
        v_fuse = p.nopsig("vector", vV)
        if taps == "C3" and tb == 0:
            p.wait("gpsimd", vV, v_fuse)
            p.dma("gpsimd", dbg_f.rearrange("(c p) t -> p c t", p=128),
                  fuse[:, :].rearrange("p (c t) -> p c t", t=TB), sxd)
            finish(p, nc, all_g_dma, all_h_dma, gcc)
            return nc
        PSDN = (PS0, PS1, PS2, PS3)
        for oph in range(2):
            NOG2 = max(1, DM // 1024)
            dn_mm = []
            p.wait("tensor", vV)
            for icg4 in range(ICT // 4):
                par = icg4 % 2
                p.wait("gpsimd", tT, gw_cons[par])
                gwd = p.dma("gpsimd", wdv(par),
                            w_down[icg4 * 512:(icg4 + 1) * 512, oph * (DM // 2):(oph + 1) * (DM // 2)]
                            .rearrange("(a p) n -> p a n", p=128),
                            gw[par])
                p.wait("tensor", gw[par], gwd)
                p.wait("tensor", vV, v_fuse)
                for j4 in range(4):
                    ic = icg4 * 4 + j4
                    for t1 in range(T128):
                        for og2 in range(NOG2):
                            PSd = PSDN[t1 % 4][:, og2 * 512:(og2 + 1) * 512]
                            p.op("tensor", lambda e, ic=ic, j4=j4, og2=og2, par=par, t1=t1, tb=tb, PSd=PSd: e.matmul(
                                PSd, fuse[:, ic * TB + t1 * 128: ic * TB + (t1 + 1) * 128],
                                wdv(par)[:, j4, og2 * 512:(og2 + 1) * 512],
                                start=(ic == 0), stop=(ic == ICT - 1), skip_group_check=True),
                                tT if (j4 == 3 and t1 == T128 - 1 and og2 == NOG2 - 1) else None)
                dn_mm.append(tT.n)
                gw_cons[par] = tT.n
            for t1 in range(T128):
                trow = tb * TB + t1 * 128
                for og2 in range(NOG2):
                    og = oph * NOG2 + og2
                    PSd = PSDN[t1 % 4][:, og2 * 512:(og2 + 1) * 512]
                    p.wait("vector", tT, dn_mm[-1])
                    obp = oi % 2
                    if oi >= 2:
                        p.wait("vector", ho[oi % 2], odma[oi - 2])
                    p.op("vector", lambda e, og=og, PSd=PSd, obp=obp: e.tensor_mul(
                        outb[:, obp * 512:(obp + 1) * 512], PSd, sdb[:, og * 512:(og + 1) * 512]), None)
                    vo = p.op("vector", lambda e, og=og, trow=trow, obp=obp: e.tensor_add(
                        outb[:, obp * 512:(obp + 1) * 512], outb[:, obp * 512:(obp + 1) * 512],
                        x1T[:, (trow // 128) * DM + og * 512: (trow // 128) * DM + (og + 1) * 512]), vV)
                    p.wait("sync", vV, vo)
                    odma.append(p.dma("sync", out_loc[trow:trow + 128, og * 512:(og + 1) * 512],
                                      outb[:, obp * 512:(obp + 1) * 512], ho[oi % 2]))
                    oi += 1
    finish(p, nc, all_g_dma, all_h_dma, gcc, all_hts)
    return nc


def finish(p, nc, all_g_dma, all_h_dma, gcc, all_hts=()):
    for s_ in all_h_dma:
        p.wait("sync", s_)
    for s_ in all_hts:
        p.wait("scalar", s_)
    for s_ in all_g_dma:
        p.wait("gpsimd", s_)
    p.wait("gpsimd", gcc)
    with nc.Block() as block:
        @block.sync
        def _(e):
            for f in p.q["sync"]:
                f(e)

        @block.scalar
        def _(e):
            for f in p.q["scalar"]:
                f(e)

        @block.tensor
        def _(e):
            for f in p.q["tensor"]:
                f(e)

        @block.vector
        def _(e):
            for f in p.q["vector"]:
                f(e)

        @block.gpsimd
        def _(e):
            for f in p.q["gpsimd"]:
                f(e)


class _Cfg:
    def __init__(self):
        self.B = 2; self.S = 4096; self.H = 16; self.dh = 128
        self.DM = 2048; self.I = 8192; self.CH = 512; self.KT = 128; self.NC = 8
        self.HL = 4; self.TOKC = 1024; self.T8 = 8; self.DCH = 16
        self.NQC = 8; self.ICT = 64; self.NDIAG = 4; self.QCT = 4
        self.stop_after = "C"
        self.TB = 512; self.NTB = 2; self.T128 = 4


def _host_inputs(cfg, x, w_qkv, w_out, q_gain, w_gate, w_up, w_down, s_gate, s_up, s_down):
    H, HL, SPB = cfg.H, cfg.HL, cfg.S // 4
    x = np.ascontiguousarray(np.asarray(x, dtype=np.float32))
    xT = np.ascontiguousarray(np.transpose(x, (0, 2, 1)))
    inv = np.power(10000.0, -np.arange(64, dtype=np.float64) / 64.0)
    th = np.arange(cfg.S, dtype=np.float64)[:, None] * inv[None, :]
    cosT = np.cos(th).astype(np.float32)
    sinT = np.sin(th).astype(np.float32)
    ident = np.eye(128, dtype=np.float32)
    q_gain = np.asarray(q_gain, dtype=np.float32)
    in_maps = []
    for c in range(cfg.NC):
        b, r = c // 4, c % 4
        s0 = r * SPB
        BLK = 1536 * SPB
        offs = []
        for hh in range(HL):
            for rp in range(4):
                offs.append(hh * 4 * BLK + rp * BLK + 1024 * SPB + r * SPB * 128)
        for hh in range(HL):
            offs.append(hh * 4 * BLK + r * 128 * SPB)
        for hh in range(HL):
            offs.append(hh * 4 * BLK + 512 * SPB + r * 128 * SPB)
        for hh in range(HL):
            offs.append(hh * 2048 * SPB + r * 128 * SPB)
        in_maps.append({
            "xT_loc": np.ascontiguousarray(xT[b, :, s0:s0 + SPB]),
            "w_qkv": np.ascontiguousarray(np.asarray(w_qkv, dtype=np.float32)),
            "w_out": np.ascontiguousarray(np.asarray(w_out, dtype=np.float32)),
            "q_gain": q_gain,
            "gain_loc": np.ascontiguousarray(q_gain[r * HL:(r + 1) * HL]),
            "w_gate": np.ascontiguousarray(np.asarray(w_gate, dtype=np.int8)),
            "w_up": np.ascontiguousarray(np.asarray(w_up, dtype=np.int8)),
            "w_down": np.ascontiguousarray(np.asarray(w_down, dtype=np.int8)),
            "s_gate": np.ascontiguousarray(np.asarray(s_gate, dtype=np.float32)),
            "s_up": np.ascontiguousarray(np.asarray(s_up, dtype=np.float32)),
            "s_down": np.ascontiguousarray(np.asarray(s_down, dtype=np.float32)),
            "cosL": np.ascontiguousarray(cosT[s0:s0 + SPB]),
            "sinL": np.ascontiguousarray(sinT[s0:s0 + SPB]),
            "ident": ident,
            "offtab": np.asarray(offs, dtype=np.int32),
        })
    return in_maps


def kernel(L0_attn_x, L0_attn_w_qkv, L0_attn_w_out, L0_attn_q_gain,
           L0_ffn_w_gate, L0_ffn_w_up, L0_ffn_w_down,
           L0_ffn_s_gate, L0_ffn_s_up, L0_ffn_s_down):
    cfg = _Cfg()
    in_maps = _host_inputs(cfg, L0_attn_x, L0_attn_w_qkv, L0_attn_w_out, L0_attn_q_gain,
                           L0_ffn_w_gate, L0_ffn_w_up, L0_ffn_w_down,
                           L0_ffn_s_gate, L0_ffn_s_up, L0_ffn_s_down)
    nc = build(cfg)
    res = run_bass_kernel_spmd(nc, in_maps, core_ids=list(range(8)))
    SPB = cfg.S // 4
    out = np.empty((cfg.B, cfg.S, cfg.DM), dtype=np.float32)
    for c in range(cfg.NC):
        b, r = c // 4, c % 4
        out[b, r * SPB:(r + 1) * SPB, :] = np.asarray(res.results[c]["out_loc"])
    return out


# version 16
# speedup vs baseline: 1.0681x; 1.0056x over previous
import numpy as np
import concourse.bass as bass
import concourse.mybir as mybir
from concourse.bass_utils import run_bass_kernel_spmd

F32 = mybir.dt.float32
BF16 = mybir.dt.bfloat16
I8 = mybir.dt.int8
I32 = mybir.dt.int32
AF = mybir.ActivationFunctionType
ALU = mybir.AluOpType

ENGS = ("sync", "scalar", "tensor", "vector", "gpsimd")


class Sem:
    def __init__(self, nc, name):
        self.h = nc.alloc_semaphore(name)
        self.n = 0

    def bump(self, k=1):
        self.n += k
        return self.n


class SAlloc:
    def __init__(self, nc, base, limit):
        self.nc, self.base, self.limit = nc, base, limit
        self.used = []

    def alloc(self, name, shape, dtype, align=64):
        per = int(np.prod(shape[1:])) * mybir.dt.size(dtype)
        per = (per + align - 1) // align * align
        off = self.base
        for (o, e, _) in sorted(self.used):
            if off + per <= o:
                break
            off = max(off, e)
        assert off + per <= self.limit, f"SBUF overflow: {name} {per}B at {off}"
        self.used.append((off, off + per, name))
        return self.nc.alloc_sbuf_tensor_at(name, list(shape), dtype, offset=off)

    def free(self, name):
        self.used = [u for u in self.used if u[2] != name]


class Plan:

    def __init__(self, nc):
        self.nc = nc
        self.q = {e: [] for e in ENGS}
        self.self_sems = {}

    def em(self, eng, fn):
        self.q[eng].append(fn)

    def wait(self, eng, sem, val=None):
        v = sem.n if val is None else val
        if v > 0:
            self.em(eng, lambda e, s=sem.h, vv=v: e.wait_ge(s, vv))

    def op(self, eng, fn, sem=None, k=1):
        ss = self.self_sems.get(eng)
        if ss is not None:
            assert sem is None or sem is ss
            self.wait(eng, ss)
            v = ss.bump(1)
            self.em(eng, lambda e, f=fn, s=ss.h: f(e).then_inc(s, 1))
            return v
        if sem is None:
            self.em(eng, fn)
            return None
        v = sem.bump(k)
        self.em(eng, lambda e, f=fn, s=sem.h, kk=k: f(e).then_inc(s, kk))
        return v

    def nopsig(self, eng, sem, k=1):
        ss = self.self_sems.get(eng)
        if ss is not None:
            assert sem is ss
            return ss.n
        v = sem.bump(k)
        self.em(eng, lambda e, s=sem.h, kk=k: e.nop().then_inc(s, kk))
        return v

    def dma(self, eng, dst, src, sem, **kw):
        v = sem.bump(16)
        self.em(eng, lambda e, d=dst, s=src, ss=sem.h, kw2=dict(kw): e.dma_start(d, s, **kw2).then_inc(ss, 16))
        return v


def build(cfg):
    nc = bass.Bass("TRN2", target_bir_lowering=False)
    B, S, DM, H, I = cfg.B, cfg.S, cfg.DM, cfg.H, cfg.I
    HL, TOKC, T8, DCH, NQC, CH, KT = cfg.HL, cfg.TOKC, cfg.T8, cfg.DCH, cfg.NQC, cfg.CH, cfg.KT
    SPB = S // 4
    NDIAG = cfg.NDIAG
    QCT = DM // 512
    NT = min(512, TOKC)
    NTT = TOKC // NT
    ICT = cfg.ICT
    TB, NTB, T128 = cfg.TB, cfg.NTB, cfg.T128
    taps = cfg.stop_after

    di = lambda n, sh, dt: nc.dram_tensor(n, sh, dt, kind="ExternalInput").ap()
    do = lambda n, sh, dt: nc.dram_tensor(n, sh, dt, kind="ExternalOutput").ap()
    dint = lambda n, sh, dt, out=False: (do(n, sh, dt) if out else nc.dram_tensor(n, sh, dt, kind="Internal").ap())

    xT_loc = di("xT_loc", [DM, TOKC], F32)
    w_qkv = di("w_qkv", [DM, 3 * DM], F32)
    w_out = di("w_out", [DM, DM], F32)
    q_gain = di("q_gain", [H], F32)
    gain_loc = di("gain_loc", [HL], F32)
    w_gate = di("w_gate", [DM, I], I8)
    w_up = di("w_up", [DM, I], I8)
    w_down = di("w_down", [I, DM], I8)
    s_gate = di("s_gate", [I], F32)
    s_up = di("s_up", [I], F32)
    s_down = di("s_down", [DM], F32)
    cosL = di("cosL", [TOKC, 64], F32)
    sinL = di("sinL", [TOKC, 64], F32)
    ident_in = di("ident", [128, 128], F32)
    offtab = di("offtab", [7 * HL], I32)

    BLK = 1536 * TOKC
    KOFF = 512 * TOKC
    VOFF = 1024 * TOKC
    agx_in = dint("agx_in", [HL * BLK], BF16, out=(taps == "A"))
    agw_in = nc.dram_tensor("agw_in", [256], BF16, kind="Internal").ap()
    agw_out = nc.dram_tensor("agw_out", [1024], BF16, kind="Internal").ap()
    if taps != "A":
        agx_out = dint("agx_out", [4 * HL * BLK], BF16)
        ay_in = dint("ay_in", [4 * HL * 128, SPB], BF16, out=(taps == "B"))
    if taps in ("C", "C1", "C2", "C3"):
        ay_out = dint("ay_out", [16 * HL * 128, SPB], BF16)
        out_loc = do("out_loc", [TOKC, DM], F32)
    if taps == "C1":
        dbg_x1 = do("dbg_x1", [DM, TOKC], F32)
    if taps == "C2":
        dbg_xn2 = do("dbg_xn2", [DM, TOKC], F32)
    if taps == "C3":
        dbg_f = do("dbg_f", [I, TB], F32)

    PS0 = nc.alloc_psum_tensor("PS0", [128, 1024], F32)
    PS1 = nc.alloc_psum_tensor("PS1", [128, 1024], F32)
    PS2 = nc.alloc_psum_tensor("PS2", [128, 1024], F32)
    _pt_guard = nc.psum_tensor("PT", [128, 2048], BF16)
    PT = _pt_guard.__enter__()
    PS3 = None

    sa = SAlloc(nc, (nc.sbuf_base + 63) // 64 * 64, nc.sbuf_top)
    ident_bf = sa.alloc("ident_bf", [128, 128], BF16)
    ident_f = sa.alloc("ident_f", [128, 128], F32)
    masks = sa.alloc("masks", [128, NDIAG * CH], BF16)
    cos_sb = sa.alloc("cos_sb", [128, T8 * 64], F32)
    sin_sb = sa.alloc("sin_sb", [128, T8 * 64], F32)
    ones_bf = sa.alloc("ones_bf", [128, 8], BF16)
    ones_fr = sa.alloc("ones_fr", [1, 128], F32)
    gain_sb = sa.alloc("gain_sb", [1, H], F32)
    gloc_sb = sa.alloc("gloc_sb", [1, HL], F32)
    negrow = sa.alloc("negrow", [1, HL], F32)
    gb = sa.alloc("gb", [128, H], F32)
    negabsg = sa.alloc("negabsg", [128, HL], F32)
    ind_sb = sa.alloc("ind_sb", [128, NQC * NQC], BF16)
    offtab_sb = sa.alloc("offtab_sb", [1, 7 * HL], I32)
    trash_s = sa.alloc("trash_s", [128, 1024], BF16)
    eps_un = sa.alloc("eps_un", [128, 1], F32)
    eps_ln = sa.alloc("eps_ln", [1, 1], F32)
    sg_sb = sa.alloc("sg_sb", [128, ICT], F32)
    su_sb = sa.alloc("su_sb", [128, ICT], F32)

    p = Plan(nc)
    gcc = Sem(nc, "gcc")
    gA = Sem(nc, "gA")
    tT = Sem(nc, "tT")
    vV = Sem(nc, "vV")
    sS = Sem(nc, "sS")
    g_const = Sem(nc, "g_const")
    sxd = Sem(nc, "sxd")
    gw = [Sem(nc, "gw0"), Sem(nc, "gw1")]
    gq = [Sem(nc, "gq0"), Sem(nc, "gq1")]
    gys = [Sem(nc, "gy%d" % i) for i in range(4)]
    h_const = Sem(nc, "h_const")
    hv = [Sem(nc, "hv0"), Sem(nc, "hv1")]
    ht = [Sem(nc, "ht0"), Sem(nc, "ht1")]
    hden = Sem(nc, "hden")
    hy = [Sem(nc, "hy0"), Sem(nc, "hy1")]
    hq = [Sem(nc, "hq0"), Sem(nc, "hq1")]
    hsd = Sem(nc, "hsd")
    hts = [Sem(nc, "hts0"), Sem(nc, "hts1")]
    hx = Sem(nc, "hx")
    ho = [Sem(nc, "ho0"), Sem(nc, "ho1")]
    all_g_dma = [g_const, sxd, gw[0], gw[1], gq[0], gq[1]] + gys
    p_selfsems_marker = True
    all_h_dma = [h_const, hv[0], hv[1], ht[0], ht[1], hden, hy[0], hy[1], hx, ho[0], ho[1], hq[0], hq[1], hsd]
    all_hts = hts
    RPOOL = nc.alloc_register(mybir.EngineType.Pool, "roff")
    RSYNC = nc.alloc_register(mybir.EngineType.SP, "roffs")
    p.self_sems = {"vector": vV, "scalar": sS}
    gw_cons = {0: 0, 1: 0}

    p.dma("gpsimd", ident_bf[:, :], ident_in, g_const)
    p.dma("sync", ident_f[:, :], ident_in, h_const)
    p.dma("sync", cos_sb[:, :].rearrange("p (a j) -> p a j", j=64),
          cosL.rearrange("(a p) j -> p a j", p=128), h_const)
    p.dma("sync", sin_sb[:, :].rearrange("p (a j) -> p a j", j=64),
          sinL.rearrange("(a p) j -> p a j", p=128), h_const)
    p.dma("sync", gain_sb[0:1, :], q_gain[None, :], h_const)
    p.dma("sync", gloc_sb[0:1, :], gain_loc[None, :], h_const)
    h_off = p.dma("sync", offtab_sb[0:1, :], offtab[None, :], h_const)
    h_gain = h_const.n
    p.op("vector", lambda e: e.memset(ones_bf[:, :], 1.0), vV)
    p.op("vector", lambda e: e.memset(ones_fr[:, :], 1.0), vV)
    vz = p.op("vector", lambda e: e.memset(ind_sb[:, :], 0.0), vV)
    p.wait("vector", vV, vz)
    for qc in range(NQC):
        p.op("vector", lambda e, qc=qc: e.memset(ind_sb[:, qc * NQC + qc: qc * NQC + qc + 1], 1.0), vV)
    p.op("vector", lambda e: e.memset(masks[:, :], 1.0), vV)
    p.op("vector", lambda e: e.memset(eps_un[:, :], 1e-6), vV)
    p.op("vector", lambda e: e.memset(eps_ln[:, :], 1e-5), vV)
    v_consts = vV.n
    p.wait("gpsimd", vV, v_consts)
    for i in range(NDIAG):
        p.op("gpsimd", lambda e, i=i: e.affine_select(
            masks[:, i * CH:(i + 1) * CH], masks[:, i * CH:(i + 1) * CH],
            [[1, CH]], ALU.is_ge, 0.0, base=-i * 128, channel_multiplier=-1), gA)
    p.wait("scalar", h_const, h_gain)
    s_abs = p.op("scalar", lambda e: e.activation(negrow[0:1, :], gloc_sb[0:1, :], AF.Abs), sS)
    p.wait("vector", sS, s_abs)
    v_neg = p.op("vector", lambda e: e.tensor_scalar_mul(negrow[0:1, :], negrow[0:1, :], -1.0), vV)
    p.wait("tensor", h_const, h_gain)
    p.wait("tensor", vV, v_neg)
    p.op("tensor", lambda e: e.matmul(PS0[:, 0:H], ones_fr[0:1, :], gain_sb[0:1, :], start=True, stop=True), None)
    t_bc = p.op("tensor", lambda e: e.matmul(PS0[:, H:H + HL], ones_fr[0:1, :], negrow[0:1, :],
                                             start=True, stop=True), tT)
    p.wait("vector", tT, t_bc)
    p.op("vector", lambda e: e.tensor_copy(gb[:, :], PS0[:, 0:H]), vV)
    v_gainbc = p.op("vector", lambda e: e.tensor_copy(negabsg[:, :], PS0[:, H:H + HL]), vV)

    mrow = sa.alloc("mrow", [1, TOKC], F32)
    msqrow = sa.alloc("msqrow", [1, TOKC], F32)
    varrow = sa.alloc("varrow", [1, TOKC], F32)
    stdrow = sa.alloc("stdrow", [1, TOKC], F32)
    rrow = sa.alloc("rrow", [1, TOKC], F32)
    mrrow = sa.alloc("mrrow", [1, TOKC], F32)
    xsb = sa.alloc("xsb", [128, DCH * TOKC], BF16)
    xnc = sa.alloc("xnc", [128, TOKC], F32)
    xsq = sa.alloc("xsq", [128, 2 * TOKC], BF16)
    wqb = sa.alloc("wqb", [128, 2 * DCH * 512], BF16)
    qstage = sa.alloc("qstage", [128, H * T8 * 128], BF16)
    kstage = sa.alloc("kstage", [128, H * T8 * 128], BF16)
    vtmp = sa.alloc("vtmp", [128, 2 * 512], BF16)
    tstage = sa.alloc("tstage", [128, 2 * 512], BF16)
    rtmp = sa.alloc("rtmp", [128, 4 * T8 * 64], BF16)
    ssq4 = sa.alloc("ssq4", [128, 8], F32)
    std4 = sa.alloc("std4", [128, 8], F32)
    rq4 = sa.alloc("rq4", [128, 8], F32)

    xv = lambda c: xsb[:, c * TOKC:(c + 1) * TOKC]
    g_x = p.dma("gpsimd", xsb[:, :].rearrange("p (c t) -> p c t", t=TOKC),
                xT_loc.rearrange("(c p) t -> p c t", p=128), sxd)
    if taps != "A":
        p.wait("gpsimd", g_const)
        p.dma("gpsimd", agw_in.rearrange("(a b) -> a b", a=2), ident_bf[0:2, 0:128], g_const)
        p.wait("gpsimd", g_const)
        p.op("gpsimd", lambda e: e.collective_compute(
            "AllGather", ALU.bypass, replica_groups=[[0, 1, 2, 3], [4, 5, 6, 7]],
            ins=[agw_in], outs=[agw_out]), gcc)

    def ln_stats(src_of, sq_of, sqbuf, wait_vec=None):
        mmv = []
        sqv = []
        for c in range(DCH):
            if wait_vec is not None:
                p.wait("scalar", vV, wait_vec)
            if c >= 2:
                p.wait("scalar", tT, mmv[c - 2])
            sqv.append(p.op("scalar", lambda e, c=c: e.activation(
                sqbuf[:, (c % 2) * TOKC:(c % 2 + 1) * TOKC], src_of(c), AF.Square), sS))
            if wait_vec is not None:
                p.wait("tensor", vV, wait_vec)
            p.wait("tensor", sS, sqv[c])
            for tt in range(NTT):
                p.op("tensor", lambda e, c=c, tt=tt: e.matmul(
                    PS0[0:1, tt * NT:(tt + 1) * NT], ones_bf[:, 0:1],
                    src_of(c)[:, tt * NT:(tt + 1) * NT],
                    start=(c == 0), stop=(c == DCH - 1)), None)
                p.op("tensor", lambda e, c=c, tt=tt: e.matmul(
                    PS1[0:1, tt * NT:(tt + 1) * NT], ones_bf[:, 0:1],
                    sqbuf[:, (c % 2) * TOKC + tt * NT:(c % 2) * TOKC + (tt + 1) * NT],
                    start=(c == 0), stop=(c == DCH - 1)), tT if tt == NTT - 1 else None)
            mmv.append(tT.n)
        return tT.n

    def ln_finalize(t_wait, inv_n):
        p.wait("vector", tT, t_wait)
        p.op("vector", lambda e: e.tensor_scalar_mul(mrow[0:1, :], PS0[0:1, 0:TOKC], inv_n), None)
        p.op("vector", lambda e: e.tensor_scalar_mul(msqrow[0:1, :], PS1[0:1, 0:TOKC], inv_n), None)
        p.op("vector", lambda e: e.tensor_mul(varrow[0:1, :], mrow[0:1, :], mrow[0:1, :]), None)
        vsub = p.op("vector", lambda e: e.tensor_sub(varrow[0:1, :], msqrow[0:1, :], varrow[0:1, :]), vV)
        p.wait("scalar", vV, vsub)
        ss_ = p.op("scalar", lambda e: e.activation(stdrow[0:1, :], varrow[0:1, :], AF.Sqrt, bias=eps_ln[0:1, :]), sS)
        p.wait("vector", sS, ss_)
        p.op("vector", lambda e: e.reciprocal(rrow[0:1, :], stdrow[0:1, :]), None)
        vr = p.op("vector", lambda e: e.tensor_mul(mrrow[0:1, :], mrow[0:1, :], rrow[0:1, :]), vV)
        p.wait("tensor", vV, vr)
        for tt in range(NTT):
            p.op("tensor", lambda e, tt=tt: e.matmul(PS0[:, tt * NT:(tt + 1) * NT], ones_fr[0:1, :],
                                                     rrow[0:1, tt * NT:(tt + 1) * NT], start=True, stop=True), None)
            p.op("tensor", lambda e, tt=tt: e.matmul(PS1[:, tt * NT:(tt + 1) * NT], ones_fr[0:1, :],
                                                     mrrow[0:1, tt * NT:(tt + 1) * NT], start=True, stop=True),
                 tT if tt == NTT - 1 else None)
        return tT.n

    p.wait("scalar", sxd, g_x)
    p.wait("tensor", sxd, g_x)
    p.wait("tensor", vV, v_gainbc)
    t_st1 = ln_stats(xv, None, xsq)
    t_bc1 = ln_finalize(t_st1, 1.0 / DM)
    p.wait("vector", tT, t_bc1)
    for c in range(DCH):
        p.op("vector", lambda e, c=c: e.tensor_mul(xnc[:, :], xv(c), PS0[:, 0:TOKC]), None)
        p.op("vector", lambda e, c=c: e.tensor_sub(xv(c), xnc[:, :], PS1[:, 0:TOKC]), None)
    v_xn = p.nopsig("vector", vV)

    NCT = 3 * DM // 512
    wv = lambda par: wqb[:, :].rearrange("p (b c n) -> p b c n", b=2, n=512)[:, par]
    cosv = cos_sb[:, :].rearrange("p (a j) -> p a j", j=64)
    sinv = sin_sb[:, :].rearrange("p (a j) -> p a j", j=64)
    rt = lambda i: rtmp[:, :].rearrange("p (i a j) -> p i a j", i=4, j=64)[:, i]
    RG = [[0, 1, 2, 3], [4, 5, 6, 7]]
    g_ag0 = gcc.n
    evac_done = []
    psq_sig = []
    vdma = []
    vtile_done = []
    tcp = []
    tdma = []
    gidx = 0
    tgrp = 0
    ct_order = list(range(2 * QCT, 3 * QCT)) + [x for g in range(QCT) for x in (g, QCT + g)]
    for ci, ct in enumerate(ct_order):
        par_w = ci % 2
        p.wait("gpsimd", tT, gw_cons[par_w])
        g_w = p.dma("gpsimd", wv(par_w),
                    w_qkv[:, ct * 512:(ct + 1) * 512].rearrange("(c p) n -> p c n", p=128), gw[par_w])
        is_q = ct < QCT
        is_k = QCT <= ct < 2 * QCT
        h0 = 4 * (ct % QCT)
        for t8 in range(T8):
            gi = ci * T8 + t8
            par = gi % 2
            PSq = PS2[:, par * 512:(par + 1) * 512]
            p.wait("tensor", gw[par_w], g_w)
            p.wait("tensor", vV, v_xn)
            if gi >= 2:
                p.wait("tensor", vV, evac_done[gi - 2])
            for c in range(DCH):
                p.op("tensor", lambda e, c=c, t8=t8, ct=ct, par_w=par_w, PSq=PSq: e.matmul(
                    PSq, xsb[:, c * TOKC + t8 * 128: c * TOKC + (t8 + 1) * 128],
                    wv(par_w)[:, c, :], start=(c == 0), stop=(c == DCH - 1)),
                    tT if c == DCH - 1 else None)
            sig = tT.n
            psq_sig.append(sig)
            gw_cons[par_w] = sig
            if is_q or is_k:
                stg = qstage if is_q else kstage
                par2 = gi % 2
                p.wait("scalar", tT, sig)
                for j in range(4):
                    p.op("scalar", lambda e, j=j, PSq=PSq, par2=par2: e.activation(
                        trash_s[:, j * 128:(j + 1) * 128], PSq[:, j * 128:(j + 1) * 128], AF.Square,
                        accum_out=ssq4[:, par2 * 4 + j: par2 * 4 + j + 1]), None)
                ssig = p.op("scalar", lambda e, par2=par2: e.activation(
                    std4[:, par2 * 4:(par2 + 1) * 4], ssq4[:, par2 * 4:(par2 + 1) * 4],
                    AF.Sqrt, bias=eps_un[:, :]), sS)
                p.wait("vector", sS, ssig)
                p.op("vector", lambda e, par2=par2: e.reciprocal(
                    rq4[:, par2 * 4:(par2 + 1) * 4], std4[:, par2 * 4:(par2 + 1) * 4]), None)
                if is_q:
                    p.op("vector", lambda e, par2=par2, h0=h0: e.tensor_mul(
                        rq4[:, par2 * 4:(par2 + 1) * 4], rq4[:, par2 * 4:(par2 + 1) * 4],
                        gb[:, h0:h0 + 4]), None)
                for j in range(4):
                    p.op("vector", lambda e, j=j, h0=h0, t8=t8, par2=par2, PSq=PSq, stg=stg: e.tensor_scalar_mul(
                        stg[:, ((h0 + j) * T8 + t8) * 128:((h0 + j) * T8 + t8 + 1) * 128],
                        PSq[:, j * 128:(j + 1) * 128],
                        rq4[:, par2 * 4 + j: par2 * 4 + j + 1]), None)
                evac_done.append(p.nopsig("vector", vV))
            else:
                par2 = gi % 2
                p.wait("vector", tT, sig)
                if gidx >= 2:
                    p.wait("vector", hv[par2], vtile_done[gidx - 2])
                cp = p.op("vector", lambda e, par2=par2, PSq=PSq: e.tensor_copy(
                    vtmp[:, par2 * 512:(par2 + 1) * 512], PSq), vV)
                evac_done.append(cp)
                p.wait("sync", vV, cp)
                nsub = 4 // HL if HL < 4 else 1
                hps = 4 // nsub
                for s_i in range(nsub):
                    hq0 = h0 + s_i * hps
                    g_grp = hq0 // HL
                    off = VOFF + g_grp * TOKC * 128 + t8 * 128 * 128
                    vdma.append(p.dma("sync",
                                      bass.AP(agx_in.tensor, off,
                                              [[128, 128], [BLK, hps], [1, 128]]),
                                      vtmp[:, par2 * 512 + s_i * hps * 128: par2 * 512 + (s_i + 1) * hps * 128]
                                      .rearrange("p (h d) -> p h d", d=128),
                                      hv[par2]))
                vtile_done.append(hv[par2].n)
                gidx += 1
        if is_k:
            g = ct - QCT
            for stg in (qstage, kstage):
                sv = stg[:, :].rearrange("p (h a d) -> p h a d", h=H, d=128)
                for h in range(h0, h0 + 4):
                    if h % 2 == 0:
                        lo = sv[:, h, :, 0:64]
                        hi = sv[:, h, :, 64:128]
                        p.op("vector", lambda e, lo=lo: e.tensor_mul(rt(0), lo, cosv), None)
                        p.op("vector", lambda e, hi=hi: e.tensor_mul(rt(1), hi, sinv), None)
                        p.op("vector", lambda e, lo=lo: e.tensor_mul(rt(2), lo, sinv), None)
                        p.op("vector", lambda e, hi=hi: e.tensor_mul(rt(3), hi, cosv), None)
                        p.op("vector", lambda e, lo=lo: e.tensor_sub(lo, rt(0), rt(1)), None)
                        p.op("vector", lambda e, hi=hi: e.tensor_add(hi, rt(2), rt(3)), None)
            v_hset = p.nopsig("vector", vV)
            p.wait("tensor", vV, v_hset)
            p.wait("tensor", g_const)
            for stg, koff in ((qstage, 0), (kstage, KOFF)):
                for h in range(h0, h0 + 4):
                    ghead = h // HL
                    hh_l = h % HL
                    for grp in range(max(1, T8 // 4)):
                        n4 = min(4, T8 - grp * 4)
                        par_t = tgrp % 2
                        if tgrp >= 2:
                            p.wait("tensor", vV, tcp[tgrp - 2])
                        for j in range(n4):
                            t8 = grp * 4 + j
                            p.op("tensor", lambda e, h=h, t8=t8, par_t=par_t, j=j, stg=stg: e.transpose(
                                PT[:, par_t * 1024 + j * 128: par_t * 1024 + (j + 1) * 128],
                                stg[:, (h * T8 + t8) * 128:(h * T8 + t8 + 1) * 128],
                                ident_bf[:, :]), tT if j == n4 - 1 else None)
                        ts = tT.n
                        p.wait("vector", tT, ts)
                        if tgrp >= 2:
                            p.wait("vector", ht[tgrp % 2], tdma[tgrp - 2])
                        c2 = p.op("vector", lambda e, par_t=par_t, n4=n4: e.tensor_copy(
                            tstage[:, par_t * 512:par_t * 512 + n4 * 128],
                            PT[:, par_t * 1024:par_t * 1024 + n4 * 128]), vV)
                        tcp.append(c2)
                        p.wait("sync", vV, c2)
                        tdma.append(p.dma("sync",
                                          bass.AP(agx_in.tensor,
                                                  hh_l * BLK + koff + ghead * 128 * TOKC + grp * 512,
                                                  [[TOKC, 128], [1, n4 * 128]]),
                                          tstage[:, par_t * 512:par_t * 512 + n4 * 128], ht[par_t]))
                        tgrp += 1
    sa.free("cos_sb")
    sa.free("sin_sb")
    if taps != "A":
        p.wait("gpsimd", ht[0])
        p.wait("gpsimd", ht[1])
        p.wait("gpsimd", hv[0])
        p.wait("gpsimd", hv[1])
        for hh in range(HL):
            p.op("gpsimd", lambda e, hh=hh: e.collective_compute(
                "AllGather", ALU.bypass, replica_groups=RG,
                ins=[agx_in[hh * BLK:(hh + 1) * BLK]],
                outs=[agx_out[hh * 4 * BLK:(hh + 1) * 4 * BLK]]), gcc)
    g_agqkv = gcc.n
    for n in ("xsb", "xnc", "xsq", "wqb", "qstage", "kstage", "vtmp", "tstage", "rtmp", "ssq4", "std4", "rq4"):
        sa.free(n)

    _pt_guard.__exit__(None, None, None)
    PS3 = nc.alloc_psum_tensor("PS3", [128, 1024], F32)
    if taps == "A":
        finish(p, nc, all_g_dma, all_h_dma, gcc, all_hts)
        return nc

    qsb = sa.alloc("qsb", [128, 2 * S], BF16)
    ksb = sa.alloc("ksb", [128, 2 * S], BF16)
    vsb = sa.alloc("vsb", [128, 2 * S], BF16)
    y_un = sa.alloc("y_un", [128, S], BF16)
    pTs = sa.alloc("pTs", [128, 6 * CH], BF16)
    den_sb = sa.alloc("den_sb", [NQC, CH], F32)
    den_rows = sa.alloc("den_rows", [1, NQC * CH], F32)
    ysc = sa.alloc("ysc", [128, 2 * CH], BF16)

    def regdma(dst_ap, src_handle, tabidx, pattern, sem, eng="gpsimd"):
        v = sem.bump(16)
        reg = RPOOL if eng == "gpsimd" else RSYNC

        def f(e, dst=dst_ap, th=src_handle, ti=tabidx, pat=pattern, s=sem.h, reg=reg):
            e.reg_load(reg, offtab_sb[0:1, ti:ti + 1])
            e.dma_start(dst, bass.AP(th, reg, pat)).then_inc(s, 16)
        p.em(eng, f)
        return v

    agx_h = agx_out.tensor
    b_ld = []
    b_ldv = []
    b_freed = []
    prev_exp_last = 0
    prev_cons_last = 0
    prev_yv_last = 0
    ydma = []
    yqi = 0
    g_ag0 = g_agqkv - HL
    p.wait("gpsimd", h_const, h_off)
    for hh in range(HL):
        par = hh % 2
        p.wait("gpsimd", gcc, g_ag0 + hh + 1)
        if hh >= 2:
            p.wait("gpsimd", tT, b_freed[hh - 2])
        regdma(qsb[:, par * S:(par + 1) * S].rearrange("p (a t) -> p a t", t=SPB),
               agx_h, 4 * HL + hh, [[SPB, 128], [BLK, 4], [1, SPB]], gq[par])
        regdma(ksb[:, par * S:(par + 1) * S].rearrange("p (a t) -> p a t", t=SPB),
               agx_h, 5 * HL + hh, [[SPB, 128], [BLK, 4], [1, SPB]], gq[par])
        p.wait("sync", gcc, g_ag0 + hh + 1)
        if hh == 0:
            p.wait("sync", h_const, h_off)
        if hh >= 2:
            p.wait("sync", tT, b_freed[hh - 2])
        for rp in range(4):
            regdma(vsb[:, par * S + rp * SPB: par * S + (rp + 1) * SPB].rearrange("p (b d) -> p b d", d=128),
                   agx_h, hh * 4 + rp, [[128, 128], [128 * 128, SPB // 128], [1, 128]], hq[par], eng="sync")
        b_ld.append(gq[par].n)
        b_ldv.append(hq[par].n)
        exp_sig = []
        cons_sig = []
        av_ev = []
        den_started = False
        pairs = []
        for qc in range(NQC):
            nkt = (qc + 1) * CH // KT
            for jp in range(nkt // 2):
                pairs.append((qc, jp, nkt))
        NP = len(pairs)

        def emit_scores(j):
            qc, jp, nkt = pairs[j]
            ppar = j % 3
            PSsc = (PS0, PS1, PS3)[ppar]
            p.wait("tensor", gq[par], b_ld[hh])
            p.wait("tensor", hq[par], b_ldv[hh])
            if j >= 3:
                p.wait("tensor", sS, exp_sig[j - 3])
            elif prev_exp_last:
                p.wait("tensor", sS, prev_exp_last)
            for u in range(2):
                kt = jp * 2 + u
                p.op("tensor", lambda e, kt=kt, u=u, PSsc=PSsc, qc=qc, par=par: e.matmul(
                    PSsc[:, u * CH:(u + 1) * CH],
                    ksb[:, par * S + kt * KT: par * S + (kt + 1) * KT],
                    qsb[:, par * S + qc * CH: par * S + (qc + 1) * CH],
                    start=True, stop=True, skip_group_check=True), tT if u == 1 else None)
            return tT.n

        def emit_exp_mask(j, scs):
            qc, jp, nkt = pairs[j]
            ppar = j % 3
            PSsc = (PS0, PS1, PS3)[ppar]
            p.wait("scalar", tT, scs)
            if j >= 3:
                p.wait("scalar", tT, cons_sig[j - 3])
            elif prev_cons_last:
                p.wait("scalar", tT, prev_cons_last)
            es = p.op("scalar", lambda e, ppar=ppar, PSsc=PSsc, hh=hh: e.activation(
                pTs[:, ppar * 2 * CH:(ppar + 1) * 2 * CH], PSsc[:, 0:2 * CH], AF.Exp,
                bias=negabsg[:, hh:hh + 1]), sS)
            exp_sig.append(es)
            base_kt = jp * 2
            need_mask = [u for u in range(2) if (base_kt + u) * KT >= qc * CH]
            vm = None
            if need_mask:
                p.wait("vector", sS, es)
                for u in need_mask:
                    kt = base_kt + u
                    i = (kt * KT - qc * CH) // 128
                    vm = p.op("vector", lambda e, u=u, ppar=ppar, i=i: e.tensor_mul(
                        pTs[:, (ppar * 2 + u) * CH:(ppar * 2 + u + 1) * CH],
                        pTs[:, (ppar * 2 + u) * CH:(ppar * 2 + u + 1) * CH],
                        masks[:, i * CH:(i + 1) * CH]), vV)
            return es, vm

        def emit_av(j, es, vm):
            nonlocal den_started
            qc, jp, nkt = pairs[j]
            ppar = j % 3
            if vm is not None:
                p.wait("tensor", vV, vm)
            else:
                p.wait("tensor", sS, es)
            if jp == 0 and qc > 0:
                p.wait("tensor", vV, av_ev[qc - 1])
            if jp == 0 and qc == 0 and prev_yv_last:
                p.wait("tensor", vV, prev_yv_last)
            for u in range(2):
                kt = jp * 2 + u
                p.op("tensor", lambda e, kt=kt, u=u, ppar=ppar, nkt=nkt, par=par: e.matmul(
                    PS2[:, 0:CH], vsb[:, par * S + kt * KT: par * S + (kt + 1) * KT],
                    pTs[:, (ppar * 2 + u) * CH:(ppar * 2 + u + 1) * CH],
                    start=(kt == 0), stop=(kt == nkt - 1), skip_group_check=True), None)
                p.op("tensor", lambda e, u=u, ppar=ppar, qc=qc, ds=den_started, last=(
                    qc == NQC - 1 and jp == nkt // 2 - 1 and u == 1): e.matmul(
                    PS2[0:NQC, 512:512 + CH], ind_sb[:, qc * NQC:(qc + 1) * NQC],
                    pTs[:, (ppar * 2 + u) * CH:(ppar * 2 + u + 1) * CH],
                    start=(not ds), stop=last, skip_group_check=True), tT if u == 1 else None)
                den_started = True
            cons_sig.append(tT.n)
            if jp == nkt // 2 - 1:
                p.wait("vector", tT, cons_sig[-1])
                av_ev.append(p.op("vector", lambda e, qc=qc: e.tensor_copy(
                    y_un[:, qc * CH:(qc + 1) * CH], PS2[:, 0:CH]), vV))

        scs = {0: emit_scores(0)}
        for j in range(NP):
            es, vm = emit_exp_mask(j, scs[j])
            if j + 1 < NP:
                scs[j + 1] = emit_scores(j + 1)
            emit_av(j, es, vm)
        b_freed.append(cons_sig[-1])
        vr = p.op("vector", lambda e: e.reciprocal(den_sb[:, :], PS2[0:NQC, 512:512 + CH]), vV)
        p.wait("sync", vV, vr)
        dsd = p.dma("sync", den_rows[0:1, :], den_sb[:, :], hden)
        yv = None
        for qc in range(NQC):
            p.wait("tensor", hden, dsd)
            p.wait("tensor", vV, av_ev[-1])
            if yv is not None:
                p.wait("tensor", vV, yv)
            tb_ = p.op("tensor", lambda e, qc=qc: e.matmul(
                PS2[:, 0:CH], ones_fr[0:1, :], den_rows[0:1, qc * CH:(qc + 1) * CH],
                start=True, stop=True, skip_group_check=True), tT)
            p.wait("vector", tT, tb_)
            if yqi >= 2:
                p.wait("vector", hy[yqi % 2], ydma[yqi - 2])
            yv = p.op("vector", lambda e, qc=qc, yqi=yqi: e.tensor_mul(
                ysc[:, (yqi % 2) * CH:(yqi % 2 + 1) * CH],
                y_un[:, qc * CH:(qc + 1) * CH], PS2[:, 0:CH]), vV)
            p.wait("sync", vV, yv)
            tb2 = qc * CH // SPB
            cof = (qc * CH) % SPB
            ydma.append(p.dma("sync",
                              ay_in[(hh * 4 + tb2) * 128:(hh * 4 + tb2 + 1) * 128, cof:cof + CH],
                              ysc[:, (yqi % 2) * CH:(yqi % 2 + 1) * CH], hy[yqi % 2]))
            yqi += 1
        prev_exp_last = exp_sig[-1]
        prev_cons_last = cons_sig[-1]
        prev_yv_last = yv
        if taps != "B":
            p.wait("gpsimd", hy[0])
            p.wait("gpsimd", hy[1])
            p.op("gpsimd", lambda e, hh=hh: e.collective_compute(
                "AllGather", ALU.bypass, replica_groups=RG,
                ins=[ay_in[hh * 512:(hh + 1) * 512, :]],
                outs=[ay_out[hh * 2048:(hh + 1) * 2048, :]]), gcc)
    if taps == "B":
        finish(p, nc, all_g_dma, all_h_dma, gcc, all_hts)
        return nc
    g_agy = gcc.n

    for n in ("qsb", "ksb", "vsb", "y_un", "pTs", "den_sb", "den_rows", "ysc"):
        sa.free(n)
    sa.free("masks")
    yT = sa.alloc("yT", [128, DCH * TOKC], BF16)
    x1 = sa.alloc("x1", [128, DCH * TOKC], F32)
    wob = sa.alloc("wob", [128, DCH * DM], BF16)
    ay_h = ay_out.tensor
    p.wait("sync", hy[0])
    p.wait("sync", hy[1])
    p.wait("sync", hden)
    h_x1 = p.dma("sync", x1[:, :].rearrange("p (c t) -> p c t", t=TOKC),
                 xT_loc.rearrange("(c p) t -> p c t", p=128), hx)
    p.dma("sync", sg_sb[:, :], s_gate.rearrange("(a p) -> p a", p=128), hx,
          allow_slow_non_contiguous=True)
    h_su = p.dma("sync", su_sb[:, :], s_up.rearrange("(a p) -> p a", p=128), hx,
                 allow_slow_non_contiguous=True)
    h_x1 = h_su

    gwo = p.dma("gpsimd", wob[:, :].rearrange("p (c n) -> p c n", n=DM),
                w_out.rearrange("(c p) n -> p c n", p=128), gw[0])
    wo_ev = []
    wi = 0
    for hh in range(HL):
        p.wait("gpsimd", gcc, g_agy - HL + hh + 1)
        regdma(yT[:, hh * 4 * TOKC:(hh + 1) * 4 * TOKC].rearrange("p (s t) -> p s t", t=TOKC),
               ay_h, 6 * HL + hh, [[SPB, 128], [512 * SPB, 4], [1, SPB]], gys[hh])
        p.wait("tensor", gys[hh], gys[hh].n)
        if hh == 0:
            p.wait("tensor", gw[0], gwo)
        for oct_ in range(DCH):
            for tt in range(NTT):
                ppar = wi % 2
                PSo = PS0[:, ppar * 512: ppar * 512 + NT]
                if wi >= 2:
                    p.wait("tensor", vV, wo_ev[wi - 2])
                for src_ in range(4):
                    c = src_ * HL + hh
                    pos = hh * 4 + src_
                    p.op("tensor", lambda e, c=c, pos=pos, oct_=oct_, tt=tt, PSo=PSo, src_=src_: e.matmul(
                        PSo, wob[:, c * DM + oct_ * 128: c * DM + (oct_ + 1) * 128],
                        yT[:, pos * TOKC + tt * NT: pos * TOKC + (tt + 1) * NT],
                        start=(src_ == 0), stop=(src_ == 3)), tT if src_ == 3 else None)
                tsig = tT.n
                p.wait("vector", tT, tsig)
                p.wait("vector", hx, h_x1)
                wo_ev.append(p.op("vector", lambda e, oct_=oct_, tt=tt, PSo=PSo: e.tensor_add(
                    x1[:, oct_ * TOKC + tt * NT: oct_ * TOKC + (tt + 1) * NT], PSo,
                    x1[:, oct_ * TOKC + tt * NT: oct_ * TOKC + (tt + 1) * NT]), vV))
                wi += 1
    v_x1 = p.nopsig("vector", vV)
    if taps == "C1":
        p.wait("sync", vV, v_x1)
        p.dma("sync", dbg_x1.rearrange("(c p) t -> p c t", p=128),
              x1[:, :].rearrange("p (c t) -> p c t", t=TOKC), hx)
        finish(p, nc, all_g_dma, all_h_dma, gcc, all_hts)
        return nc
    sa.free("yT")
    sa.free("wob")

    xn2 = sa.alloc("xn2", [128, DCH * TOKC], BF16)
    xsq2 = sa.alloc("xsq2", [128, 2 * TOKC], BF16)
    xnc2 = sa.alloc("xnc2", [128, TOKC], F32)
    p.wait("vector", vV, v_x1)
    for c in range(DCH):
        p.op("vector", lambda e, c=c: e.tensor_copy(xn2[:, c * TOKC:(c + 1) * TOKC],
                                                    x1[:, c * TOKC:(c + 1) * TOKC]), None)
    v_xb = p.nopsig("vector", vV)
    xv2 = lambda c: xn2[:, c * TOKC:(c + 1) * TOKC]
    t_st2 = ln_stats(xv2, None, xsq2, wait_vec=v_xb)
    t_bc2 = ln_finalize(t_st2, 1.0 / DM)
    for n in ("mrow", "msqrow", "varrow", "stdrow", "rrow", "mrrow"):
        sa.free(n)
    p.wait("vector", tT, t_bc2)
    for c in range(DCH):
        p.op("vector", lambda e, c=c: e.tensor_mul(xnc2[:, :], xv2(c), PS0[:, 0:TOKC]), None)
        p.op("vector", lambda e, c=c: e.tensor_sub(xv2(c), xnc2[:, :], PS1[:, 0:TOKC]), None)
    v_xn2 = p.nopsig("vector", vV)
    if taps == "C2":
        p.wait("gpsimd", vV, v_xn2)
        p.dma("gpsimd", dbg_xn2.rearrange("(c p) t -> p c t", p=128),
              xn2[:, :].rearrange("p (c t) -> p c t", t=TOKC), sxd)
        finish(p, nc, all_g_dma, all_h_dma, gcc, all_hts)
        return nc

    p.wait("tensor", h_const)
    sa.free("xsq2")
    sa.free("xnc2")
    x1T = sa.alloc("x1T", [128, T8 * DM], BF16)
    tpi = 0
    tp_cp = []
    for t8 in range(T8):
        for c in range(DCH):
            par = tpi % 2
            if tpi >= 2:
                p.wait("tensor", vV, tp_cp[tpi - 2])
            tsg = p.op("tensor", lambda e, t8=t8, c=c, par=par: e.transpose(
                PS2[:, par * 512: par * 512 + 128],
                x1[:, c * TOKC + t8 * 128: c * TOKC + (t8 + 1) * 128], ident_f[:, :]), tT)
            p.wait("vector", tT, tsg)
            tp_cp.append(p.op("vector", lambda e, t8=t8, c=c, par=par: e.tensor_copy(
                x1T[:, t8 * DM + c * 128: t8 * DM + (c + 1) * 128],
                PS2[:, par * 512: par * 512 + 128]), vV))
            tpi += 1
    v_x1T = p.nopsig("vector", vV)
    gw_cons[0] = max(gw_cons[0], tT.n)
    gw_cons[1] = max(gw_cons[1], tT.n)
    sa.free("x1")

    fuse = sa.alloc("fuse", [128, ICT * TB], BF16)
    wgb = sa.alloc("wgb", [128, 2 * DCH * 256], BF16)
    wub = sa.alloc("wub", [128, 2 * DCH * 256], BF16)
    wdb = sa.alloc("wdb", [128, 2 * 4 * (DM // 2)], BF16)
    outb = sa.alloc("outb", [128, 2 * 512], F32)
    sdb = sa.alloc("sdb", [128, DM], F32)
    sd_row = sa.alloc("sd_row", [1, DM], F32)
    h_sd = p.dma("sync", sd_row[0:1, :], s_down[None, :], hsd)
    p.wait("tensor", hsd, h_sd)
    p.wait("tensor", vV, v_x1T)
    for tt in range(DM // 512):
        if tt > 0:
            p.wait("tensor", vV)
        tsd = p.op("tensor", lambda e, tt=tt: e.matmul(PS2[:, 0:512], ones_fr[0:1, :],
                                                       sd_row[0:1, tt * 512:(tt + 1) * 512],
                                                       start=True, stop=True), tT)
        p.wait("vector", tT, tsd)
        p.op("vector", lambda e, tt=tt: e.tensor_copy(sdb[:, tt * 512:(tt + 1) * 512], PS2[:, 0:512]), vV)
    v_sdb = p.nopsig("vector", vV)
    sa.free("sd_row")

    NICG = ICT // 2
    wgv = lambda par: wgb[:, :].rearrange("p (b c n) -> p b c n", b=2, n=256)[:, par]
    wuv = lambda par: wub[:, :].rearrange("p (b c n) -> p b c n", b=2, n=256)[:, par]
    wdv = lambda par: wdb[:, :].rearrange("p (b a n) -> p b a n", b=2, a=4)[:, par]
    odma = []
    oi = 0
    for tb in range(NTB):
        p.wait("tensor", vV)
        gu_ev = []
        gu_sact = []
        gu_mm = []
        for icg in range(NICG):
            par = icg % 2
            p.wait("gpsimd", tT, gw_cons[par])
            p.dma("gpsimd", wgv(par), w_gate[:, icg * 256:(icg + 1) * 256]
                  .rearrange("(c p) n -> p c n", p=128), gw[par])
            gw2 = p.dma("gpsimd", wuv(par), w_up[:, icg * 256:(icg + 1) * 256]
                        .rearrange("(c p) n -> p c n", p=128), gw[par])
            for j in range(2):
                ic = icg * 2 + j
                pgu = ic % 2
                PSgu = PS2 if pgu == 0 else PS0
                PSg = PSgu[:, 0:TB]
                PSu = PSgu[:, 512:512 + TB]
                p.wait("tensor", gw[par], gw2)
                p.wait("tensor", vV, v_xn2)
                p.wait("tensor", vV, v_sdb)
                if ic >= 2:
                    p.wait("tensor", vV, gu_ev[ic - 2])
                    p.wait("tensor", sS, gu_sact[ic - 2])
                for c in range(DCH):
                    p.op("tensor", lambda e, c=c, j=j, par=par, tb=tb, PSg=PSg: e.matmul(
                        PSg, wgv(par)[:, c, j * 128:(j + 1) * 128],
                        xn2[:, c * TOKC + tb * TB: c * TOKC + (tb + 1) * TB],
                        start=(c == 0), stop=(c == DCH - 1)), None)
                for c in range(DCH):
                    p.op("tensor", lambda e, c=c, j=j, par=par, tb=tb, PSu=PSu: e.matmul(
                        PSu, wuv(par)[:, c, j * 128:(j + 1) * 128],
                        xn2[:, c * TOKC + tb * TB: c * TOKC + (tb + 1) * TB],
                        start=(c == 0), stop=(c == DCH - 1)), tT if c == DCH - 1 else None)
                tsg = tT.n
                gu_mm.append(tT.n)
                p.wait("scalar", tT, tsg)
                if ic >= 2:
                    p.wait("scalar", vV, gu_ev[ic - 2])
                p.op("scalar", lambda e, ic=ic, PSg=PSg, pgu=pgu: e.activation(
                    trash_s[:, pgu * TB:(pgu + 1) * TB], PSg, AF.Silu, scale=sg_sb[:, ic:ic + 1]), None)
                gu_sact.append(p.nopsig("scalar", sS))
                p.wait("vector", sS, gu_sact[-1])
                gu_ev.append(p.op("vector", lambda e, ic=ic, PSu=PSu, pgu=pgu: e.scalar_tensor_tensor(
                    fuse[:, ic * TB:(ic + 1) * TB], PSu, su_sb[:, ic:ic + 1], trash_s[:, pgu * TB:(pgu + 1) * TB],
                    ALU.mult, ALU.mult), vV))
                gw_cons[par] = tT.n
        v_fuse = p.nopsig("vector", vV)
        if taps == "C3" and tb == 0:
            p.wait("gpsimd", vV, v_fuse)
            p.dma("gpsimd", dbg_f.rearrange("(c p) t -> p c t", p=128),
                  fuse[:, :].rearrange("p (c t) -> p c t", t=TB), sxd)
            finish(p, nc, all_g_dma, all_h_dma, gcc)
            return nc
        PSDN = (PS0, PS1, PS2, PS3)
        for oph in range(2):
            NOG2 = max(1, DM // 1024)
            dn_mm = []
            p.wait("tensor", vV)
            for icg4 in range(ICT // 4):
                par = icg4 % 2
                p.wait("gpsimd", tT, gw_cons[par])
                gwd = p.dma("gpsimd", wdv(par),
                            w_down[icg4 * 512:(icg4 + 1) * 512, oph * (DM // 2):(oph + 1) * (DM // 2)]
                            .rearrange("(a p) n -> p a n", p=128),
                            gw[par])
                p.wait("tensor", gw[par], gwd)
                p.wait("tensor", vV, v_fuse)
                for j4 in range(4):
                    ic = icg4 * 4 + j4
                    for t1 in range(T128):
                        for og2 in range(NOG2):
                            PSd = PSDN[t1 % 4][:, og2 * 512:(og2 + 1) * 512]
                            p.op("tensor", lambda e, ic=ic, j4=j4, og2=og2, par=par, t1=t1, tb=tb, PSd=PSd: e.matmul(
                                PSd, fuse[:, ic * TB + t1 * 128: ic * TB + (t1 + 1) * 128],
                                wdv(par)[:, j4, og2 * 512:(og2 + 1) * 512],
                                start=(ic == 0), stop=(ic == ICT - 1), skip_group_check=True),
                                tT if (j4 == 3 and t1 == T128 - 1 and og2 == NOG2 - 1) else None)
                dn_mm.append(tT.n)
                gw_cons[par] = tT.n
            for t1 in range(T128):
                trow = tb * TB + t1 * 128
                for og2 in range(NOG2):
                    og = oph * NOG2 + og2
                    PSd = PSDN[t1 % 4][:, og2 * 512:(og2 + 1) * 512]
                    p.wait("vector", tT, dn_mm[-1])
                    obp = oi % 2
                    if oi >= 2:
                        p.wait("vector", ho[oi % 2], odma[oi - 2])
                    p.op("vector", lambda e, og=og, PSd=PSd, obp=obp: e.tensor_mul(
                        outb[:, obp * 512:(obp + 1) * 512], PSd, sdb[:, og * 512:(og + 1) * 512]), None)
                    vo = p.op("vector", lambda e, og=og, trow=trow, obp=obp: e.tensor_add(
                        outb[:, obp * 512:(obp + 1) * 512], outb[:, obp * 512:(obp + 1) * 512],
                        x1T[:, (trow // 128) * DM + og * 512: (trow // 128) * DM + (og + 1) * 512]), vV)
                    p.wait("sync", vV, vo)
                    odma.append(p.dma("sync", out_loc[trow:trow + 128, og * 512:(og + 1) * 512],
                                      outb[:, obp * 512:(obp + 1) * 512], ho[oi % 2]))
                    oi += 1
    finish(p, nc, all_g_dma, all_h_dma, gcc, all_hts)
    return nc


def finish(p, nc, all_g_dma, all_h_dma, gcc, all_hts=()):
    for s_ in all_h_dma:
        p.wait("sync", s_)
    for s_ in all_hts:
        p.wait("scalar", s_)
    for s_ in all_g_dma:
        p.wait("gpsimd", s_)
    p.wait("gpsimd", gcc)
    with nc.Block() as block:
        @block.sync
        def _(e):
            for f in p.q["sync"]:
                f(e)

        @block.scalar
        def _(e):
            for f in p.q["scalar"]:
                f(e)

        @block.tensor
        def _(e):
            for f in p.q["tensor"]:
                f(e)

        @block.vector
        def _(e):
            for f in p.q["vector"]:
                f(e)

        @block.gpsimd
        def _(e):
            for f in p.q["gpsimd"]:
                f(e)


class _Cfg:
    def __init__(self):
        self.B = 2; self.S = 4096; self.H = 16; self.dh = 128
        self.DM = 2048; self.I = 8192; self.CH = 512; self.KT = 128; self.NC = 8
        self.HL = 4; self.TOKC = 1024; self.T8 = 8; self.DCH = 16
        self.NQC = 8; self.ICT = 64; self.NDIAG = 4; self.QCT = 4
        self.stop_after = "C"
        self.TB = 512; self.NTB = 2; self.T128 = 4


def _host_inputs(cfg, x, w_qkv, w_out, q_gain, w_gate, w_up, w_down, s_gate, s_up, s_down):
    H, HL, SPB = cfg.H, cfg.HL, cfg.S // 4
    x = np.ascontiguousarray(np.asarray(x, dtype=np.float32))
    xT = np.ascontiguousarray(np.transpose(x, (0, 2, 1)))
    inv = np.power(10000.0, -np.arange(64, dtype=np.float64) / 64.0)
    th = np.arange(cfg.S, dtype=np.float64)[:, None] * inv[None, :]
    cosT = np.cos(th).astype(np.float32)
    sinT = np.sin(th).astype(np.float32)
    ident = np.eye(128, dtype=np.float32)
    q_gain = np.asarray(q_gain, dtype=np.float32)
    in_maps = []
    for c in range(cfg.NC):
        b, r = c // 4, c % 4
        s0 = r * SPB
        BLK = 1536 * SPB
        offs = []
        for hh in range(HL):
            for rp in range(4):
                offs.append(hh * 4 * BLK + rp * BLK + 1024 * SPB + r * SPB * 128)
        for hh in range(HL):
            offs.append(hh * 4 * BLK + r * 128 * SPB)
        for hh in range(HL):
            offs.append(hh * 4 * BLK + 512 * SPB + r * 128 * SPB)
        for hh in range(HL):
            offs.append(hh * 2048 * SPB + r * 128 * SPB)
        in_maps.append({
            "xT_loc": np.ascontiguousarray(xT[b, :, s0:s0 + SPB]),
            "w_qkv": np.ascontiguousarray(np.asarray(w_qkv, dtype=np.float32)),
            "w_out": np.ascontiguousarray(np.asarray(w_out, dtype=np.float32)),
            "q_gain": q_gain,
            "gain_loc": np.ascontiguousarray(q_gain[r * HL:(r + 1) * HL]),
            "w_gate": np.ascontiguousarray(np.asarray(w_gate, dtype=np.int8)),
            "w_up": np.ascontiguousarray(np.asarray(w_up, dtype=np.int8)),
            "w_down": np.ascontiguousarray(np.asarray(w_down, dtype=np.int8)),
            "s_gate": np.ascontiguousarray(np.asarray(s_gate, dtype=np.float32)),
            "s_up": np.ascontiguousarray(np.asarray(s_up, dtype=np.float32)),
            "s_down": np.ascontiguousarray(np.asarray(s_down, dtype=np.float32)),
            "cosL": np.ascontiguousarray(cosT[s0:s0 + SPB]),
            "sinL": np.ascontiguousarray(sinT[s0:s0 + SPB]),
            "ident": ident,
            "offtab": np.asarray(offs, dtype=np.int32),
        })
    return in_maps


def kernel(L0_attn_x, L0_attn_w_qkv, L0_attn_w_out, L0_attn_q_gain,
           L0_ffn_w_gate, L0_ffn_w_up, L0_ffn_w_down,
           L0_ffn_s_gate, L0_ffn_s_up, L0_ffn_s_down):
    cfg = _Cfg()
    in_maps = _host_inputs(cfg, L0_attn_x, L0_attn_w_qkv, L0_attn_w_out, L0_attn_q_gain,
                           L0_ffn_w_gate, L0_ffn_w_up, L0_ffn_w_down,
                           L0_ffn_s_gate, L0_ffn_s_up, L0_ffn_s_down)
    nc = build(cfg)
    res = run_bass_kernel_spmd(nc, in_maps, core_ids=list(range(8)))
    SPB = cfg.S // 4
    out = np.empty((cfg.B, cfg.S, cfg.DM), dtype=np.float32)
    for c in range(cfg.NC):
        b, r = c // 4, c % 4
        out[b, r * SPB:(r + 1) * SPB, :] = np.asarray(res.results[c]["out_loc"])
    return out
